# version 05
# baseline (speedup 1.0000x reference)
import numpy as np
import ml_dtypes
import concourse.bass as bass
import concourse.mybir as mybir
from concourse.bass_utils import run_bass_kernel_spmd

F32 = mybir.dt.float32
BF16 = mybir.dt.bfloat16
I32 = mybir.dt.int32
I8 = mybir.dt.int8
AF = mybir.ActivationFunctionType
ALU = mybir.AluOpType
AX = mybir.AxisListType
BFnp = ml_dtypes.bfloat16

ENGS = ("pe", "act", "dve", "pool", "sp")
ENG_ATTR = {"pe": "tensor", "act": "scalar", "dve": "vector", "pool": "gpsimd",
            "sp": "sync"}

class KB:

    def __init__(self, nc):
        self.nc = nc
        self.sem = {}
        self.cnt = {e: 0 for e in ENGS}
        self.cnt["dma"] = 0
        self._es = None
        self.dep = {}
        self.grp_rr = {}
        self.know = {e: {} for e in ENGS}
        self.snap = {}
        self.hw = {}

    def init_sems(self, es):
        self._es = es
        for e in ENGS:
            self.sem[e] = es.enter_context(self.nc.semaphore(f"sem_{e}"))

    def _ensure_grp(self, g):
        if g not in self.sem:
            self.sem[g] = self._es.enter_context(self.nc.semaphore(f"sem_{g}"))
            self.cnt[g] = 0

    def eng(self, e):
        return getattr(self.nc, ENG_ATTR[e])

    def _tname(self, ap):
        if isinstance(ap, tuple):
            ap, rk = ap
            t = ap.tensor if isinstance(ap, bass.AP) else ap
            return t.name + "#" + str(rk)
        t = ap.tensor if isinstance(ap, bass.AP) else ap
        return t.name

    def _unwrap(self, ap):
        return ap[0] if isinstance(ap, tuple) else ap

    def _need(self, waiter, producer):
        peng, pcnt = producer
        if pcnt <= 0:
            return False
        return self.know[waiter].get(peng, 0) < pcnt

    def _wait(self, waiter, producer):
        peng, pcnt = producer
        self.eng(waiter).wait_ge(self.sem[peng], pcnt)
        kw = self.know[waiter]
        kw[peng] = max(kw.get(peng, 0), pcnt)
        sn = self.snap.get(peng, {}).get(pcnt)
        if sn:
            for x, c in sn.items():
                if kw.get(x, 0) < c:
                    kw[x] = c

    def op(self, e, fn, reads=(), writes=(), dma=None, n=1, win=2):
        deps = []
        def is_psum(ap):
            ap = self._unwrap(ap)
            t = ap.tensor if isinstance(ap, bass.AP) else ap
            return "PSum" in type(t).__name__
        def dep_keys(ap):
            nm = self._tname(ap)
            ks = [nm]
            if "#" in nm:
                ks.append(nm.split("#")[0])
            else:
                ks.extend(k for k in self.dep if k.startswith(nm + "#"))
            return ks
        for ap in reads:
            for k in dep_keys(ap):
                d = self.dep.get(k)
                if d:
                    deps.extend(d["w"].items())
        for ap in writes:
            for k in dep_keys(ap):
                d = self.dep.get(k)
                if d:
                    items = list(d["w"].items()) + list(d["r"].items())
                    if e == "pe" and is_psum(ap):
                        items = [(pe_, pc) for (pe_, pc) in items if pe_ != "pe"]
                    deps.extend(items)
        byeng = {}
        for (pe, pc) in deps:
            byeng[pe] = max(byeng.get(pe, 0), pc)
        if dma:
            idx = self.grp_rr.get(dma, 0)
            self.grp_rr[dma] = (idx + 1) % win
            sem_eng = f"{dma}#{idx}"
            self._ensure_grp(sem_eng)
            byeng[sem_eng] = max(byeng.get(sem_eng, 0), self.cnt[sem_eng])
        else:
            sem_eng = e
        for pe, pc in byeng.items():
            if self._need(e, (pe, pc)):
                self._wait(e, (pe, pc))
        inst = fn(self.eng(e))
        step = 16 if dma else n
        self.cnt[sem_eng] += step
        inst.then_inc(self.sem[sem_eng], step)
        stamp = (sem_eng, self.cnt[sem_eng])
        kn = dict(self.know[e])
        kn[e] = self.cnt[e] if not dma else kn.get(e, 0)
        self.snap.setdefault(sem_eng, {})[self.cnt[sem_eng]] = kn
        for ap in writes:
            nm = self._tname(ap)
            if "#" not in nm:
                for k in [k for k in self.dep if k.startswith(nm + "#")]:
                    del self.dep[k]
            d = self.dep.get(nm)
            if d is None or not dma:
                self.dep[nm] = {"w": {stamp[0]: stamp[1]}, "r": {}}
            else:
                d["w"][stamp[0]] = stamp[1]
                d["r"] = {}
        for ap in reads:
            nm = self.dep.setdefault(self._tname(ap), {"w": {}, "r": {}})
            nm["r"][stamp[0]] = stamp[1]
        return inst

    def dma(self, dst, src, q="sp", grp="dma"):
        d2, s2 = self._unwrap(dst), self._unwrap(src)
        return self.op(q, lambda g: g.dma_start(out=d2, in_=s2),
                       reads=[src], writes=[dst], dma=grp)

    def dmat(self, dst, src, q="sp", grp="dma"):
        return self.op(q, lambda g: g.dma_start_transpose(out=dst, in_=src),
                       reads=[src], writes=[dst], dma=grp)

    def mm(self, out, lhsT, rhs, start=True, stop=True):
        o2, l2, r2 = self._unwrap(out), self._unwrap(lhsT), self._unwrap(rhs)
        return self.op("pe", lambda g: g.matmul(o2, lhsT=l2, rhs=r2,
                                                start=start, stop=stop),
                       reads=[lhsT, rhs], writes=[out])

    def trans(self, out, in_, ident):
        o2, i2 = self._unwrap(out), self._unwrap(in_)
        return self.op("pe", lambda g: g.transpose(o2, in_=i2, identity=ident),
                       reads=[in_, ident], writes=[out])

    def tt(self, e, out, a, b, alu):
        o2, a2, b2 = self._unwrap(out), self._unwrap(a), self._unwrap(b)
        return self.op(e, lambda g: g.tensor_tensor(out=o2, in0=a2, in1=b2, op=alu),
                       reads=[a, b], writes=[out])

    def ts(self, e, out, a, s1, op0, s2=None, op1=ALU.bypass, accum=None):
        reads = [a] + [x for x in (s1, s2) if isinstance(x, (bass.AP, tuple))]
        writes = [out] + ([accum] if accum is not None else [])
        o2, a2 = self._unwrap(out), self._unwrap(a)
        s1u = self._unwrap(s1) if isinstance(s1, tuple) else s1
        s2u = self._unwrap(s2) if isinstance(s2, tuple) else s2
        return self.op(e, lambda g: g.tensor_scalar(
            out=o2, in0=a2, scalar1=s1u, scalar2=s2u, op0=op0, op1=op1,
            accum_out=self._unwrap(accum) if accum is not None else None),
            reads=reads, writes=writes)

    def stt(self, out, a, s, b, op0, op1, accum=None):
        reads = [a, b] + ([s] if isinstance(s, (bass.AP, tuple)) else [])
        writes = [out] + ([accum] if accum is not None else [])
        o2, a2, b2 = self._unwrap(out), self._unwrap(a), self._unwrap(b)
        su = self._unwrap(s) if isinstance(s, tuple) else s
        return self.op("dve", lambda g: g.scalar_tensor_tensor(
            out=o2, in0=a2, scalar=su, in1=b2, op0=op0, op1=op1,
            accum_out=self._unwrap(accum) if accum is not None else None),
            reads=reads, writes=writes)

    def constcol(self, val):
        if not hasattr(self, "_ccache"):
            self._ccache = {}
        if val not in self._ccache:
            t = self._es.enter_context(
                self.nc.sbuf_tensor(f"cc_{len(self._ccache)}", [128, 1], F32))
            self.memset("dve", t[:], float(val))
            self._ccache[val] = t
        return self._ccache[val][:, 0:1]

    def act(self, out, in_, func, bias=None, scale=None, accum=None):
        kw = {}
        if bias is not None:
            if isinstance(bias, float):
                bias = self.constcol(bias)
            kw["bias"] = bias
        if scale is not None:
            kw["scale"] = scale
        if accum is not None:
            kw["accum_out"] = accum
        reads = [in_] + [x for x in (bias, scale) if isinstance(x, (bass.AP, tuple))]
        writes = [out] + ([accum] if accum is not None else [])
        if "bias" in kw and isinstance(kw["bias"], tuple):
            kw["bias"] = self._unwrap(kw["bias"])
        if "scale" in kw and isinstance(kw["scale"], tuple):
            kw["scale"] = self._unwrap(kw["scale"])
        if "accum_out" in kw and isinstance(kw["accum_out"], tuple):
            kw["accum_out"] = self._unwrap(kw["accum_out"])
        o2, i2 = self._unwrap(out), self._unwrap(in_)
        return self.op("act", lambda g: g.activation(out=o2, in_=i2, func=func,
                                                     **kw),
                       reads=reads, writes=writes)

    def copy(self, e, out, in_):
        o2, i2 = self._unwrap(out), self._unwrap(in_)
        if e == "act":
            return self.op("act", lambda g: g.copy(out=o2, in_=i2),
                           reads=[in_], writes=[out])
        return self.op(e, lambda g: g.tensor_copy(out=o2, in_=i2),
                       reads=[in_], writes=[out])

    def red(self, e, out, in_, alu, axis=AX.X, absval=None, negate=None):
        o2, i2 = self._unwrap(out), self._unwrap(in_)
        return self.op(e, lambda g: g.tensor_reduce(
            out=o2, in_=i2, axis=axis, op=alu,
            apply_absolute_value=absval, negate=negate),
            reads=[in_], writes=[out])

    def ttr(self, out, a, b, op0, op1, accum, scale=1.0, seed=0.0):
        return self.op("dve", lambda g: g.tensor_tensor_reduce(
            out=out, in0=a, in1=b, scale=scale, scalar=seed, op0=op0, op1=op1,
            accum_out=accum), reads=[a, b], writes=[out, accum])

    def memset(self, e, ap, val):
        a2 = self._unwrap(ap)
        return self.op(e, lambda g: g.memset(a2, val), writes=[ap])

    def barrier_all(self):
        marks = {}
        for e in ENGS:
            eng = self.eng(e)
            i = eng.nop() if e in ("act", "pe", "sp") else eng.engine_nop()
            self.cnt[e] += 1
            i.then_inc(self.sem[e], 1)
            marks[e] = self.cnt[e]
        for e in ENGS:
            for o in ENGS:
                if o != e and self._need(e, (o, marks[o])):
                    self._wait(e, (o, marks[o]))


def softplus_dev(kb, out, in_, tmp1, tmp2):
    kb.act(tmp1, in_, AF.Abs)
    kb.act(tmp1, tmp1, AF.Exp, scale=-1.0)
    kb.act(tmp1, tmp1, AF.Ln, bias=1.0)
    kb.ts("dve", tmp2, in_, 0.0, ALU.max)
    kb.tt("dve", out, tmp2, tmp1, ALU.add)


def log_sigmoid_dev(kb, out, in_, tmp1, tmp2):
    kb.act(tmp1, in_, AF.Abs)
    kb.act(tmp1, tmp1, AF.Exp, scale=-1.0)
    kb.act(tmp1, tmp1, AF.Ln, bias=1.0)
    kb.ts("dve", tmp2, in_, 0.0, ALU.min)
    kb.tt("dve", out, tmp2, tmp1, ALU.subtract)



def build_program(cfg):
    nc = bass.Bass("TRN2", target_bir_lowering=False)
    kb = KB(nc)
    TOK, HID, NH, HD, NCH, CC = 2048, 2048, 16, 128, 16, 128

    dt = nc.dram_tensor
    xin = dt("xin", [TOK, HID], F32, kind="ExternalInput").ap()
    wq = dt("wq", [HID, HID], F32, kind="ExternalInput").ap()
    wk = dt("wk", [HID, HID], F32, kind="ExternalInput").ap()
    wv = dt("wv", [HID, HID], F32, kind="ExternalInput").ap()
    wab = dt("wab", [HID, 32], F32, kind="ExternalInput").ap()
    bab = dt("bab", [1, 32], F32, kind="ExternalInput").ap()
    lamT = dt("lamT", [HD, NH], F32, kind="ExternalInput").ap()
    onw = dt("onw", [HD, 1], F32, kind="ExternalInput").ap()
    wo = dt("wo", [HID, HID], F32, kind="ExternalInput").ap()
    w1T = dt("w1T", [HID, 8192], F32, kind="ExternalInput").ap()
    w2T = dt("w2T", [HID, 8192], F32, kind="ExternalInput").ap()
    w3T = dt("w3T", [8192, HID], F32, kind="ExternalInput").ap()
    g1r = dt("g1r", [1, HID], F32, kind="ExternalInput").ap()
    g2r = dt("g2r", [1, HID], F32, kind="ExternalInput").ap()
    g3r = dt("g3r", [1, 8192], F32, kind="ExternalInput").ap()
    ident = dt("ident", [128, 128], F32, kind="ExternalInput").ap()
    identb = dt("identb", [128, 128], BF16, kind="ExternalInput").ap()
    triu = dt("triu", [128, 128], F32, kind="ExternalInput").ap()
    tril_m = dt("tril_m", [128, 128], F32, kind="ExternalInput").ap()
    onescol = dt("onescol", [1, 128], F32, kind="ExternalInput").ap()
    onesd = dt("onesd", [128, 1], F32, kind="ExternalInput").ap()
    cmask = dt("cmask", [8, 8], F32, kind="ExternalInput").ap()
    cmaskcol = dt("cmaskcol", [8, 1], F32, kind="ExternalInput").ap()
    e127 = dt("e127", [128, 1], F32, kind="ExternalInput").ap()
    onescolb = dt("onescolb", [1, 128], BF16, kind="ExternalInput").ap()
    onesdb = dt("onesdb", [128, 1], BF16, kind="ExternalInput").ap()

    out = dt("outp", [TOK, HID], F32, kind="ExternalOutput").ap()

    ex = cfg.get("expose", ())
    kd = lambda nm: "ExternalOutput" if nm in ex else "Internal"
    XN_TM = dt("XN_TM", [TOK, HID], F32, kind=kd("XN_TM")).ap()
    XN_FM = dt("XN_FM", [HID, TOK], BF16, kind=kd("XN_FM")).ap()
    QFM = dt("QFM", [NCH, NH, HD, CC], F32, kind=kd("QFM")).ap()
    KFM = dt("KFM", [NCH, NH, HD, CC], BF16, kind=kd("KFM")).ap()
    KTM = dt("KTM", [NCH, CC, NH * HD], BF16, kind=kd("KTM")).ap()
    VTM = dt("VTM", [NCH, CC, NH * HD], BF16, kind=kd("VTM")).ap()
    VFM = dt("VFM", [NCH, NH, HD, CC], BF16, kind=kd("VFM")).ap()
    GTM = dt("GTM", [NCH, CC, NH], F32, kind=kd("GTM")).ap()
    BETA = dt("BETA", [NCH, CC, NH], F32, kind=kd("BETA")).ap()
    THR = dt("THR", [NCH, CC, NH], F32, kind=kd("THR")).ap()
    RQ = dt("RQ", [NCH, CC, NH], F32, kind=kd("RQ")).ap()
    ATTNO = dt("ATTNO", [TOK, HID], F32, kind=kd("ATTNO")).ap()
    ONORM = dt("ONORM", [NCH, HD, TOK // NCH * 16], BF16, kind=kd("ONORM")).ap()
    CIN = dt("CIN", [128, 4112], F32).ap()
    COUT = dt("COUT", [1024, 4112], F32).ap()
    XQ1 = dt("XQ1", [HID, TOK], BF16, kind=kd("XQ1")).ap()
    XQ2 = dt("XQ2", [HID, TOK], BF16, kind=kd("XQ2")).ap()
    XQ3 = dt("XQ3", [8192, TOK], BF16, kind=kd("XQ3")).ap()
    HG3 = dt("HG3", [TOK, 8192], BF16, kind=kd("HG3")).ap()
    dbg_out = {}
    for name, shape in cfg.get("dbg", {}).items():
        dbg_out[name] = dt("dbg_" + name, shape, F32, kind="ExternalOutput").ap()

    from contextlib import ExitStack
    es = ExitStack()
    kb._es = es
    sbuf = lambda n, s, d: es.enter_context(nc.sbuf_tensor(n, s, d))
    psum = lambda n, s, d: es.enter_context(nc.psum_tensor(n, s, d))
    kb.init_sems(es)

    c_ident = sbuf("c_ident", [128, 128], F32)
    c_identb = sbuf("c_identb", [128, 128], BF16)
    c_triu = sbuf("c_triu", [128, 128], F32)
    c_trilm = sbuf("c_trilm", [128, 128], F32)
    c_ones1 = sbuf("c_ones1", [1, 128], F32)
    c_ones1b = sbuf("c_ones1b", [1, 128], BF16)
    c_onesdb = sbuf("c_onesdb", [128, 1], BF16)
    c_e127 = sbuf("c_e127", [128, 1], F32)
    lamE2 = sbuf("lamE2", [128, 16], F32)
    lamdb = [sbuf(f"lamdb{h}", [128, 128], BF16) for h in range(16)]
    c_onesd = sbuf("c_onesd", [128, 1], F32)
    c_bab = sbuf("c_bab", [1, 32], F32)
    lam_sb = sbuf("lam_sb", [128, NH], F32)
    onw_sb = sbuf("onw_sb", [128, 1], F32)

    BF = [sbuf(f"BF{i}", [128, 2048], F32) for i in range(8)]
    BH = [sbuf(f"BH{i}", [128, 2048], BF16) for i in range(16)]

    PSA = psum("PSA", [128, 1024], F32)
    PSB = psum("PSB", [128, 1024], F32)
    PSC = psum("PSC", [128, 1024], F32)
    PSD = psum("PSD", [128, 1024], F32)
    psP = [PSA[:, 0:512], PSA[:, 512:1024], PSB[:, 0:512], PSB[:, 512:1024]]
    psZ = PSC[:, 0:512]
    psT = [PSC[:, 512:640], PSC[:, 640:768]]

    with es:
        kb.dma(c_ident[:], ident)
        kb.dma(c_identb[:], identb)
        kb.dma(c_triu[:], triu)
        kb.dma(c_trilm[:], tril_m)
        kb.dma(c_ones1[:], onescol)
        kb.dma(c_onesd[:], onesd)
        kb.dma(c_e127[:], e127)
        kb.copy("dve", c_ones1b[:], c_ones1[:])
        kb.copy("dve", c_onesdb[:], c_onesd[:])
        kb.dma(c_bab[:], bab)
        kb.dma(onw_sb[:], onw)
        t16a = sbuf("t16a", [128, NH], F32)
        t16b = sbuf("t16b", [128, NH], F32)
        kb.dma(lam_sb[:], lamT)
        softplus_dev(kb, lam_sb[:], lam_sb[:], t16a[:], t16b[:])
        kb.ts("dve", lam_sb[:], lam_sb[:], 0.25, ALU.add)
        kb.ts("dve", lamE2[:], lam_sb[:], 1e-8, ALU.add)
        for h in range(16):
            kb.ts("dve", lamdb[h][:], c_ident[:], lam_sb[:, h:h + 1], ALU.mult)

        if "p0" in cfg["phases"]:
            sx = [BF[0], BF[1]]
            xnt = [BF[2], BF[3]]
            stats = sbuf("p0_stats", [128, 24], F32)
            mv = sbuf("p0_mv", [128, 2], F32)
            rstd = sbuf("p0_rstd", [128, 1], F32)
            xfmb = sbuf("p0_xfmb", [128, 128], BF16)
            pT = psT
            for tt in range(16):
                b = tt % 2
                kb.dma(sx[b][:], xin[tt * 128:(tt + 1) * 128, :], grp=f"dx{b}")
                for j in range(4):
                    kb.op("dve", lambda g, j=j, b=b: g.bn_stats(
                        out=stats[:, j * 6:(j + 1) * 6], in_=sx[b][:, j * 512:(j + 1) * 512]),
                        reads=[sx[b]], writes=[stats])
                kb.op("dve", lambda g: g.bn_aggr(out=mv[:], in_=stats[:]),
                      reads=[stats], writes=[mv])
                kb.act(rstd[:], mv[:, 1:2], AF.Ln, bias=1e-5)
                kb.act(rstd[:], rstd[:], AF.Exp, scale=-0.5)
                kb.ts("dve", xnt[b][:], sx[b][:], mv[:, 0:1], ALU.subtract,
                      rstd[:, 0:1], ALU.mult)
                kb.dma(XN_TM[tt * 128:(tt + 1) * 128, :], xnt[b][:], q="act", grp="sx")
                for ct in range(16):
                    pb = ct % 2
                    kb.trans(pT[pb], xnt[b][:, ct * 128:(ct + 1) * 128], c_ident[:])
                    kb.copy("act", xfmb[:], pT[pb])
                    kb.dma(XN_FM[ct * 128:(ct + 1) * 128, tt * 128:(tt + 1) * 128],
                           xfmb[:], q="act", grp="sfm")

        if "p1" in cfg["phases"]:
            xnfm_sv = [BH[1], BH[2], BH[3], BH[4]]
            class _XN:
                def __getitem__(self, sl):
                    raise RuntimeError("use xnfm_at")
            def xnfm_at(ct, lo, hi):
                return xnfm_sv[ct // 4][:, (ct % 4) * 512 + lo:(ct % 4) * 512 + hi]
            wfp = [sbuf(f"p1_wfp{i}", [128, 512], F32) for i in range(2)]
            wbf = [sbuf(f"p1_wbf{i}", [128, 512], BF16) for i in range(2)]
            qtm = [BF[0], BF[1], BF[2], BF[3]]
            sq2 = BF[4]
            n2 = sbuf("p1_n2", [128, NH], F32)
            scal = sbuf("p1_scal", [128, NH], F32)
            ktb = BH[0]
            gb = sbuf("p1_gb", [128, 32], F32)
            gsm = sbuf("p1_gsm", [128, NH], F32)
            t16c = sbuf("p1_t16c", [128, NH], F32)
            fmb = sbuf("p1_fmb", [128, 128], BF16)
            fmf = sbuf("p1_fmf", [128, 128], F32)

            def proj_assemble(Wap, sblk):
                t0 = sblk * 512
                for nt in range(4):
                    for ct in range(16):
                        b = ct % 2
                        kb.dma(wfp[b][:], Wap[ct * 128:(ct + 1) * 128,
                                              nt * 512:(nt + 1) * 512], grp=f"dw{b}")
                        kb.copy("dve", wbf[b][:], wfp[b][:])
                        for tb in range(4):
                            kb.mm(psP[tb], xnfm_at(ct, tb * 128, (tb + 1) * 128),
                                  wbf[b][:], start=(ct == 0), stop=(ct == 15))
                    for tb in range(4):
                        kb.act(qtm[tb][:, nt * 512:(nt + 1) * 512], psP[tb], AF.Silu)

            for sblk in range(4):
                t0 = sblk * 512
                for ct in range(16):
                    kb.dma(xnfm_at(ct, 0, 512),
                           XN_FM[ct * 128:(ct + 1) * 128, t0:t0 + 512], grp="dxf")
                proj_assemble(wq, sblk)
                for tb in range(4):
                    ch = sblk * 4 + tb
                    q3 = qtm[tb][:].rearrange("p (h d) -> p h d", d=HD)
                    kb.act(sq2[:], qtm[tb][:], AF.Square)
                    kb.red("dve", n2[:], sq2[:].rearrange("p (h d) -> p h d", d=HD),
                           ALU.add)
                    kb.ts("dve", scal[:], n2[:], 1e-24, ALU.max, 1e-10, ALU.mult)
                    kb.dma(THR[ch], scal[:], q="act", grp="sth")
                    kb.act(t16c[:], n2[:], AF.Ln)
                    kb.act(t16c[:], t16c[:], AF.Exp, scale=0.5)
                    kb.ts("dve", t16c[:], t16c[:], 1e-12, ALU.max)
                    kb.op("dve", lambda g: g.reciprocal(out=t16c[:], in_=t16c[:]),
                          reads=[t16c], writes=[t16c])
                    kb.dma(RQ[ch], t16c[:], q="act", grp="sth")
                    for h in range(NH):
                        pb = h % 2
                        kb.trans(psT[pb], qtm[tb][:, h * HD:(h + 1) * HD], c_ident[:])
                        kb.copy("dve", fmf[:], psT[pb])
                        kb.dma(QFM[ch, h], fmf[:], q="act", grp="sq")
                proj_assemble(wk, sblk)
                for tb in range(4):
                    ch = sblk * 4 + tb
                    kb.act(sq2[:], qtm[tb][:], AF.Square)
                    kb.red("dve", n2[:], sq2[:].rearrange("p (h d) -> p h d", d=HD),
                           ALU.add)
                    kb.act(scal[:], n2[:], AF.Ln)
                    kb.act(scal[:], scal[:], AF.Exp, scale=0.5)
                    kb.ts("dve", scal[:], scal[:], 1e-12, ALU.max)
                    kb.op("dve", lambda g: g.reciprocal(out=scal[:], in_=scal[:]),
                          reads=[scal], writes=[scal])
                    for h in range(NH):
                        kb.ts("dve", sq2[:, h * HD:(h + 1) * HD],
                              qtm[tb][:, h * HD:(h + 1) * HD], scal[:, h:h + 1],
                              ALU.mult)
                    kb.copy("act", ktb[:], sq2[:])
                    kb.dma(KTM[ch], ktb[:], q="act", grp="sk")
                    for h in range(NH):
                        pb = h % 2
                        kb.trans(psT[pb], sq2[:, h * HD:(h + 1) * HD], c_ident[:])
                        kb.copy("dve", fmb[:], psT[pb])
                        kb.dma(KFM[ch, h], fmb[:], q="act", grp="sk")
                proj_assemble(wv, sblk)
                for tb in range(4):
                    ch = sblk * 4 + tb
                    kb.copy("dve", ktb[:], qtm[tb][:])
                    kb.dma(VTM[ch], ktb[:], q="act", grp="sv")
                    for h in range(NH):
                        pb = h % 2
                        kb.trans(psT[pb], qtm[tb][:, h * HD:(h + 1) * HD],
                                 c_ident[:])
                        kb.copy("dve", fmb[:], psT[pb])
                        kb.dma(VFM[ch, h], fmb[:], q="act", grp="sv")
                for tb in range(4):
                    ch = sblk * 4 + tb
                    for ct in range(16):
                        b = ct % 2
                        kb.dma(wfp[b][:, :32], wab[ct * 128:(ct + 1) * 128, :],
                               grp=f"dw{b}")
                        kb.copy("dve", wbf[b][:, :32], wfp[b][:, :32])
                        kb.mm(psZ[:, 0:32], xnfm_at(ct, tb * 128, (tb + 1) * 128),
                              wbf[b][:, :32], start=(ct == 0), stop=False)
                    kb.copy("dve", wbf[0][:1, :32], c_bab[:])
                    kb.mm(psZ[:, 0:32], c_ones1b[:], wbf[0][:1, :32],
                          start=False, stop=True)
                    kb.copy("dve", gb[:], psZ[:, 0:32])
                    kb.act(t16c[:], gb[:, 16:32], AF.Sigmoid)
                    kb.dma(BETA[ch], t16c[:], q="act", grp="sg")
                    log_sigmoid_dev(kb, gsm[:], gb[:, :16], t16c[:], n2[:])
                    kb.mm(psZ[:, 64:80], c_triu[:], gsm[:])
                    kb.copy("dve", t16c[:], psZ[:, 64:80])
                    kb.dma(GTM[ch], t16c[:], q="act", grp="sg")

        T = dict(sbuf=sbuf, psum=psum, BF=BF, BH=BH, KTM=KTM, VTM=VTM, KFM=KFM, VFM=VFM,
                 QFM=QFM, GTM=GTM, BETA=BETA, THR=THR, RQ=RQ, CIN=CIN,
                 COUT=COUT, ONORM=ONORM, ATTNO=ATTNO, XN_TM=XN_TM, xin=xin,
                 wo=wo, c_ident=c_ident, c_trilm=c_trilm, c_ones1=c_ones1,
                 c_ones1b=c_ones1b, c_onesd=c_onesd, c_onesdb=c_onesdb,
                 c_e127=c_e127, c_identb=c_identb, lam_sb=lam_sb, lamdb=lamdb, lamE2=lamE2,
                 onw_sb=onw_sb, PSA=PSA, PSB=PSB, PSC=PSC, PSD=PSD,
                 cmask=cmask, cmaskcol=cmaskcol, out=out,
                 w1T=w1T, w2T=w2T, w3T=w3T, g1r=g1r, g2r=g2r, g3r=g3r,
                 XQ1=XQ1, XQ2=XQ2, XQ3=XQ3, HG3=HG3)
        cfg["dumps"] = dbg_out
        if ("p2a" in cfg["phases"] or "p2c" in cfg["phases"]
                or "p2b" in cfg["phases"]):
            emit_p2(kb, cfg, T)
        if "p2d" in cfg["phases"]:
            emit_p2d(kb, cfg, T)
        if {"p3pre", "p3mid", "p3b"} & cfg["phases"]:
            emit_p3(kb, cfg, T)

        for e in ("sp",):
            for gname, cnt in list(kb.cnt.items()):
                if gname in ENGS or cnt == 0:
                    continue
                kb.eng("sp").wait_ge(kb.sem[gname], cnt)

    return nc, kb


def make_core_inputs(inp, core, cfg):
    b, qt = core // 4, core % 4
    TOK = 2048
    x = np.ascontiguousarray(inp["L0_attn_x"][b, qt * TOK:(qt + 1) * TOK, :], )
    ii = np.arange(128)
    m = {
        "xin": x.astype(np.float32),
        "wq": inp["L0_attn_Wq"], "wk": inp["L0_attn_Wk"], "wv": inp["L0_attn_Wv"],
        "wab": np.ascontiguousarray(np.concatenate(
            [inp["L0_attn_Wa"], inp["L0_attn_Wb"]], 1)),
        "bab": np.concatenate([inp["L0_attn_ba"], inp["L0_attn_bb"]])[None, :],
        "lamT": np.ascontiguousarray(inp["L0_attn_lambda_params"].reshape(16, 128).T),
        "onw": inp["L0_attn_o_norm_weight"][:, None],
        "wo": inp["L0_attn_Wo"],
        "w1T": np.ascontiguousarray(inp["L0_ffn_w1"].T),
        "w2T": np.ascontiguousarray(inp["L0_ffn_w2"].T),
        "w3T": np.ascontiguousarray(inp["L0_ffn_w3"].T),
        "g1r": inp["L0_ffn_g1"][None, :], "g2r": inp["L0_ffn_g2"][None, :],
        "g3r": inp["L0_ffn_g3"][None, :],
        "ident": np.eye(128, dtype=np.float32),
        "identb": np.eye(128).astype(BFnp),
        "triu": (ii[:, None] <= ii[None, :]).astype(np.float32),
        "tril_m": (ii[:, None] <= ii[None, :]).astype(np.float32),
        "onescol": np.ones((1, 128), np.float32),
        "onesd": np.ones((128, 1), np.float32),
        "cmask": np.zeros((8, 8), np.float32),
        "cmaskcol": np.zeros((8, 1), np.float32),
        "e127": np.eye(128, dtype=np.float32)[:, 127:128],
        "onescolb": np.ones((1, 128)).astype(BFnp),
        "onesdb": np.ones((128, 1)).astype(BFnp),
    }
    b_, qt_ = core // 4, core % 4
    preds = [r for r in range(b_ * 4, b_ * 4 + qt_)]
    cmc = np.zeros((8, 1), np.float32)
    cm2 = np.zeros((8, 8), np.float32)
    for r in preds:
        cmc[r, 0] = 1.0
        for rp in range(r + 1, b_ * 4 + qt_):
            cm2[rp, r] = 1.0
    m["cmask"] = cm2
    m["cmaskcol"] = cmc
    return {k: np.ascontiguousarray(v.astype(np.float32) if v.dtype == np.float64
                                    else v) for k, v in m.items()}


def emit_p2(kb, cfg, T):
    nc = kb.nc
    NH, HD, CC = 16, 128, 128
    NCH = cfg.get("n_chunks", 16)
    sbuf, psum = T["sbuf"], T["psum"]
    (KTM, VTM, KFM, VFM, QFM, GTM, BETA, THR, RQ, CIN, COUT, ONORM) = (
        T["KTM"], T["VTM"], T["KFM"], T["VFM"], T["QFM"], T["GTM"], T["BETA"],
        T["THR"], T["RQ"], T["CIN"], T["COUT"], T["ONORM"])
    c_ident, c_trilm, c_ones1, c_ones1b, c_onesd, c_onesdb, c_e127 = (
        T["c_ident"], T["c_trilm"], T["c_ones1"], T["c_ones1b"], T["c_onesd"],
        T["c_onesdb"], T["c_e127"])
    c_identb = T["c_identb"]
    lam_sb, lamE2, lamdb = T["lam_sb"], T["lamE2"], T["lamdb"]
    onw_sb = T["onw_sb"]
    PSA, PSB, PSC, PSD = T["PSA"], T["PSB"], T["PSC"], T["PSD"]
    S_u = (PSA, PSC)
    Q_u = (PSB, PSD)
    BF, BH = T["BF"], T["BH"]

    A0st = [sbuf(f"A0st{h}", [128, 128], F32) for h in range(NH)]
    HKst = [sbuf(f"HKst{h}", [128, 128], F32) for h in range(NH)]
    ktm_sb = BH[1]
    vtm_sb = BH[2]
    kfm_sb = BH[3]
    vfm_sb = BH[4]
    qfm_sb = BF[7]
    Gt = sbuf("Gt", [128, NH], F32)
    Bt = sbuf("Bt", [128, NH], F32)
    Tt = sbuf("Tt", [128, NH], F32)
    Rqt = sbuf("Rqt", [128, NH], F32)
    wcol = sbuf("wcol", [128, NH], F32)
    eGlb = sbuf("eGlb", [128, NH], F32)
    glrow = sbuf("glrow", [1, NH], F32)
    dgacc = sbuf("dgacc", [1, NH], F32)
    kwb = sbuf("kwb", [128, 128], BF16)
    vwb = sbuf("vwb", [128, 128], BF16)
    t16a = sbuf("p2_t16a", [128, NH], F32)
    t16b = sbuf("p2_t16b", [128, NH], F32)
    t16c = sbuf("p2_t16c", [128, NH], F32)
    t16d = sbuf("p2_t16d", [128, NH], F32)

    def glast_wcol(c):
        kb.mm(PSC[:1, 0:16], c_e127[:], Gt[:])
        kb.copy("dve", glrow[:], PSC[:1, 0:16])
        kb.mm(PSC[:, 16:32], c_ones1[:], glrow[:])
        kb.tt("dve", t16a[:], PSC[:, 16:32], Gt[:], ALU.subtract)
        kb.ts("dve", t16a[:], t16a[:], 0.0, ALU.min)
        kb.act(t16b[:], t16a[:], AF.Exp)
        kb.tt("dve", wcol[:], t16b[:], Bt[:], ALU.mult)
        kb.act(eGlb[:], PSC[:, 16:32], AF.Exp)

    def state_update(c):
        for h in range(NH):
            u, hh = h // 8, h % 8
            kb.ts("dve", kwb[:], ktm_sb[:, h * HD:(h + 1) * HD],
                  wcol[:, h:h + 1], ALU.mult)
            kb.ts("dve", vwb[:], vtm_sb[:, h * HD:(h + 1) * HD],
                  wcol[:, h:h + 1], ALU.mult)
            kb.mm(S_u[u][:, hh * 128:(hh + 1) * 128], kwb[:],
                  ktm_sb[:, h * HD:(h + 1) * HD])
            kb.mm(Q_u[u][:, hh * 128:(hh + 1) * 128], vwb[:],
                  ktm_sb[:, h * HD:(h + 1) * HD])
            kb.stt(A0st[h][:], A0st[h][:], eGlb[:, h:h + 1],
                   S_u[u][:, hh * 128:(hh + 1) * 128], ALU.mult, ALU.add)
            kb.stt(HKst[h][:], HKst[h][:], eGlb[:, h:h + 1],
                   Q_u[u][:, hh * 128:(hh + 1) * 128], ALU.mult, ALU.add)

    if "p2a" in cfg["phases"]:
        for h in range(NH):
            kb.memset("dve", A0st[h][:], 0.0)
            kb.memset("dve", HKst[h][:], 0.0)
        kb.memset("dve", dgacc[:], 0.0)
        for c in range(NCH):
            kb.dma(ktm_sb[:], KTM[c], grp="p2k")
            kb.dma(vtm_sb[:], VTM[c], grp="p2v")
            kb.dma(Gt[:], GTM[c], grp="p2g")
            kb.dma(Bt[:], BETA[c], grp="p2g")
            glast_wcol(c)
            kb.tt("dve", dgacc[:], dgacc[:], glrow[:], ALU.add)
            state_update(c)
        for h in range(NH):
            kb.dma(CIN[:, h * 128:(h + 1) * 128], A0st[h][:], grp="cinw")
            kb.dma(CIN[:, 2048 + h * 128:2048 + (h + 1) * 128], HKst[h][:],
                   grp="cinw")
        kb.memset("dve", t16d[:], 0.0)
        kb.copy("dve", t16d[0:1, :], dgacc[:])
        kb.dma(CIN[:, 4096:4112], t16d[:], grp="cinw")
        kb.op("pool", lambda g: g.collective_compute(
            "AllGather", ALU.bypass, replica_groups=[list(range(8))],
            ins=[CIN], outs=[COUT]), reads=[CIN], writes=[COUT])

    if "p2b" in cfg["phases"]:
        cmsk = sbuf("cmsk", [8, 8], F32)
        cmcol = sbuf("cmcol", [8, 1], F32)
        dgall = sbuf("dgall", [8, NH], F32)
        wt8 = sbuf("wt8", [8, NH], F32)
        wflat = sbuf("wflat", [1, 128], F32)
        wBs = sbuf("wBs", [128, 128], F32)
        gsbA = BF[0]
        gsbB = BF[1]
        kb.dma(cmsk[:], T["cmask"])
        kb.dma(cmcol[:], T["cmaskcol"])
        for r in range(8):
            kb.dma(dgall[r:r + 1, :], COUT[r * 128:r * 128 + 1, 4096:4112],
                   grp="p2b")
        kb.mm(PSC[:8, 0:16], cmsk[:], dgall[:])
        kb.act(wt8[:], PSC[:8, 0:16], AF.Exp)
        kb.ts("dve", wt8[:], wt8[:], cmcol[:, 0:1], ALU.mult)
        kb.dma(wflat[:], wt8[:], grp="p2b")
        kb.mm(PSC[:, 32:160], c_ones1[:], wflat[:])
        kb.copy("dve", wBs[:], PSC[:, 32:160])
        for h in range(NH):
            kb.memset("dve", A0st[h][:], 0.0)
            kb.memset("dve", HKst[h][:], 0.0)
        for r in range(8):
            kb.dma(gsbA[:], COUT[r * 128:(r + 1) * 128, 0:2048], grp="p2ga")
            kb.dma(gsbB[:], COUT[r * 128:(r + 1) * 128, 2048:4096], grp="p2gb")
            for h in range(NH):
                kb.stt(A0st[h][:], gsbA[:, h * 128:(h + 1) * 128],
                       wBs[:, r * 16 + h:r * 16 + h + 1], A0st[h][:],
                       ALU.mult, ALU.add)
                kb.stt(HKst[h][:], gsbB[:, h * 128:(h + 1) * 128],
                       wBs[:, r * 16 + h:r * 16 + h + 1], HKst[h][:],
                       ALU.mult, ALU.add)

    if "p2c" in cfg["phases"]:
        Xcg = BF[0]
        Rcg = BF[1]
        Pcg = BF[2]
        Pb = BH[5]
        Psc = BH[6]
        swb = BH[7]
        pqb = BH[8]
        rrb = BH[9]
        wm = BH[10]
        k2tm = BH[11]
        A0p = BH[12]
        hkvb = BH[13]
        expGb = BF[3]
        rqb = BF[4]
        egrqb = BF[5]
        ab_sb = BH[14]
        dn_sb = BF[6]
        onorm_sb = BH[15]
        gtr2 = sbuf("gtr2", [128, 32], F32)
        tr32 = sbuf("tr32", [32, 128], F32)
        rows32 = sbuf("rows32", [1, 4096], F32)
        T["rows32"] = rows32
        dgtile = sbuf("dgtile", [128, 128], F32)
        bmtile = sbuf("bmtile", [128, 128], F32)
        wexp = sbuf("wexp", [128, 128], F32)
        datile = sbuf("datile", [128, 128], F32)
        dsc = sbuf("dsc", [128, NH], F32)
        dnsc = sbuf("dnsc", [128, NH], F32)
        pqsc = sbuf("pqsc", [128, NH], F32)
        donec = sbuf("donec", [128, NH], F32)
        ndone = sbuf("ndone", [128, NH], F32)
        alsc = sbuf("alsc", [128, NH], F32)
        besc = sbuf("besc", [128, NH], F32)
        s16a = sbuf("s16a", [128, NH], F32)
        s16b = sbuf("s16b", [128, NH], F32)
        asbu = [sbuf(f"asbu{u}", [8, 128], F32) for u in range(2)]
        abrow = [sbuf(f"abrow{u}", [1, 1024], F32) for u in range(2)]

        def U(t, u):
            return (t[:, u * 1024:(u + 1) * 1024], u)

        def Uh(t, u, hh, w=128):
            return (t[:, u * 1024 + hh * w:u * 1024 + (hh + 1) * w], u)

        def Us(t, u):
            return (t[:, u * 8:(u + 1) * 8], u)

        def Ush(t, u, hh):
            return (t[:, u * 8 + hh:u * 8 + hh + 1], u)

        def mv_unit(u, Pb_src, Psc_src):
            for hh in range(8):
                h = u * 8 + hh
                kb.mm((S_u[u][:, hh * 128:(hh + 1) * 128], u),
                      Uh(kfm_sb, u, hh), Uh(Pb_src, u, hh))
            kb.tt("dve", U(swb, u), U(wm, u), (S_u[u][:], u), ALU.mult)
            for hh in range(8):
                h = u * 8 + hh
                kb.mm((Q_u[u][:, hh * 128:(hh + 1) * 128], u),
                      Uh(ktm_sb, u, hh), Uh(swb, u, hh),
                      start=True, stop=False)
                kb.mm((Q_u[u][:, hh * 128:(hh + 1) * 128], u),
                      Uh(A0p, u, hh), Uh(Psc_src, u, hh),
                      start=False, stop=False)
                kb.mm((Q_u[u][:, hh * 128:(hh + 1) * 128], u),
                      lamdb[h][:], Uh(Pb_src, u, hh),
                      start=False, stop=True)

        def colreduce_u(u, src, lo):
            for hh in range(8):
                kb.mm((S_u[u][:, lo + hh:lo + hh + 1], u),
                      Uh(src, u, hh), c_onesdb[:])

        def bcast_u(u, src16u):
            kb.trans((S_u[u][0:8, 128:256], u), src16u, c_ident[:])
            kb.copy("dve", asbu[u][:], (S_u[u][0:8, 128:256], u))
            kb.dma(abrow[u][:], asbu[u][:], grp=f"fl{u}")
            kb.mm((S_u[u][:, 0:512], u), c_ones1[:], abrow[u][:1, 0:512])
            kb.mm((S_u[u][:, 512:1024], u), c_ones1[:], abrow[u][:1, 512:1024])
            kb.copy("act", U(ab_sb, u), (S_u[u][:], u))

        NIT = cfg.get("n_iter", 30)
        dumps = cfg.get("dumps", {})
        def dump(name, ap):
            if name in dumps:
                kb.dma(dumps[name], ap, q="pool", grp="dmp")
        if "p2b" not in cfg["phases"]:
            for h in range(NH):
                kb.memset("dve", A0st[h][:], 0.0)
                kb.memset("dve", HKst[h][:], 0.0)
        for c in range(NCH):
            kb.dma(ktm_sb[:], KTM[c], grp="p2k")
            kb.dma(vtm_sb[:], VTM[c], grp="p2v")
            kfm_ap = KFM[c].rearrange("h d t -> d h t")
            vfm_ap = VFM[c].rearrange("h d t -> d h t")
            qfm_ap = QFM[c].rearrange("h d t -> d h t")
            kb.dma(kfm_sb[:], kfm_ap, grp="p2k")
            kb.dma(vfm_sb[:], vfm_ap, grp="p2v")
            kb.dma(qfm_sb[:], qfm_ap, grp="p2q")
            kb.dma(Gt[:], GTM[c], grp="p2g")
            kb.dma(Bt[:], BETA[c], grp="p2g")
            kb.dma(Tt[:], THR[c], grp="p2g")
            kb.dma(Rqt[:], RQ[c], grp="p2g")

            kb.copy("dve", gtr2[:, 0:16], Gt[:])
            kb.copy("dve", gtr2[:, 16:32], Rqt[:])
            kb.trans(PSC[:32, 128:256], gtr2[:], c_ident[:])
            kb.copy("dve", tr32[:], PSC[:32, 128:256])
            kb.dma(rows32[:], tr32[:], grp="fl0")
            for h in range(NH):
                u, hh = h // 8, h % 8
                kb.mm((S_u[u][:, 384:512], u), c_ones1[:],
                      rows32[:1, h * 128:(h + 1) * 128])
                kb.ts("dve", dgtile[:], (S_u[u][:, 384:512], u), Gt[:, h:h + 1],
                      ALU.subtract, 0.0, ALU.min)
                kb.act(wexp[:], dgtile[:], AF.Exp)
                kb.ts("dve", bmtile[:], c_trilm[:], Bt[:, h:h + 1], ALU.mult)
                kb.tt("dve", Uh(wm, u, hh), wexp[:], bmtile[:], ALU.mult)
            for u in range(2):
                for qq in range(2):
                    kb.mm((S_u[u][:, qq * 512:(qq + 1) * 512], u), c_ones1[:],
                          rows32[:1, 2048 + u * 1024 + qq * 512:
                                 2048 + u * 1024 + (qq + 1) * 512])
                kb.copy("dve", U(rqb, u), (S_u[u][:], u))
            kb.act(rows32[:1, 0:2048], rows32[:1, 0:2048], AF.Exp)
            for u in range(2):
                for qq in range(2):
                    kb.mm((S_u[u][:, qq * 512:(qq + 1) * 512], u), c_ones1[:],
                          rows32[:1, u * 1024 + qq * 512:
                                 u * 1024 + (qq + 1) * 512])
                kb.copy("dve", U(expGb, u), (S_u[u][:], u))
                kb.tt("pool", U(egrqb, u), U(expGb, u), U(rqb, u), ALU.mult)
            kb.act(k2tm[:], ktm_sb[:], AF.Square)
            for h in range(NH):
                u, hh = h // 8, h % 8
                kb.copy("act", Uh(A0p, u, hh), A0st[h][:])
                kb.copy("act", Uh(hkvb, u, hh), HKst[h][:])
            for h in range(NH):
                u, hh = h // 8, h % 8
                kb.mm((Q_u[u][:, hh * 128:(hh + 1) * 128], u),
                      Uh(k2tm, u, hh), Uh(wm, u, hh), start=True, stop=False)
                kb.tt("dve", datile[:], A0st[h][:], c_ident[:], ALU.mult)
                kb.mm((S_u[u][:1, 0:128], u), c_onesd[:], datile[:])
                kb.copy("dve", (rows32[:1, 2048 + h * 128:2048 + (h + 1) * 128],
                                u), (S_u[u][:1, 0:128], u))
                kb.mm((Q_u[u][:, hh * 128:(hh + 1) * 128], u),
                      (rows32[:1, 2048 + h * 128:2048 + (h + 1) * 128], u),
                      (rows32[:1, h * 128:(h + 1) * 128], "ro"),
                      start=False, stop=True)
            for h in range(NH):
                u, hh = h // 8, h % 8
                kb.act(Uh(dn_sb, u, hh), (Q_u[u][:, hh * 128:(hh + 1) * 128], u),
                       AF.Ln, bias=lamE2[:, h:h + 1])
            for u in range(2):
                kb.act(U(dn_sb, u), U(dn_sb, u), AF.Exp, scale=-1.0)
                kb.tt("dve", U(Xcg, u), U(qfm_sb, u), U(dn_sb, u), ALU.mult)
                kb.copy("act", U(Pb, u), U(Xcg, u))
                kb.tt("pool", U(Psc, u), U(Xcg, u), U(expGb, u), ALU.mult)
            for u in range(2):
                mv_unit(u, Pb, Psc)
            for u in range(2):
                kb.tt("dve", U(Rcg, u), U(qfm_sb, u), (Q_u[u][:], u),
                      ALU.subtract)
                kb.copy("dve", U(Pcg, u), U(Rcg, u))
                kb.act(U(rrb, u), U(Rcg, u), AF.Square)
                colreduce_u(u, rrb, 0)
                kb.copy("dve", Us(dsc, u), (S_u[u][:, 0:8], u))
                kb.memset("dve", Us(donec, u), 0.0)
            if c == 0:
                dump("X0", Xcg[:])

            def seg1(u):
                kb.tt("dve", Us(s16a, u), Us(dsc, u), Us(Tt, u), ALU.is_lt)
                kb.tt("dve", Us(donec, u), Us(donec, u), Us(s16a, u), ALU.max)
                kb.copy("act", U(Pb, u), U(Pcg, u))
                kb.tt("pool", U(Psc, u), U(Pcg, u), U(expGb, u), ALU.mult)
                mv_unit(u, Pb, Psc)
                kb.tt("dve", U(pqb, u), U(Pcg, u), (Q_u[u][:], u), ALU.mult)
                colreduce_u(u, pqb, 0)
                kb.copy("dve", Us(pqsc, u), (S_u[u][:, 0:8], u))

            def seg2(u):
                kb.act(Us(s16a, u), Us(pqsc, u), AF.Abs)
                kb.tt("dve", Us(s16a, u), Us(s16a, u), Us(Tt, u), ALU.is_lt)
                kb.tt("dve", Us(donec, u), Us(donec, u), Us(s16a, u), ALU.max)
                kb.ts("dve", Us(ndone, u), Us(donec, u), -1.0, ALU.mult,
                      1.0, ALU.add)
                kb.tt("dve", Us(s16a, u), Us(pqsc, u), Us(ndone, u), ALU.mult)
                kb.tt("dve", Us(s16a, u), Us(s16a, u), Us(donec, u), ALU.add)
                kb.op("dve", lambda g, u=u: g.reciprocal(
                    out=s16b[:, u * 8:(u + 1) * 8],
                    in_=s16a[:, u * 8:(u + 1) * 8]),
                    reads=[Us(s16a, u)], writes=[Us(s16b, u)])
                kb.tt("dve", Us(alsc, u), Us(dsc, u), Us(s16b, u), ALU.mult)
                kb.tt("dve", Us(alsc, u), Us(alsc, u), Us(ndone, u), ALU.mult)
                bcast_u(u, Us(alsc, u))
                kb.tt("pool", U(pqb, u), U(ab_sb, u), U(Pcg, u), ALU.mult)
                kb.tt("pool", U(Xcg, u), U(Xcg, u), U(pqb, u), ALU.add)
                kb.tt("dve", U(rrb, u), U(ab_sb, u), (Q_u[u][:], u), ALU.mult)
                kb.tt("dve", U(Rcg, u), U(Rcg, u), U(rrb, u), ALU.subtract)
                kb.act(U(rrb, u), U(Rcg, u), AF.Square)
                colreduce_u(u, rrb, 8)
                kb.copy("dve", Us(dnsc, u), (S_u[u][:, 8:16], u))
                kb.tt("dve", Us(s16a, u), Us(dnsc, u), Us(dsc, u), ALU.subtract)
                kb.tt("dve", Us(s16a, u), Us(s16a, u), Us(ndone, u), ALU.mult)
                kb.tt("dve", Us(dnsc, u), Us(dsc, u), Us(s16a, u), ALU.add)
                kb.tt("dve", Us(s16a, u), Us(dsc, u), Us(ndone, u), ALU.mult)
                kb.tt("dve", Us(s16a, u), Us(s16a, u), Us(donec, u), ALU.add)
                kb.op("dve", lambda g, u=u: g.reciprocal(
                    out=s16b[:, u * 8:(u + 1) * 8],
                    in_=s16a[:, u * 8:(u + 1) * 8]),
                    reads=[Us(s16a, u)], writes=[Us(s16b, u)])
                kb.tt("dve", Us(besc, u), Us(dnsc, u), Us(s16b, u), ALU.mult)
                kb.tt("dve", Us(besc, u), Us(besc, u), Us(ndone, u), ALU.mult)
                kb.copy("dve", Us(dsc, u), Us(dnsc, u))
                bcast_u(u, Us(besc, u))
                kb.tt("pool", U(pqb, u), U(ab_sb, u), U(Pcg, u), ALU.mult)
                kb.tt("dve", U(Pcg, u), U(Rcg, u), U(pqb, u), ALU.add)

            for it in range(NIT):
                seg1(0)
                seg1(1)
                seg2(0)
                seg2(1)

            for u in range(2):
                kb.tt("dve", U(pqb, u), U(Xcg, u), U(rqb, u), ALU.mult)
                kb.tt("pool", U(rrb, u), U(Xcg, u), U(egrqb, u), ALU.mult)
                for hh in range(8):
                    kb.mm((S_u[u][:, hh * 128:(hh + 1) * 128], u),
                          Uh(vfm_sb, u, hh), Uh(pqb, u, hh))
                kb.tt("dve", U(swb, u), U(wm, u), (S_u[u][:], u), ALU.mult)
                for hh in range(8):
                    h = u * 8 + hh
                    kb.mm((Q_u[u][:, hh * 128:(hh + 1) * 128], u),
                          Uh(ktm_sb, u, hh), Uh(swb, u, hh),
                          start=True, stop=False)
                    kb.mm((Q_u[u][:, hh * 128:(hh + 1) * 128], u),
                          Uh(hkvb, u, hh), Uh(rrb, u, hh),
                          start=False, stop=True)
                kb.act(U(pqb, u), (Q_u[u][:], u), AF.Square)
                colreduce_u(u, pqb, 0)
                kb.ts("dve", Us(s16a, u), (S_u[u][:, 0:8], u), 1.0 / 128,
                      ALU.mult, 1e-6, ALU.add)
                kb.act(Us(s16a, u), Us(s16a, u), AF.Ln)
                kb.act(Us(s16a, u), Us(s16a, u), AF.Exp, scale=-0.5)
                bcast_u(u, Us(s16a, u))
                kb.stt(U(onorm_sb, u), (Q_u[u][:], u), onw_sb[:, 0:1],
                       U(ab_sb, u), ALU.mult, ALU.mult)
                kb.dma(ONORM[c][:, u * 1024:(u + 1) * 1024],
                       U(onorm_sb, u), grp=f"son{u}")
            glast_wcol(c)
            state_update(c)


def emit_p2d(kb, cfg, T):
    nc = kb.nc
    BF, BH = T["BF"], T["BH"]
    PSA, PSB = T["PSA"], T["PSB"]
    pd4 = [PSA[:, 0:512], PSA[:, 512:1024], PSB[:, 0:512], PSB[:, 512:1024]]
    ONORM, ATTNO, xin, wo = T["ONORM"], T["ATTNO"], T["xin"], T["wo"]
    sbuf = T["sbuf"]
    wod = [sbuf(f"wod{i}", [128, 512], F32) for i in range(2)]
    wob = [sbuf(f"wob{i}", [128, 512], BF16) for i in range(2)]
    NCH = cfg.get("n_chunks", 16)
    for tg in range(4):
        chs = [tg * 4 + i for i in range(4) if tg * 4 + i < NCH]
        if not chs:
            continue
        for i, ch in enumerate(chs):
            kb.dma(BH[1 + i][:], ONORM[ch], grp="p2d_on")
        for cq in range(4):
            for ct in range(16):
                b = ct % 2
                kb.dma(wod[b][:], wo[ct * 128:(ct + 1) * 128,
                                     cq * 512:(cq + 1) * 512], grp=f"p2dw{b}")
                kb.copy("dve", wob[b][:], wod[b][:])
                for i, ch in enumerate(chs):
                    kb.mm(pd4[i],
                          BH[1 + i][:, ct * 128:(ct + 1) * 128], wob[b][:],
                          start=(ct == 0), stop=(ct == 15))
            for i, ch in enumerate(chs):
                kb.dma(BF[4 + i][:, 0:512],
                       xin[ch * 128:(ch + 1) * 128, cq * 512:(cq + 1) * 512],
                       grp="p2dx")
                kb.tt("dve", BF[4 + i][:, 512:1024], pd4[i],
                      BF[4 + i][:, 0:512], ALU.add)
                kb.dma(ATTNO[ch * 128:(ch + 1) * 128, cq * 512:(cq + 1) * 512],
                       BF[4 + i][:, 512:1024], q="act", grp="p2do")


def emit_p3(kb, cfg, T):
    nc = kb.nc
    BF, BH = T["BF"], T["BH"]
    PSA, PSB, PSC, PSD = T["PSA"], T["PSB"], T["PSC"], T["PSD"]
    sbuf = T["sbuf"]
    XN_TM, ATTNO, out = T["XN_TM"], T["ATTNO"], T["out"]
    w1T, w2T, w3T = T["w1T"], T["w2T"], T["w3T"]
    g1r, g2r, g3r = T["g1r"], T["g2r"], T["g3r"]
    XQ1, XQ2, XQ3, HG3 = T["XQ1"], T["XQ2"], T["XQ3"], T["HG3"]
    c_ones1, c_ones1b, c_onesdb, c_ident, c_identb = (
        T["c_ones1"], T["c_ones1b"], T["c_onesdb"], T["c_ident"], T["c_identb"])
    c_onesd = T["c_onesd"]
    zi32 = sbuf("zi32", [128, 2048], I32)
    col1 = sbuf("p3_col1", [128, 1], F32)
    col2 = sbuf("p3_col2", [128, 1], F32)
    rstd = sbuf("p3_rstd", [128, 1], F32)
    inv1 = sbuf("p3_inv1", [128, 16], F32)
    inv2 = sbuf("p3_inv2", [128, 16], F32)
    h2st = sbuf("p3_h2st", [128, 16], F32)
    hmx = sbuf("p3_hmx", [128, 16], F32)
    sc3 = sbuf("p3_sc3", [128, 16], F32)
    iv3 = sbuf("p3_iv3", [128, 16], F32)
    srow = sbuf("p3_srow", [1, 8], F32)
    acol = sbuf("p3_acol", [128, 4], F32)
    thrc = sbuf("p3_thrc", [128, 4], F32)
    NCH = 16

    for wi, (wT, nblk) in enumerate(((w1T, 64), (w2T, 64), (w3T, 64))):
        wvT = wT.rearrange("a b -> (a b)").rearrange("(n p f) -> n p f",
                                                     p=128, f=2048)
        s16 = sbuf("p3_s16", [1, 16], F32) if wi == 0 else s16
        for blk in range(nblk):
            b = blk % 2
            kb.dma(BF[b][:], wvT[blk], grp=f"p3s{b}")
            kb.act(BF[2 + b][:], BF[b][:], AF.Abs)
            tgts = (PSA[:1, 0:512], PSA[:1, 512:1024],
                    PSB[:1, 0:512], PSB[:1, 512:1024])
            for q4 in range(4):
                kb.mm(tgts[q4], c_onesd[:], BF[2 + b][:, q4 * 512:(q4 + 1) * 512],
                      start=(blk == 0), stop=(blk == nblk - 1))
        for q4, tgt in enumerate((PSA[:1, 0:512], PSA[:1, 512:1024],
                                  PSB[:1, 0:512], PSB[:1, 512:1024])):
            kb.red("dve", s16[:1, q4 * 4:(q4 + 1) * 4],
                   tgt.rearrange("a (g x) -> a g x", x=128), ALU.add)
        kb.red("dve", srow[:1, wi:wi + 1], s16[:1, :], ALU.add)
    kb.ts("dve", srow[:1, 0:3], srow[:1, 0:3], 1.0 / 16777216.0, ALU.mult)
    kb.ts("dve", srow[:1, 3:6], srow[:1, 0:3], -1.0, ALU.mult, 1.0, ALU.add)
    kb.op("dve", lambda g: g.reciprocal(out=srow[:1, 3:6], in_=srow[:1, 3:6]),
          reads=[srow], writes=[srow])
    kb.ts("dve", srow[:1, 6:8], srow[:1, 0:2], 1.0, ALU.add)
    kb.tt("dve", srow[:1, 6:8], srow[:1, 6:8], srow[:1, 3:5], ALU.mult)
    kb.act(srow[:1, 6:8], srow[:1, 6:8], AF.Ln)
    kb.ts("dve", srow[:1, 6:8], srow[:1, 6:8], 0.5, ALU.mult)
    kb.ts("dve", srow[:1, 5:6], srow[:1, 2:3], 1.0, ALU.add)
    kb.tt("dve", srow[:1, 5:6], srow[:1, 5:6], srow[:1, 3:4].rearrange("a b -> a b"),
          ALU.mult) if False else None
    kb.ts("dve", srow[:1, 3:4], srow[:1, 2:3], -1.0, ALU.mult, 1.0, ALU.add)
    kb.op("dve", lambda g: g.reciprocal(out=srow[:1, 3:4], in_=srow[:1, 3:4]),
          reads=[srow], writes=[srow])
    kb.tt("dve", srow[:1, 5:6], srow[:1, 5:6], srow[:1, 3:4], ALU.mult)
    kb.act(srow[:1, 5:6], srow[:1, 5:6], AF.Ln)
    kb.ts("dve", srow[:1, 5:6], srow[:1, 5:6], 0.5, ALU.mult)
    kb.ts("dve", srow[:1, 0:3], srow[:1, 0:3], 1e-8, ALU.add,
          0.5493061443340549, ALU.mult)
    kb.mm(PSC[:, 0:8], c_ones1[:], srow[:1, 0:8])
    kb.copy("dve", thrc[:, 0:3], PSC[:, 0:3])
    kb.copy("dve", acol[:, 0:3], PSC[:, 5:8])

    def round_chain(y_f32, out_f32):
        kb.copy("dve", zi32[:], y_f32)
        kb.stt(BF[5][:], zi32[:], -1.0, y_f32, ALU.mult, ALU.add)
        kb.ts("dve", BH[14][:], BF[5][:], 0.5, ALU.is_gt)
        kb.ts("dve", BH[15][:], BF[5][:], -0.5, ALU.is_lt)
        kb.tt("pool", BF[5][:], BH[14][:], zi32[:], ALU.add)
        kb.tt("dve", out_f32, BF[5][:], BH[15][:], ALU.subtract)

    if "p3pre" in cfg["phases"]:
        for gi, grow in enumerate((g1r, g2r)):
            kb.dma(srow[:1, 0:1], grow[:1, 0:1], grp="p3g") if False else None
        g1t, g2t = BF[2], BF[3]
        grows = T["rows32"]
        for gi, (grow, gt) in enumerate(((g1r, g1t), (g2r, g2t))):
            kb.dma(grows[:1, 0:2048], grow, grp="p3g")
            for qq in range(2):
                kb.mm(PSA[:, qq * 512:(qq + 1) * 512], c_ones1[:],
                      grows[:1, qq * 512:(qq + 1) * 512])
                kb.mm(PSB[:, qq * 512:(qq + 1) * 512], c_ones1[:],
                      grows[:1, 1024 + qq * 512:1024 + (qq + 1) * 512])
            kb.copy("dve", gt[:, 0:1024], PSA[:])
            kb.copy("dve", gt[:, 1024:2048], PSB[:])
        for tb in range(NCH):
            kb.dma(BF[0][:], XN_TM[tb * 128:(tb + 1) * 128, :], grp="p3x")
            kb.act(BF[1][:], BF[0][:], AF.Square)
            kb.red("dve", col1[:], BF[1][:], ALU.add)
            kb.ts("dve", col1[:], col1[:], 1.0 / 2048, ALU.mult, 1e-8, ALU.add)
            kb.act(rstd[:], col1[:], AF.Ln)
            kb.act(rstd[:], rstd[:], AF.Exp, scale=-0.5)
            for gi, (gt, XQ, invv) in enumerate(((g1t, XQ1, inv1),
                                                 (g2t, XQ2, inv2))):
                kb.stt(BF[1][:], BF[0][:], rstd[:, 0:1], gt[:], ALU.mult,
                       ALU.mult)
                kb.red("dve", col1[:], BF[1][:], ALU.max, absval=True)
                kb.ts("dve", col1[:], col1[:], 1e-4, ALU.max)
                kb.op("dve", lambda g: g.reciprocal(out=col2[:], in_=col1[:]),
                      reads=[col1], writes=[col2])
                kb.ts("dve", invv[:, tb:tb + 1], col1[:], 1.0 / 127, ALU.mult)
                kb.ts("dve", col2[:], col2[:], 127.0, ALU.mult)
                kb.act(BF[4][:], BF[1][:], AF.Copy, scale=col2[:, 0:1])
                round_chain(BF[4][:], BF[1][:])
                for ct in range(16):
                    pb = ct % 2
                    kb.trans(PSC[:, 512 + pb * 128:512 + (pb + 1) * 128],
                             BF[1][:, ct * 128:(ct + 1) * 128], c_ident[:])
                    kb.copy("act", BH[12][:, ct * 128:(ct + 1) * 128],
                            PSC[:, 512 + pb * 128:512 + (pb + 1) * 128])
                kb.dma(XQ[:, tb * 128:(tb + 1) * 128].rearrange(
                           "(c p) t -> p c t", p=128),
                       BH[12][:], q="act", grp="p3q")

    if "p3mid" in cfg["phases"]:
        g3t = BF[6]
        grow3 = T["rows32"]
        kb.memset("dve", h2st[:], 0.0)
        kb.memset("dve", hmx[:], 0.0)
        for hb in range(16):
            kb.dma(grow3[:1, 0:512], g3r[0:1, hb * 512:(hb + 1) * 512], grp="p3g")
            kb.mm(PSC[:, 0:512], c_ones1[:], grow3[:1, 0:512])
            kb.copy("dve", g3t[:, 0:512], PSC[:, 0:512])
            for wi, (wT, wst) in enumerate(((w1T, 0), (w2T, 1))):
                for ct in range(16):
                    kb.dma(BF[0][:, 0:512], wT[ct * 128:(ct + 1) * 128,
                                               hb * 512:(hb + 1) * 512],
                           grp=f"p3w{wi}")
                    kb.ts("dve", BF[1][:, 0:512], BF[0][:, 0:512],
                          thrc[:, wi:wi + 1], ALU.is_gt)
                    kb.ts("dve", BF[1][:, 512:1024], BF[0][:, 0:512],
                          thrc[:, wi:wi + 1], ALU.is_lt) if False else None
                    kb.ts("pool", BF[4][:, 0:512], BF[0][:, 0:512],
                          0.0, ALU.subtract) if False else None
                    kb.ts("dve", BF[4][:, 0:512], BF[0][:, 0:512], -1.0, ALU.mult)
                    kb.ts("dve", BF[4][:, 512:1024], BF[4][:, 0:512],
                          thrc[:, wi:wi + 1], ALU.is_gt)
                    kb.tt("dve", BH[wst * 4 + ct // 4][:,
                          (ct % 4) * 512:(ct % 4 + 1) * 512],
                          BF[1][:, 0:512], BF[4][:, 512:1024], ALU.subtract)
            for tb in range(NCH):
                kb.dma(BH[8][:], XQ1[:, tb * 128:(tb + 1) * 128].rearrange(
                    "(c p) t -> p c t", p=128), grp="p3xs")
                kb.dma(BH[9][:], XQ2[:, tb * 128:(tb + 1) * 128].rearrange(
                    "(c p) t -> p c t", p=128), grp="p3xs")
                for ct in range(16):
                    kb.mm(PSA[:, 0:512], BH[8][:, ct * 128:(ct + 1) * 128],
                          BH[0 + ct // 4][:, (ct % 4) * 512:(ct % 4 + 1) * 512],
                          start=(ct == 0), stop=(ct == 15))
                    kb.mm(PSB[:, 0:512], BH[9][:, ct * 128:(ct + 1) * 128],
                          BH[4 + ct // 4][:, (ct % 4) * 512:(ct % 4 + 1) * 512],
                          start=(ct == 0), stop=(ct == 15))
                kb.ts("dve", col1[:], inv1[:, tb:tb + 1], 1.0, ALU.mult)
                kb.tt("dve", col1[:], col1[:], acol[:, 1:2], ALU.mult)
                kb.ts("dve", col2[:], inv2[:, tb:tb + 1], 1.0, ALU.mult)
                kb.tt("dve", col2[:], col2[:], acol[:, 2:3], ALU.mult)
                kb.act(BF[0][:, 0:512], PSA[:, 0:512], AF.Silu,
                       scale=col1[:, 0:1])
                kb.act(BF[0][:, 512:1024], PSB[:, 0:512], AF.Copy,
                       scale=col2[:, 0:1])
                kb.tt("dve", BF[1][:, 0:512], BF[0][:, 0:512],
                      BF[0][:, 512:1024], ALU.mult)
                kb.tt("dve", BH[13][:, 0:512], BF[1][:, 0:512], g3t[:, 0:512],
                      ALU.mult)
                kb.dma(HG3[tb * 128:(tb + 1) * 128, hb * 512:(hb + 1) * 512],
                       BH[13][:, 0:512], q="act", grp="p3h")
                kb.act(BF[4][:, 0:512], BF[1][:, 0:512], AF.Square,
                       accum=col1[:])
                kb.tt("dve", h2st[:, tb:tb + 1], h2st[:, tb:tb + 1], col1[:],
                      ALU.add)
                kb.red("dve", col2[:], BH[13][:, 0:512], ALU.max, absval=True)
                kb.tt("dve", hmx[:, tb:tb + 1], hmx[:, tb:tb + 1], col2[:],
                      ALU.max)

    if "p3b" in cfg["phases"]:
        kb.ts("dve", h2st[:], h2st[:], 1.0 / 8192, ALU.mult, 1e-8, ALU.add)
        kb.act(h2st[:], h2st[:], AF.Ln)
        kb.act(h2st[:], h2st[:], AF.Exp, scale=-0.5)
        kb.tt("dve", hmx[:], hmx[:], h2st[:], ALU.mult)
        kb.ts("dve", hmx[:], hmx[:], 1e-4, ALU.max)
        kb.tt("dve", iv3[:], hmx[:], acol[:, 0:1].rearrange("a b -> a b")
              if False else hmx[:], ALU.bypass) if False else None
        kb.op("dve", lambda g: g.reciprocal(out=sc3[:], in_=hmx[:]),
              reads=[hmx], writes=[sc3])
        kb.ts("dve", sc3[:], sc3[:], 127.0, ALU.mult)
        kb.tt("dve", sc3[:], sc3[:], h2st[:], ALU.mult)
        kb.ts("dve", iv3[:], hmx[:], 1.0 / 127, ALU.mult)
        for i in range(16):
            kb.tt("dve", iv3[:, i:i + 1], iv3[:, i:i + 1], acol[:, 0:1], ALU.mult)
        for tb in range(NCH):
            for hq in range(4):
                kb.dma(BH[8][:], HG3[tb * 128:(tb + 1) * 128,
                                     hq * 2048:(hq + 1) * 2048], grp="p3h2")
                kb.act(BF[1][:], BH[8][:], AF.Copy, scale=sc3[:, tb:tb + 1])
                round_chain(BF[1][:], BF[4][:])
                for ct in range(16):
                    pb = ct % 2
                    kb.trans(PSC[:, 512 + pb * 128:512 + (pb + 1) * 128],
                             BF[4][:, ct * 128:(ct + 1) * 128], c_ident[:])
                    kb.copy("act", BH[12][:, ct * 128:(ct + 1) * 128],
                            PSC[:, 512 + pb * 128:512 + (pb + 1) * 128])
                kb.dma(XQ3[hq * 2048:(hq + 1) * 2048,
                           tb * 128:(tb + 1) * 128].rearrange(
                           "(c p) t -> p c t", p=128),
                       BH[12][:], q="act", grp="p3q3")
        for c8 in range(8):
            for hh in range(16):
                for sub in range(4):
                    r0 = hh * 512 + sub * 128
                    kb.dma(BF[0][:, 0:256], w3T[r0:r0 + 128,
                                                c8 * 256:(c8 + 1) * 256],
                           grp="p3w3")
                    kb.ts("dve", BF[1][:, 0:256], BF[0][:, 0:256],
                          thrc[:, 2:3], ALU.is_gt)
                    kb.ts("dve", BF[4][:, 0:256], BF[0][:, 0:256], -1.0, ALU.mult)
                    kb.ts("dve", BF[4][:, 256:512], BF[4][:, 0:256],
                          thrc[:, 2:3], ALU.is_gt)
                    kb.tt("dve", BH[(hh % 4) * 2 + sub // 2][:,
                          (sub % 2) * 256 + ((hh // 4) % 2) * 512:
                          (sub % 2) * 256 + ((hh // 4) % 2) * 512 + 256],
                          BF[1][:, 0:256], BF[4][:, 256:512], ALU.subtract) \
                        if False else None
                    kb.tt("dve", BH[hh // 2][:, (hh % 2) * 1024 + sub * 256:
                                              (hh % 2) * 1024 + (sub + 1) * 256],
                          BF[1][:, 0:256], BF[4][:, 256:512], ALU.subtract)
            for tb in range(NCH):
                kb.dma(BH[8][:], XQ3[:, tb * 128:(tb + 1) * 128].rearrange(
                    "(c p) t -> p (c t)", p=128), grp="p3xs3") \
                    if False else None
                for hq in range(4):
                    kb.dma(BH[8 + hq][:, 0:2048], XQ3[
                        hq * 2048:(hq + 1) * 2048,
                        tb * 128:(tb + 1) * 128].rearrange(
                        "(c p) t -> p c t", p=128), grp="p3xs3")
                for hh in range(16):
                    for sub in range(4):
                        hidx = hh * 4 + sub
                        kb.mm(PSD[:, 0:256],
                              BH[8 + hidx // 16][:, (hidx % 16) * 128:
                                                 (hidx % 16 + 1) * 128],
                              BH[hh // 2][:, (hh % 2) * 1024 + sub * 256:
                                           (hh % 2) * 1024 + (sub + 1) * 256],
                              start=(hidx == 0), stop=(hidx == 63))
                kb.dma(BF[6][:, 0:256], ATTNO[tb * 128:(tb + 1) * 128,
                                              c8 * 256:(c8 + 1) * 256],
                       grp="p3at")
                kb.stt(BF[6][:, 256:512], PSD[:, 0:256], iv3[:, tb:tb + 1],
                       BF[6][:, 0:256], ALU.mult, ALU.add)
                kb.dma(out[tb * 128:(tb + 1) * 128, c8 * 256:(c8 + 1) * 256],
                       BF[6][:, 256:512], q="act", grp="p3out")



def kernel(**inputs):
    cfg = {"phases": {"p0", "p1", "p2a", "p2b", "p2c", "p2d", "p3pre",
                      "p3mid", "p3b"}, "n_chunks": 16, "expose": (), "dbg": {}}
    nc, kb = build_program(cfg)
    maps = [make_core_inputs(inputs, c, cfg) for c in range(8)]
    res = run_bass_kernel_spmd(nc, maps, core_ids=list(range(8)))
    out = np.zeros((2, 8192, 2048), np.float32)
    for c in range(8):
        b, qt = c // 4, c % 4
        out[b, qt * 2048:(qt + 1) * 2048] = np.asarray(res.results[c]["outp"])
    return out


# version 06
# speedup vs baseline: 1.3064x; 1.3064x over previous
import numpy as np
import ml_dtypes
import concourse.bass as bass
import concourse.mybir as mybir
from concourse.bass_utils import run_bass_kernel_spmd

F32 = mybir.dt.float32
BF16 = mybir.dt.bfloat16
I32 = mybir.dt.int32
I8 = mybir.dt.int8
AF = mybir.ActivationFunctionType
ALU = mybir.AluOpType
AX = mybir.AxisListType
BFnp = ml_dtypes.bfloat16

ENGS = ("pe", "act", "dve", "pool", "sp")
ENG_ATTR = {"pe": "tensor", "act": "scalar", "dve": "vector", "pool": "gpsimd",
            "sp": "sync"}

class KB:

    def __init__(self, nc):
        self.nc = nc
        self.sem = {}
        self.cnt = {e: 0 for e in ENGS}
        self.cnt["dma"] = 0
        self._es = None
        self.dep = {}
        self.grp_rr = {}
        self.know = {e: {} for e in ENGS}
        self.snap = {}
        self.hw = {}

    def init_sems(self, es):
        self._es = es
        for e in ENGS:
            self.sem[e] = es.enter_context(self.nc.semaphore(f"sem_{e}"))

    def _ensure_grp(self, g):
        if g not in self.sem:
            self.sem[g] = self._es.enter_context(self.nc.semaphore(f"sem_{g}"))
            self.cnt[g] = 0

    def eng(self, e):
        return getattr(self.nc, ENG_ATTR[e])

    def _tname(self, ap):
        if isinstance(ap, tuple):
            ap, rk = ap
            t = ap.tensor if isinstance(ap, bass.AP) else ap
            return t.name + "#" + str(rk)
        t = ap.tensor if isinstance(ap, bass.AP) else ap
        return t.name

    def _unwrap(self, ap):
        return ap[0] if isinstance(ap, tuple) else ap

    def _need(self, waiter, producer):
        peng, pcnt = producer
        if pcnt <= 0:
            return False
        return self.know[waiter].get(peng, 0) < pcnt

    def _wait(self, waiter, producer):
        peng, pcnt = producer
        self.eng(waiter).wait_ge(self.sem[peng], pcnt)
        kw = self.know[waiter]
        kw[peng] = max(kw.get(peng, 0), pcnt)
        sn = self.snap.get(peng, {}).get(pcnt)
        if sn:
            for x, c in sn.items():
                if kw.get(x, 0) < c:
                    kw[x] = c

    def op(self, e, fn, reads=(), writes=(), dma=None, n=1, win=2):
        deps = []
        def is_psum(ap):
            ap = self._unwrap(ap)
            t = ap.tensor if isinstance(ap, bass.AP) else ap
            return "PSum" in type(t).__name__
        def dep_keys(ap):
            nm = self._tname(ap)
            ks = [nm]
            if "#" in nm:
                ks.append(nm.split("#")[0])
            else:
                ks.extend(k for k in self.dep if k.startswith(nm + "#"))
            return ks
        for ap in reads:
            for k in dep_keys(ap):
                d = self.dep.get(k)
                if d:
                    deps.extend(d["w"].items())
        for ap in writes:
            for k in dep_keys(ap):
                d = self.dep.get(k)
                if d:
                    items = list(d["w"].items()) + list(d["r"].items())
                    if e == "pe" and is_psum(ap):
                        items = [(pe_, pc) for (pe_, pc) in items if pe_ != "pe"]
                    deps.extend(items)
        byeng = {}
        for (pe, pc) in deps:
            byeng[pe] = max(byeng.get(pe, 0), pc)
        if dma:
            idx = self.grp_rr.get(dma, 0)
            self.grp_rr[dma] = (idx + 1) % win
            sem_eng = f"{dma}#{idx}"
            self._ensure_grp(sem_eng)
            byeng[sem_eng] = max(byeng.get(sem_eng, 0), self.cnt[sem_eng])
        else:
            sem_eng = e
        for pe, pc in byeng.items():
            if self._need(e, (pe, pc)):
                self._wait(e, (pe, pc))
        inst = fn(self.eng(e))
        step = 16 if dma else n
        self.cnt[sem_eng] += step
        inst.then_inc(self.sem[sem_eng], step)
        stamp = (sem_eng, self.cnt[sem_eng])
        kn = dict(self.know[e])
        kn[e] = self.cnt[e] if not dma else kn.get(e, 0)
        self.snap.setdefault(sem_eng, {})[self.cnt[sem_eng]] = kn
        for ap in writes:
            nm = self._tname(ap)
            if "#" not in nm:
                for k in [k for k in self.dep if k.startswith(nm + "#")]:
                    del self.dep[k]
            d = self.dep.get(nm)
            if d is None or not dma:
                self.dep[nm] = {"w": {stamp[0]: stamp[1]}, "r": {}}
            else:
                d["w"][stamp[0]] = stamp[1]
                d["r"] = {}
        for ap in reads:
            nm = self.dep.setdefault(self._tname(ap), {"w": {}, "r": {}})
            nm["r"][stamp[0]] = stamp[1]
        return inst

    def dma(self, dst, src, q="sp", grp="dma"):
        d2, s2 = self._unwrap(dst), self._unwrap(src)
        return self.op(q, lambda g: g.dma_start(out=d2, in_=s2),
                       reads=[src], writes=[dst], dma=grp)

    def dmat(self, dst, src, q="sp", grp="dma"):
        return self.op(q, lambda g: g.dma_start_transpose(out=dst, in_=src),
                       reads=[src], writes=[dst], dma=grp)

    def mm(self, out, lhsT, rhs, start=True, stop=True):
        o2, l2, r2 = self._unwrap(out), self._unwrap(lhsT), self._unwrap(rhs)
        return self.op("pe", lambda g: g.matmul(o2, lhsT=l2, rhs=r2,
                                                start=start, stop=stop),
                       reads=[lhsT, rhs], writes=[out])

    def trans(self, out, in_, ident):
        o2, i2 = self._unwrap(out), self._unwrap(in_)
        return self.op("pe", lambda g: g.transpose(o2, in_=i2, identity=ident),
                       reads=[in_, ident], writes=[out])

    def tt(self, e, out, a, b, alu):
        o2, a2, b2 = self._unwrap(out), self._unwrap(a), self._unwrap(b)
        return self.op(e, lambda g: g.tensor_tensor(out=o2, in0=a2, in1=b2, op=alu),
                       reads=[a, b], writes=[out])

    def ts(self, e, out, a, s1, op0, s2=None, op1=ALU.bypass, accum=None):
        reads = [a] + [x for x in (s1, s2) if isinstance(x, (bass.AP, tuple))]
        writes = [out] + ([accum] if accum is not None else [])
        o2, a2 = self._unwrap(out), self._unwrap(a)
        s1u = self._unwrap(s1) if isinstance(s1, tuple) else s1
        s2u = self._unwrap(s2) if isinstance(s2, tuple) else s2
        return self.op(e, lambda g: g.tensor_scalar(
            out=o2, in0=a2, scalar1=s1u, scalar2=s2u, op0=op0, op1=op1,
            accum_out=self._unwrap(accum) if accum is not None else None),
            reads=reads, writes=writes)

    def stt(self, out, a, s, b, op0, op1, accum=None):
        reads = [a, b] + ([s] if isinstance(s, (bass.AP, tuple)) else [])
        writes = [out] + ([accum] if accum is not None else [])
        o2, a2, b2 = self._unwrap(out), self._unwrap(a), self._unwrap(b)
        su = self._unwrap(s) if isinstance(s, tuple) else s
        return self.op("dve", lambda g: g.scalar_tensor_tensor(
            out=o2, in0=a2, scalar=su, in1=b2, op0=op0, op1=op1,
            accum_out=self._unwrap(accum) if accum is not None else None),
            reads=reads, writes=writes)

    def constcol(self, val):
        if not hasattr(self, "_ccache"):
            self._ccache = {}
        if val not in self._ccache:
            t = self._es.enter_context(
                self.nc.sbuf_tensor(f"cc_{len(self._ccache)}", [128, 1], F32))
            self.memset("dve", t[:], float(val))
            self._ccache[val] = t
        return self._ccache[val][:, 0:1]

    def act(self, out, in_, func, bias=None, scale=None, accum=None):
        kw = {}
        if bias is not None:
            if isinstance(bias, float):
                bias = self.constcol(bias)
            kw["bias"] = bias
        if scale is not None:
            kw["scale"] = scale
        if accum is not None:
            kw["accum_out"] = accum
        reads = [in_] + [x for x in (bias, scale) if isinstance(x, (bass.AP, tuple))]
        writes = [out] + ([accum] if accum is not None else [])
        if "bias" in kw and isinstance(kw["bias"], tuple):
            kw["bias"] = self._unwrap(kw["bias"])
        if "scale" in kw and isinstance(kw["scale"], tuple):
            kw["scale"] = self._unwrap(kw["scale"])
        if "accum_out" in kw and isinstance(kw["accum_out"], tuple):
            kw["accum_out"] = self._unwrap(kw["accum_out"])
        o2, i2 = self._unwrap(out), self._unwrap(in_)
        return self.op("act", lambda g: g.activation(out=o2, in_=i2, func=func,
                                                     **kw),
                       reads=reads, writes=writes)

    def copy(self, e, out, in_):
        o2, i2 = self._unwrap(out), self._unwrap(in_)
        if e == "act":
            return self.op("act", lambda g: g.copy(out=o2, in_=i2),
                           reads=[in_], writes=[out])
        return self.op(e, lambda g: g.tensor_copy(out=o2, in_=i2),
                       reads=[in_], writes=[out])

    def red(self, e, out, in_, alu, axis=AX.X, absval=None, negate=None):
        o2, i2 = self._unwrap(out), self._unwrap(in_)
        return self.op(e, lambda g: g.tensor_reduce(
            out=o2, in_=i2, axis=axis, op=alu,
            apply_absolute_value=absval, negate=negate),
            reads=[in_], writes=[out])

    def ttr(self, out, a, b, op0, op1, accum, scale=1.0, seed=0.0):
        return self.op("dve", lambda g: g.tensor_tensor_reduce(
            out=out, in0=a, in1=b, scale=scale, scalar=seed, op0=op0, op1=op1,
            accum_out=accum), reads=[a, b], writes=[out, accum])

    def memset(self, e, ap, val):
        a2 = self._unwrap(ap)
        return self.op(e, lambda g: g.memset(a2, val), writes=[ap])

    def barrier_all(self):
        marks = {}
        for e in ENGS:
            eng = self.eng(e)
            i = eng.nop() if e in ("act", "pe", "sp") else eng.engine_nop()
            self.cnt[e] += 1
            i.then_inc(self.sem[e], 1)
            marks[e] = self.cnt[e]
        for e in ENGS:
            for o in ENGS:
                if o != e and self._need(e, (o, marks[o])):
                    self._wait(e, (o, marks[o]))


def softplus_dev(kb, out, in_, tmp1, tmp2):
    kb.act(tmp1, in_, AF.Abs)
    kb.act(tmp1, tmp1, AF.Exp, scale=-1.0)
    kb.act(tmp1, tmp1, AF.Ln, bias=1.0)
    kb.ts("dve", tmp2, in_, 0.0, ALU.max)
    kb.tt("dve", out, tmp2, tmp1, ALU.add)


def log_sigmoid_dev(kb, out, in_, tmp1, tmp2):
    kb.act(tmp1, in_, AF.Abs)
    kb.act(tmp1, tmp1, AF.Exp, scale=-1.0)
    kb.act(tmp1, tmp1, AF.Ln, bias=1.0)
    kb.ts("dve", tmp2, in_, 0.0, ALU.min)
    kb.tt("dve", out, tmp2, tmp1, ALU.subtract)



def build_program(cfg):
    nc = bass.Bass("TRN2", target_bir_lowering=False)
    kb = KB(nc)
    TOK, HID, NH, HD, NCH, CC = 2048, 2048, 16, 128, 16, 128

    dt = nc.dram_tensor
    xin = dt("xin", [TOK, HID], F32, kind="ExternalInput").ap()
    wq = dt("wq", [HID, HID], F32, kind="ExternalInput").ap()
    wk = dt("wk", [HID, HID], F32, kind="ExternalInput").ap()
    wv = dt("wv", [HID, HID], F32, kind="ExternalInput").ap()
    wab = dt("wab", [HID, 32], F32, kind="ExternalInput").ap()
    bab = dt("bab", [1, 32], F32, kind="ExternalInput").ap()
    lamT = dt("lamT", [HD, NH], F32, kind="ExternalInput").ap()
    onw = dt("onw", [HD, 1], F32, kind="ExternalInput").ap()
    wo = dt("wo", [HID, HID], F32, kind="ExternalInput").ap()
    w1T = dt("w1T", [HID, 8192], F32, kind="ExternalInput").ap()
    w2T = dt("w2T", [HID, 8192], F32, kind="ExternalInput").ap()
    w3T = dt("w3T", [8192, HID], F32, kind="ExternalInput").ap()
    g1r = dt("g1r", [1, HID], F32, kind="ExternalInput").ap()
    g2r = dt("g2r", [1, HID], F32, kind="ExternalInput").ap()
    g3r = dt("g3r", [1, 8192], F32, kind="ExternalInput").ap()
    ident = dt("ident", [128, 128], F32, kind="ExternalInput").ap()
    identb = dt("identb", [128, 128], BF16, kind="ExternalInput").ap()
    triu = dt("triu", [128, 128], F32, kind="ExternalInput").ap()
    tril_m = dt("tril_m", [128, 128], F32, kind="ExternalInput").ap()
    onescol = dt("onescol", [1, 128], F32, kind="ExternalInput").ap()
    onesd = dt("onesd", [128, 1], F32, kind="ExternalInput").ap()
    cmask = dt("cmask", [8, 8], F32, kind="ExternalInput").ap()
    cmaskcol = dt("cmaskcol", [8, 1], F32, kind="ExternalInput").ap()
    e127 = dt("e127", [128, 1], F32, kind="ExternalInput").ap()
    onescolb = dt("onescolb", [1, 128], BF16, kind="ExternalInput").ap()
    onesdb = dt("onesdb", [128, 1], BF16, kind="ExternalInput").ap()

    out = dt("outp", [TOK, HID], F32, kind="ExternalOutput").ap()

    ex = cfg.get("expose", ())
    kd = lambda nm: "ExternalOutput" if nm in ex else "Internal"
    XN_TM = dt("XN_TM", [TOK, HID], F32, kind=kd("XN_TM")).ap()
    XN_FM = dt("XN_FM", [HID, TOK], BF16, kind=kd("XN_FM")).ap()
    QFM = dt("QFM", [NCH, NH, HD, CC], F32, kind=kd("QFM")).ap()
    KFM = dt("KFM", [NCH, NH, HD, CC], BF16, kind=kd("KFM")).ap()
    KTM = dt("KTM", [NCH, CC, NH * HD], BF16, kind=kd("KTM")).ap()
    VTM = dt("VTM", [NCH, CC, NH * HD], BF16, kind=kd("VTM")).ap()
    VFM = dt("VFM", [NCH, NH, HD, CC], BF16, kind=kd("VFM")).ap()
    GTM = dt("GTM", [NCH, CC, NH], F32, kind=kd("GTM")).ap()
    BETA = dt("BETA", [NCH, CC, NH], F32, kind=kd("BETA")).ap()
    THR = dt("THR", [NCH, CC, NH], F32, kind=kd("THR")).ap()
    RQ = dt("RQ", [NCH, CC, NH], F32, kind=kd("RQ")).ap()
    ATTNO = dt("ATTNO", [TOK, HID], F32, kind=kd("ATTNO")).ap()
    ONORM = dt("ONORM", [NCH, HD, TOK // NCH * 16], BF16, kind=kd("ONORM")).ap()
    CIN = dt("CIN", [128, 4112], F32).ap()
    COUT = dt("COUT", [1024, 4112], F32).ap()
    XQ1 = dt("XQ1", [HID, TOK], BF16, kind=kd("XQ1")).ap()
    XQ2 = dt("XQ2", [HID, TOK], BF16, kind=kd("XQ2")).ap()
    XQ3 = dt("XQ3", [8192, TOK], BF16, kind=kd("XQ3")).ap()
    HG3 = dt("HG3", [TOK, 8192], BF16, kind=kd("HG3")).ap()
    dbg_out = {}
    for name, shape in cfg.get("dbg", {}).items():
        dbg_out[name] = dt("dbg_" + name, shape, F32, kind="ExternalOutput").ap()

    from contextlib import ExitStack
    es = ExitStack()
    kb._es = es
    sbuf = lambda n, s, d: es.enter_context(nc.sbuf_tensor(n, s, d))
    psum = lambda n, s, d: es.enter_context(nc.psum_tensor(n, s, d))
    kb.init_sems(es)

    c_ident = sbuf("c_ident", [128, 128], F32)
    c_identb = sbuf("c_identb", [128, 128], BF16)
    c_triu = sbuf("c_triu", [128, 128], F32)
    c_trilm = sbuf("c_trilm", [128, 128], F32)
    c_ones1 = sbuf("c_ones1", [1, 128], F32)
    c_ones1b = sbuf("c_ones1b", [1, 128], BF16)
    c_onesdb = sbuf("c_onesdb", [128, 1], BF16)
    c_e127 = sbuf("c_e127", [128, 1], F32)
    lamE2 = sbuf("lamE2", [128, 16], F32)
    lamdb = [sbuf(f"lamdb{h}", [128, 128], BF16) for h in range(16)]
    c_onesd = sbuf("c_onesd", [128, 1], F32)
    c_bab = sbuf("c_bab", [1, 32], F32)
    lam_sb = sbuf("lam_sb", [128, NH], F32)
    onw_sb = sbuf("onw_sb", [128, 1], F32)

    BF = [sbuf(f"BF{i}", [128, 2048], F32) for i in range(8)]
    BH = [sbuf(f"BH{i}", [128, 2048], BF16) for i in range(16)]

    PSA = psum("PSA", [128, 1024], F32)
    PSB = psum("PSB", [128, 1024], F32)
    PSC = psum("PSC", [128, 1024], F32)
    PSD = psum("PSD", [128, 1024], F32)
    psP = [PSA[:, 0:512], PSA[:, 512:1024], PSB[:, 0:512], PSB[:, 512:1024]]
    psZ = PSC[:, 0:512]
    psT = [PSC[:, 512:640], PSC[:, 640:768]]

    with es:
        kb.dma(c_ident[:], ident)
        kb.dma(c_identb[:], identb)
        kb.dma(c_triu[:], triu)
        kb.dma(c_trilm[:], tril_m)
        kb.dma(c_ones1[:], onescol)
        kb.dma(c_onesd[:], onesd)
        kb.dma(c_e127[:], e127)
        kb.copy("dve", c_ones1b[:], c_ones1[:])
        kb.copy("dve", c_onesdb[:], c_onesd[:])
        kb.dma(c_bab[:], bab)
        kb.dma(onw_sb[:], onw)
        t16a = sbuf("t16a", [128, NH], F32)
        t16b = sbuf("t16b", [128, NH], F32)
        kb.dma(lam_sb[:], lamT)
        softplus_dev(kb, lam_sb[:], lam_sb[:], t16a[:], t16b[:])
        kb.ts("dve", lam_sb[:], lam_sb[:], 0.25, ALU.add)
        kb.ts("dve", lamE2[:], lam_sb[:], 1e-8, ALU.add)
        for h in range(16):
            kb.ts("dve", lamdb[h][:], c_ident[:], lam_sb[:, h:h + 1], ALU.mult)

        if "p0" in cfg["phases"]:
            sx = [BF[0], BF[1]]
            xnt = [BF[2], BF[3]]
            stats = sbuf("p0_stats", [128, 24], F32)
            mv = sbuf("p0_mv", [128, 2], F32)
            rstd = sbuf("p0_rstd", [128, 1], F32)
            xfmb = sbuf("p0_xfmb", [128, 128], BF16)
            pT = psT
            for tt in range(16):
                b = tt % 2
                kb.dma(sx[b][:], xin[tt * 128:(tt + 1) * 128, :], grp=f"dx{b}")
                for j in range(4):
                    kb.op("dve", lambda g, j=j, b=b: g.bn_stats(
                        out=stats[:, j * 6:(j + 1) * 6], in_=sx[b][:, j * 512:(j + 1) * 512]),
                        reads=[sx[b]], writes=[stats])
                kb.op("dve", lambda g: g.bn_aggr(out=mv[:], in_=stats[:]),
                      reads=[stats], writes=[mv])
                kb.act(rstd[:], mv[:, 1:2], AF.Ln, bias=1e-5)
                kb.act(rstd[:], rstd[:], AF.Exp, scale=-0.5)
                kb.ts("dve", xnt[b][:], sx[b][:], mv[:, 0:1], ALU.subtract,
                      rstd[:, 0:1], ALU.mult)
                kb.dma(XN_TM[tt * 128:(tt + 1) * 128, :], xnt[b][:], q="act", grp="sx")
                for ct in range(16):
                    pb = ct % 2
                    kb.trans(pT[pb], xnt[b][:, ct * 128:(ct + 1) * 128], c_ident[:])
                    kb.copy("act", xfmb[:], pT[pb])
                    kb.dma(XN_FM[ct * 128:(ct + 1) * 128, tt * 128:(tt + 1) * 128],
                           xfmb[:], q="act", grp="sfm")

        if "p1" in cfg["phases"]:
            xnfm_sv = [BH[1], BH[2], BH[3], BH[4]]
            class _XN:
                def __getitem__(self, sl):
                    raise RuntimeError("use xnfm_at")
            def xnfm_at(ct, lo, hi):
                return xnfm_sv[ct // 4][:, (ct % 4) * 512 + lo:(ct % 4) * 512 + hi]
            wfp = [sbuf(f"p1_wfp{i}", [128, 512], F32) for i in range(2)]
            wbf = [sbuf(f"p1_wbf{i}", [128, 512], BF16) for i in range(2)]
            qtm = [BF[0], BF[1], BF[2], BF[3]]
            sq2 = BF[4]
            n2 = sbuf("p1_n2", [128, NH], F32)
            scal = sbuf("p1_scal", [128, NH], F32)
            ktb = BH[0]
            gb = sbuf("p1_gb", [128, 32], F32)
            gsm = sbuf("p1_gsm", [128, NH], F32)
            t16c = sbuf("p1_t16c", [128, NH], F32)
            fmb = sbuf("p1_fmb", [128, 128], BF16)
            fmf = sbuf("p1_fmf", [128, 128], F32)

            def proj_assemble(Wap, sblk):
                t0 = sblk * 512
                for nt in range(4):
                    for ct in range(16):
                        b = ct % 2
                        kb.dma(wfp[b][:], Wap[ct * 128:(ct + 1) * 128,
                                              nt * 512:(nt + 1) * 512], grp=f"dw{b}")
                        kb.copy("dve", wbf[b][:], wfp[b][:])
                        for tb in range(4):
                            kb.mm(psP[tb], xnfm_at(ct, tb * 128, (tb + 1) * 128),
                                  wbf[b][:], start=(ct == 0), stop=(ct == 15))
                    for tb in range(4):
                        kb.act(qtm[tb][:, nt * 512:(nt + 1) * 512], psP[tb], AF.Silu)

            for sblk in range(4):
                t0 = sblk * 512
                for ct in range(16):
                    kb.dma(xnfm_at(ct, 0, 512),
                           XN_FM[ct * 128:(ct + 1) * 128, t0:t0 + 512], grp="dxf")
                proj_assemble(wq, sblk)
                for tb in range(4):
                    ch = sblk * 4 + tb
                    q3 = qtm[tb][:].rearrange("p (h d) -> p h d", d=HD)
                    kb.act(sq2[:], qtm[tb][:], AF.Square)
                    kb.red("dve", n2[:], sq2[:].rearrange("p (h d) -> p h d", d=HD),
                           ALU.add)
                    kb.ts("dve", scal[:], n2[:], 1e-24, ALU.max, 1e-10, ALU.mult)
                    kb.dma(THR[ch], scal[:], q="act", grp="sth")
                    kb.act(t16c[:], n2[:], AF.Ln)
                    kb.act(t16c[:], t16c[:], AF.Exp, scale=0.5)
                    kb.ts("dve", t16c[:], t16c[:], 1e-12, ALU.max)
                    kb.op("dve", lambda g: g.reciprocal(out=t16c[:], in_=t16c[:]),
                          reads=[t16c], writes=[t16c])
                    kb.dma(RQ[ch], t16c[:], q="act", grp="sth")
                    for h in range(NH):
                        pb = h % 2
                        kb.trans(psT[pb], qtm[tb][:, h * HD:(h + 1) * HD], c_ident[:])
                        kb.copy("dve", fmf[:], psT[pb])
                        kb.dma(QFM[ch, h], fmf[:], q="act", grp="sq")
                proj_assemble(wk, sblk)
                for tb in range(4):
                    ch = sblk * 4 + tb
                    kb.act(sq2[:], qtm[tb][:], AF.Square)
                    kb.red("dve", n2[:], sq2[:].rearrange("p (h d) -> p h d", d=HD),
                           ALU.add)
                    kb.act(scal[:], n2[:], AF.Ln)
                    kb.act(scal[:], scal[:], AF.Exp, scale=0.5)
                    kb.ts("dve", scal[:], scal[:], 1e-12, ALU.max)
                    kb.op("dve", lambda g: g.reciprocal(out=scal[:], in_=scal[:]),
                          reads=[scal], writes=[scal])
                    for h in range(NH):
                        kb.ts("dve", sq2[:, h * HD:(h + 1) * HD],
                              qtm[tb][:, h * HD:(h + 1) * HD], scal[:, h:h + 1],
                              ALU.mult)
                    kb.copy("act", ktb[:], sq2[:])
                    kb.dma(KTM[ch], ktb[:], q="act", grp="sk")
                    for h in range(NH):
                        pb = h % 2
                        kb.trans(psT[pb], sq2[:, h * HD:(h + 1) * HD], c_ident[:])
                        kb.copy("dve", fmb[:], psT[pb])
                        kb.dma(KFM[ch, h], fmb[:], q="act", grp="sk")
                proj_assemble(wv, sblk)
                for tb in range(4):
                    ch = sblk * 4 + tb
                    kb.copy("dve", ktb[:], qtm[tb][:])
                    kb.dma(VTM[ch], ktb[:], q="act", grp="sv")
                    for h in range(NH):
                        pb = h % 2
                        kb.trans(psT[pb], qtm[tb][:, h * HD:(h + 1) * HD],
                                 c_ident[:])
                        kb.copy("dve", fmb[:], psT[pb])
                        kb.dma(VFM[ch, h], fmb[:], q="act", grp="sv")
                for tb in range(4):
                    ch = sblk * 4 + tb
                    for ct in range(16):
                        b = ct % 2
                        kb.dma(wfp[b][:, :32], wab[ct * 128:(ct + 1) * 128, :],
                               grp=f"dw{b}")
                        kb.copy("dve", wbf[b][:, :32], wfp[b][:, :32])
                        kb.mm(psZ[:, 0:32], xnfm_at(ct, tb * 128, (tb + 1) * 128),
                              wbf[b][:, :32], start=(ct == 0), stop=False)
                    kb.copy("dve", wbf[0][:1, :32], c_bab[:])
                    kb.mm(psZ[:, 0:32], c_ones1b[:], wbf[0][:1, :32],
                          start=False, stop=True)
                    kb.copy("dve", gb[:], psZ[:, 0:32])
                    kb.act(t16c[:], gb[:, 16:32], AF.Sigmoid)
                    kb.dma(BETA[ch], t16c[:], q="act", grp="sg")
                    log_sigmoid_dev(kb, gsm[:], gb[:, :16], t16c[:], n2[:])
                    kb.mm(psZ[:, 64:80], c_triu[:], gsm[:])
                    kb.copy("dve", t16c[:], psZ[:, 64:80])
                    kb.dma(GTM[ch], t16c[:], q="act", grp="sg")

        T = dict(sbuf=sbuf, psum=psum, BF=BF, BH=BH, KTM=KTM, VTM=VTM, KFM=KFM, VFM=VFM,
                 QFM=QFM, GTM=GTM, BETA=BETA, THR=THR, RQ=RQ, CIN=CIN,
                 COUT=COUT, ONORM=ONORM, ATTNO=ATTNO, XN_TM=XN_TM, xin=xin,
                 wo=wo, c_ident=c_ident, c_trilm=c_trilm, c_ones1=c_ones1,
                 c_ones1b=c_ones1b, c_onesd=c_onesd, c_onesdb=c_onesdb,
                 c_e127=c_e127, c_identb=c_identb, lam_sb=lam_sb, lamdb=lamdb, lamE2=lamE2,
                 onw_sb=onw_sb, PSA=PSA, PSB=PSB, PSC=PSC, PSD=PSD,
                 cmask=cmask, cmaskcol=cmaskcol, out=out,
                 w1T=w1T, w2T=w2T, w3T=w3T, g1r=g1r, g2r=g2r, g3r=g3r,
                 XQ1=XQ1, XQ2=XQ2, XQ3=XQ3, HG3=HG3)
        cfg["dumps"] = dbg_out
        if ("p2a" in cfg["phases"] or "p2c" in cfg["phases"]
                or "p2b" in cfg["phases"]):
            emit_p2(kb, cfg, T)
        if "p2d" in cfg["phases"]:
            emit_p2d(kb, cfg, T)
        if {"p3pre", "p3mid", "p3b"} & cfg["phases"]:
            emit_p3(kb, cfg, T)

        for e in ("sp",):
            for gname, cnt in list(kb.cnt.items()):
                if gname in ENGS or cnt == 0:
                    continue
                kb.eng("sp").wait_ge(kb.sem[gname], cnt)

    return nc, kb


def make_core_inputs(inp, core, cfg):
    b, qt = core // 4, core % 4
    TOK = 2048
    x = np.ascontiguousarray(inp["L0_attn_x"][b, qt * TOK:(qt + 1) * TOK, :], )
    ii = np.arange(128)
    m = {
        "xin": x.astype(np.float32),
        "wq": inp["L0_attn_Wq"], "wk": inp["L0_attn_Wk"], "wv": inp["L0_attn_Wv"],
        "wab": np.ascontiguousarray(np.concatenate(
            [inp["L0_attn_Wa"], inp["L0_attn_Wb"]], 1)),
        "bab": np.concatenate([inp["L0_attn_ba"], inp["L0_attn_bb"]])[None, :],
        "lamT": np.ascontiguousarray(inp["L0_attn_lambda_params"].reshape(16, 128).T),
        "onw": inp["L0_attn_o_norm_weight"][:, None],
        "wo": inp["L0_attn_Wo"],
        "w1T": np.ascontiguousarray(inp["L0_ffn_w1"].T),
        "w2T": np.ascontiguousarray(inp["L0_ffn_w2"].T),
        "w3T": np.ascontiguousarray(inp["L0_ffn_w3"].T),
        "g1r": inp["L0_ffn_g1"][None, :], "g2r": inp["L0_ffn_g2"][None, :],
        "g3r": inp["L0_ffn_g3"][None, :],
        "ident": np.eye(128, dtype=np.float32),
        "identb": np.eye(128).astype(BFnp),
        "triu": (ii[:, None] <= ii[None, :]).astype(np.float32),
        "tril_m": (ii[:, None] <= ii[None, :]).astype(np.float32),
        "onescol": np.ones((1, 128), np.float32),
        "onesd": np.ones((128, 1), np.float32),
        "cmask": np.zeros((8, 8), np.float32),
        "cmaskcol": np.zeros((8, 1), np.float32),
        "e127": np.eye(128, dtype=np.float32)[:, 127:128],
        "onescolb": np.ones((1, 128)).astype(BFnp),
        "onesdb": np.ones((128, 1)).astype(BFnp),
    }
    b_, qt_ = core // 4, core % 4
    preds = [r for r in range(b_ * 4, b_ * 4 + qt_)]
    cmc = np.zeros((8, 1), np.float32)
    cm2 = np.zeros((8, 8), np.float32)
    for r in preds:
        cmc[r, 0] = 1.0
        for rp in range(r + 1, b_ * 4 + qt_):
            cm2[rp, r] = 1.0
    m["cmask"] = cm2
    m["cmaskcol"] = cmc
    return {k: np.ascontiguousarray(v.astype(np.float32) if v.dtype == np.float64
                                    else v) for k, v in m.items()}


def emit_p2(kb, cfg, T):
    nc = kb.nc
    NH, HD, CC = 16, 128, 128
    NCH = cfg.get("n_chunks", 16)
    sbuf, psum = T["sbuf"], T["psum"]
    (KTM, VTM, KFM, VFM, QFM, GTM, BETA, THR, RQ, CIN, COUT, ONORM) = (
        T["KTM"], T["VTM"], T["KFM"], T["VFM"], T["QFM"], T["GTM"], T["BETA"],
        T["THR"], T["RQ"], T["CIN"], T["COUT"], T["ONORM"])
    c_ident, c_trilm, c_ones1, c_ones1b, c_onesd, c_onesdb, c_e127 = (
        T["c_ident"], T["c_trilm"], T["c_ones1"], T["c_ones1b"], T["c_onesd"],
        T["c_onesdb"], T["c_e127"])
    c_identb = T["c_identb"]
    lam_sb, lamE2, lamdb = T["lam_sb"], T["lamE2"], T["lamdb"]
    onw_sb = T["onw_sb"]
    PSA, PSB, PSC, PSD = T["PSA"], T["PSB"], T["PSC"], T["PSD"]
    S_u = (PSA, PSC)
    Q_u = (PSB, PSD)
    BF, BH = T["BF"], T["BH"]

    A0st = [sbuf(f"A0st{h}", [128, 128], F32) for h in range(NH)]
    HKst = [sbuf(f"HKst{h}", [128, 128], F32) for h in range(NH)]
    ktm_sb = BH[1]
    vtm_sb = BH[2]
    kfm_sb = BH[3]
    vfm_sb = BH[4]
    qfm_sb = BF[7]
    Gt = sbuf("Gt", [128, NH], F32)
    Bt = sbuf("Bt", [128, NH], F32)
    Tt = sbuf("Tt", [128, NH], F32)
    Rqt = sbuf("Rqt", [128, NH], F32)
    wcol = sbuf("wcol", [128, NH], F32)
    eGlb = sbuf("eGlb", [128, NH], F32)
    glrow = sbuf("glrow", [1, NH], F32)
    dgacc = sbuf("dgacc", [1, NH], F32)
    kwb = sbuf("kwb", [128, 128], BF16)
    vwb = sbuf("vwb", [128, 128], BF16)
    t16a = sbuf("p2_t16a", [128, NH], F32)
    t16b = sbuf("p2_t16b", [128, NH], F32)
    t16c = sbuf("p2_t16c", [128, NH], F32)
    t16d = sbuf("p2_t16d", [128, NH], F32)

    def glast_wcol(c):
        kb.mm(PSC[:1, 0:16], c_e127[:], Gt[:])
        kb.copy("dve", glrow[:], PSC[:1, 0:16])
        kb.mm(PSC[:, 16:32], c_ones1[:], glrow[:])
        kb.tt("dve", t16a[:], PSC[:, 16:32], Gt[:], ALU.subtract)
        kb.ts("dve", t16a[:], t16a[:], 0.0, ALU.min)
        kb.act(t16b[:], t16a[:], AF.Exp)
        kb.tt("dve", wcol[:], t16b[:], Bt[:], ALU.mult)
        kb.act(eGlb[:], PSC[:, 16:32], AF.Exp)

    def state_update(c):
        for h in range(NH):
            u, hh = h // 8, h % 8
            kb.ts("dve", kwb[:], ktm_sb[:, h * HD:(h + 1) * HD],
                  wcol[:, h:h + 1], ALU.mult)
            kb.ts("dve", vwb[:], vtm_sb[:, h * HD:(h + 1) * HD],
                  wcol[:, h:h + 1], ALU.mult)
            kb.mm(S_u[u][:, hh * 128:(hh + 1) * 128], kwb[:],
                  ktm_sb[:, h * HD:(h + 1) * HD])
            kb.mm(Q_u[u][:, hh * 128:(hh + 1) * 128], vwb[:],
                  ktm_sb[:, h * HD:(h + 1) * HD])
            kb.stt(A0st[h][:], A0st[h][:], eGlb[:, h:h + 1],
                   S_u[u][:, hh * 128:(hh + 1) * 128], ALU.mult, ALU.add)
            kb.stt(HKst[h][:], HKst[h][:], eGlb[:, h:h + 1],
                   Q_u[u][:, hh * 128:(hh + 1) * 128], ALU.mult, ALU.add)

    if "p2a" in cfg["phases"]:
        for h in range(NH):
            kb.memset("dve", A0st[h][:], 0.0)
            kb.memset("dve", HKst[h][:], 0.0)
        kb.memset("dve", dgacc[:], 0.0)
        for c in range(NCH):
            kb.dma(ktm_sb[:], KTM[c], grp="p2k")
            kb.dma(vtm_sb[:], VTM[c], grp="p2v")
            kb.dma(Gt[:], GTM[c], grp="p2g")
            kb.dma(Bt[:], BETA[c], grp="p2g")
            glast_wcol(c)
            kb.tt("dve", dgacc[:], dgacc[:], glrow[:], ALU.add)
            state_update(c)
        for h in range(NH):
            kb.dma(CIN[:, h * 128:(h + 1) * 128], A0st[h][:], grp="cinw")
            kb.dma(CIN[:, 2048 + h * 128:2048 + (h + 1) * 128], HKst[h][:],
                   grp="cinw")
        kb.memset("dve", t16d[:], 0.0)
        kb.copy("dve", t16d[0:1, :], dgacc[:])
        kb.dma(CIN[:, 4096:4112], t16d[:], grp="cinw")
        kb.op("pool", lambda g: g.collective_compute(
            "AllGather", ALU.bypass, replica_groups=[list(range(8))],
            ins=[CIN], outs=[COUT]), reads=[CIN], writes=[COUT])

    if "p2b" in cfg["phases"]:
        cmsk = sbuf("cmsk", [8, 8], F32)
        cmcol = sbuf("cmcol", [8, 1], F32)
        dgall = sbuf("dgall", [8, NH], F32)
        wt8 = sbuf("wt8", [8, NH], F32)
        wflat = sbuf("wflat", [1, 128], F32)
        wBs = sbuf("wBs", [128, 128], F32)
        gsbA = BF[0]
        gsbB = BF[1]
        kb.dma(cmsk[:], T["cmask"])
        kb.dma(cmcol[:], T["cmaskcol"])
        for r in range(8):
            kb.dma(dgall[r:r + 1, :], COUT[r * 128:r * 128 + 1, 4096:4112],
                   grp="p2b")
        kb.mm(PSC[:8, 0:16], cmsk[:], dgall[:])
        kb.act(wt8[:], PSC[:8, 0:16], AF.Exp)
        kb.ts("dve", wt8[:], wt8[:], cmcol[:, 0:1], ALU.mult)
        kb.dma(wflat[:], wt8[:], grp="p2b")
        kb.mm(PSC[:, 32:160], c_ones1[:], wflat[:])
        kb.copy("dve", wBs[:], PSC[:, 32:160])
        for h in range(NH):
            kb.memset("dve", A0st[h][:], 0.0)
            kb.memset("dve", HKst[h][:], 0.0)
        for r in range(8):
            kb.dma(gsbA[:], COUT[r * 128:(r + 1) * 128, 0:2048], grp="p2ga")
            kb.dma(gsbB[:], COUT[r * 128:(r + 1) * 128, 2048:4096], grp="p2gb")
            for h in range(NH):
                kb.stt(A0st[h][:], gsbA[:, h * 128:(h + 1) * 128],
                       wBs[:, r * 16 + h:r * 16 + h + 1], A0st[h][:],
                       ALU.mult, ALU.add)
                kb.stt(HKst[h][:], gsbB[:, h * 128:(h + 1) * 128],
                       wBs[:, r * 16 + h:r * 16 + h + 1], HKst[h][:],
                       ALU.mult, ALU.add)

    if "p2c" in cfg["phases"]:
        Xcg = BF[0]
        Rcg = BF[1]
        Pcg = BF[2]
        Pb = BH[5]
        Psc = BH[6]
        swb = BH[7]
        pqb = BH[8]
        rrb = BH[9]
        wm = BH[10]
        k2tm = BH[11]
        A0p = BH[12]
        hkvb = BH[13]
        expGb = BF[3]
        rqb = BF[4]
        egrqb = BF[5]
        ab_sb = BH[14]
        dn_sb = BF[6]
        onorm_sb = BH[15]
        gtr2 = sbuf("gtr2", [128, 32], F32)
        tr32 = sbuf("tr32", [32, 128], F32)
        rows32 = sbuf("rows32", [1, 4096], F32)
        T["rows32"] = rows32
        dgtile = sbuf("dgtile", [128, 128], F32)
        bmtile = sbuf("bmtile", [128, 128], F32)
        wexp = sbuf("wexp", [128, 128], F32)
        datile = sbuf("datile", [128, 128], F32)
        dsc = sbuf("dsc", [128, NH], F32)
        dnsc = sbuf("dnsc", [128, NH], F32)
        pqsc = sbuf("pqsc", [128, NH], F32)
        donec = sbuf("donec", [128, NH], F32)
        ndone = sbuf("ndone", [128, NH], F32)
        alsc = sbuf("alsc", [128, NH], F32)
        besc = sbuf("besc", [128, NH], F32)
        s16a = sbuf("s16a", [128, NH], F32)
        s16b = sbuf("s16b", [128, NH], F32)
        abT = sbuf("abT", [16, 128], BF16)
        abrow2 = sbuf("abrow2", [1, 2048], BF16)

        def U(t, u):
            return (t[:, u * 1024:(u + 1) * 1024], u)

        def Uh(t, u, hh, w=128):
            return (t[:, u * 1024 + hh * w:u * 1024 + (hh + 1) * w], u)

        def Us(t, u):
            return (t[:, u * 8:(u + 1) * 8], u)

        def Ush(t, u, hh):
            return (t[:, u * 8 + hh:u * 8 + hh + 1], u)

        def mv_unit(u, Pb_src, Psc_src):
            for hh in range(8):
                h = u * 8 + hh
                kb.mm((S_u[u][:, hh * 128:(hh + 1) * 128], u),
                      Uh(kfm_sb, u, hh), Uh(Pb_src, u, hh))
            kb.tt("dve", U(swb, u), U(wm, u), (S_u[u][:], u), ALU.mult)
            for hh in range(8):
                h = u * 8 + hh
                kb.mm((Q_u[u][:, hh * 128:(hh + 1) * 128], u),
                      Uh(ktm_sb, u, hh), Uh(swb, u, hh),
                      start=True, stop=False)
                kb.mm((Q_u[u][:, hh * 128:(hh + 1) * 128], u),
                      Uh(A0p, u, hh), Uh(Psc_src, u, hh),
                      start=False, stop=False)
                kb.mm((Q_u[u][:, hh * 128:(hh + 1) * 128], u),
                      lamdb[h][:], Uh(Pb_src, u, hh),
                      start=False, stop=True)

        def colreduce_u(u, src, lo):
            for hh in range(8):
                kb.mm((S_u[u][:, lo + hh:lo + hh + 1], u),
                      Uh(src, u, hh), c_onesdb[:])


        NIT = cfg.get("n_iter", 30)
        dumps = cfg.get("dumps", {})
        def dump(name, ap):
            if name in dumps:
                kb.dma(dumps[name], ap, q="pool", grp="dmp")
        if "p2b" not in cfg["phases"]:
            for h in range(NH):
                kb.memset("dve", A0st[h][:], 0.0)
                kb.memset("dve", HKst[h][:], 0.0)
        for c in range(NCH):
            kb.dma(ktm_sb[:], KTM[c], grp="p2k")
            kb.dma(vtm_sb[:], VTM[c], grp="p2v")
            kfm_ap = KFM[c].rearrange("h d t -> d h t")
            vfm_ap = VFM[c].rearrange("h d t -> d h t")
            qfm_ap = QFM[c].rearrange("h d t -> d h t")
            kb.dma(kfm_sb[:], kfm_ap, grp="p2k")
            kb.dma(vfm_sb[:], vfm_ap, grp="p2v")
            kb.dma(qfm_sb[:], qfm_ap, grp="p2q")
            kb.dma(Gt[:], GTM[c], grp="p2g")
            kb.dma(Bt[:], BETA[c], grp="p2g")
            kb.dma(Tt[:], THR[c], grp="p2g")
            kb.dma(Rqt[:], RQ[c], grp="p2g")

            kb.copy("dve", gtr2[:, 0:16], Gt[:])
            kb.copy("dve", gtr2[:, 16:32], Rqt[:])
            kb.trans(PSC[:32, 128:256], gtr2[:], c_ident[:])
            kb.copy("dve", tr32[:], PSC[:32, 128:256])
            kb.dma(rows32[:], tr32[:], grp="fl0")
            for h in range(NH):
                u, hh = h // 8, h % 8
                kb.mm((S_u[u][:, 384:512], u), c_ones1[:],
                      rows32[:1, h * 128:(h + 1) * 128])
                kb.ts("dve", dgtile[:], (S_u[u][:, 384:512], u), Gt[:, h:h + 1],
                      ALU.subtract, 0.0, ALU.min)
                kb.act(wexp[:], dgtile[:], AF.Exp)
                kb.ts("dve", bmtile[:], c_trilm[:], Bt[:, h:h + 1], ALU.mult)
                kb.tt("dve", Uh(wm, u, hh), wexp[:], bmtile[:], ALU.mult)
            for u in range(2):
                for qq in range(2):
                    kb.mm((S_u[u][:, qq * 512:(qq + 1) * 512], u), c_ones1[:],
                          rows32[:1, 2048 + u * 1024 + qq * 512:
                                 2048 + u * 1024 + (qq + 1) * 512])
                kb.copy("dve", U(rqb, u), (S_u[u][:], u))
            kb.act(rows32[:1, 0:2048], rows32[:1, 0:2048], AF.Exp)
            for u in range(2):
                for qq in range(2):
                    kb.mm((S_u[u][:, qq * 512:(qq + 1) * 512], u), c_ones1[:],
                          rows32[:1, u * 1024 + qq * 512:
                                 u * 1024 + (qq + 1) * 512])
                kb.copy("dve", U(expGb, u), (S_u[u][:], u))
                kb.tt("pool", U(egrqb, u), U(expGb, u), U(rqb, u), ALU.mult)
            kb.act(k2tm[:], ktm_sb[:], AF.Square)
            for h in range(NH):
                u, hh = h // 8, h % 8
                kb.copy("act", Uh(A0p, u, hh), A0st[h][:])
                kb.copy("act", Uh(hkvb, u, hh), HKst[h][:])
            for h in range(NH):
                u, hh = h // 8, h % 8
                kb.mm((Q_u[u][:, hh * 128:(hh + 1) * 128], u),
                      Uh(k2tm, u, hh), Uh(wm, u, hh), start=True, stop=False)
                kb.tt("dve", datile[:], A0st[h][:], c_ident[:], ALU.mult)
                kb.mm((S_u[u][:1, 0:128], u), c_onesd[:], datile[:])
                kb.copy("dve", (rows32[:1, 2048 + h * 128:2048 + (h + 1) * 128],
                                u), (S_u[u][:1, 0:128], u))
                kb.mm((Q_u[u][:, hh * 128:(hh + 1) * 128], u),
                      (rows32[:1, 2048 + h * 128:2048 + (h + 1) * 128], u),
                      (rows32[:1, h * 128:(h + 1) * 128], "ro"),
                      start=False, stop=True)
            for h in range(NH):
                u, hh = h // 8, h % 8
                kb.act(Uh(dn_sb, u, hh), (Q_u[u][:, hh * 128:(hh + 1) * 128], u),
                       AF.Ln, bias=lamE2[:, h:h + 1])
            for u in range(2):
                kb.act(U(dn_sb, u), U(dn_sb, u), AF.Exp, scale=-1.0)
                kb.tt("dve", U(Xcg, u), U(qfm_sb, u), U(dn_sb, u), ALU.mult)
                kb.copy("act", U(Pb, u), U(Xcg, u))
                kb.tt("pool", U(Psc, u), U(Xcg, u), U(expGb, u), ALU.mult)
            for u in range(2):
                mv_unit(u, Pb, Psc)
            for u in range(2):
                kb.tt("dve", U(Rcg, u), U(qfm_sb, u), (Q_u[u][:], u),
                      ALU.subtract)
                kb.copy("dve", U(Pcg, u), U(Rcg, u))
                kb.act(U(rrb, u), U(Rcg, u), AF.Square)
                colreduce_u(u, rrb, 0)
                kb.copy("dve", Us(dsc, u), (S_u[u][:, 0:8], u))
                kb.memset("dve", Us(donec, u), 0.0)
            if c == 0:
                dump("X0", Xcg[:])

            def route(src16):
                kb.trans((S_u[0][0:16, 128:256], 0), src16, c_ident[:])
                kb.copy("dve", abT[:], (S_u[0][0:16, 128:256], 0))
                kb.dma(abrow2[:], abT[:], grp="fl0")
                for u in range(2):
                    kb.mm((S_u[u][:, 0:512], u), c_ones1b[:],
                          abrow2[:1, u * 1024:u * 1024 + 512])
                    kb.mm((S_u[u][:, 512:1024], u), c_ones1b[:],
                          abrow2[:1, u * 1024 + 512:(u + 1) * 1024])
                    kb.copy("act", U(ab_sb, u), (S_u[u][:], u))

            def seg1(u):
                kb.copy("act", U(Pb, u), U(Pcg, u))
                kb.tt("pool", U(Psc, u), U(Pcg, u), U(expGb, u), ALU.mult)
                mv_unit(u, Pb, Psc)
                kb.tt("dve", U(pqb, u), U(Pcg, u), (Q_u[u][:], u), ALU.mult)
                colreduce_u(u, pqb, 0)
                kb.copy("dve", Us(pqsc, u), (S_u[u][:, 0:8], u))

            def smalla():
                kb.tt("dve", s16a[:], dsc[:], Tt[:], ALU.is_lt)
                kb.tt("dve", donec[:], donec[:], s16a[:], ALU.max)
                kb.act(s16a[:], pqsc[:], AF.Abs)
                kb.tt("dve", s16a[:], s16a[:], Tt[:], ALU.is_lt)
                kb.tt("dve", donec[:], donec[:], s16a[:], ALU.max)
                kb.ts("dve", ndone[:], donec[:], -1.0, ALU.mult, 1.0, ALU.add)
                kb.tt("dve", s16a[:], pqsc[:], ndone[:], ALU.mult)
                kb.tt("dve", s16a[:], s16a[:], donec[:], ALU.add)
                kb.op("dve", lambda g: g.reciprocal(out=s16b[:], in_=s16a[:]),
                      reads=[s16a], writes=[s16b])
                kb.tt("dve", alsc[:], dsc[:], s16b[:], ALU.mult)
                kb.tt("dve", alsc[:], alsc[:], ndone[:], ALU.mult)

            def upd(u):
                kb.tt("pool", U(pqb, u), U(ab_sb, u), U(Pcg, u), ALU.mult)
                kb.tt("pool", U(Xcg, u), U(Xcg, u), U(pqb, u), ALU.add)
                kb.tt("dve", U(rrb, u), U(ab_sb, u), (Q_u[u][:], u), ALU.mult)
                kb.tt("dve", U(Rcg, u), U(Rcg, u), U(rrb, u), ALU.subtract)
                kb.act(U(rrb, u), U(Rcg, u), AF.Square)
                colreduce_u(u, rrb, 8)
                kb.copy("dve", Us(dnsc, u), (S_u[u][:, 8:16], u))

            def smallb():
                kb.tt("dve", s16a[:], dnsc[:], dsc[:], ALU.subtract)
                kb.tt("dve", s16a[:], s16a[:], ndone[:], ALU.mult)
                kb.tt("dve", dnsc[:], dsc[:], s16a[:], ALU.add)
                kb.tt("dve", s16a[:], dsc[:], ndone[:], ALU.mult)
                kb.tt("dve", s16a[:], s16a[:], donec[:], ALU.add)
                kb.op("dve", lambda g: g.reciprocal(out=s16b[:], in_=s16a[:]),
                      reads=[s16a], writes=[s16b])
                kb.tt("dve", besc[:], dnsc[:], s16b[:], ALU.mult)
                kb.tt("dve", besc[:], besc[:], ndone[:], ALU.mult)
                kb.copy("dve", dsc[:], dnsc[:])

            def pupd(u):
                kb.tt("pool", U(pqb, u), U(ab_sb, u), U(Pcg, u), ALU.mult)
                kb.tt("dve", U(Pcg, u), U(Rcg, u), U(pqb, u), ALU.add)

            for it in range(NIT):
                seg1(0)
                seg1(1)
                smalla()
                route(alsc[:])
                upd(0)
                upd(1)
                smallb()
                route(besc[:])
                pupd(0)
                pupd(1)

            for u in range(2):
                kb.tt("dve", U(pqb, u), U(Xcg, u), U(rqb, u), ALU.mult)
                kb.tt("pool", U(rrb, u), U(Xcg, u), U(egrqb, u), ALU.mult)
                for hh in range(8):
                    kb.mm((S_u[u][:, hh * 128:(hh + 1) * 128], u),
                          Uh(vfm_sb, u, hh), Uh(pqb, u, hh))
                kb.tt("dve", U(swb, u), U(wm, u), (S_u[u][:], u), ALU.mult)
                for hh in range(8):
                    h = u * 8 + hh
                    kb.mm((Q_u[u][:, hh * 128:(hh + 1) * 128], u),
                          Uh(ktm_sb, u, hh), Uh(swb, u, hh),
                          start=True, stop=False)
                    kb.mm((Q_u[u][:, hh * 128:(hh + 1) * 128], u),
                          Uh(hkvb, u, hh), Uh(rrb, u, hh),
                          start=False, stop=True)
                kb.act(U(pqb, u), (Q_u[u][:], u), AF.Square)
                colreduce_u(u, pqb, 0)
                kb.ts("dve", Us(s16a, u), (S_u[u][:, 0:8], u), 1.0 / 128,
                      ALU.mult, 1e-6, ALU.add)
                kb.act(Us(s16a, u), Us(s16a, u), AF.Ln)
                kb.act(Us(s16a, u), Us(s16a, u), AF.Exp, scale=-0.5)
                kb.trans((S_u[u][0:8, 128:256], u), Us(s16a, u), c_ident[:])
                kb.copy("dve", abT[0:8, :], (S_u[u][0:8, 128:256], u))
                kb.dma(abrow2[:1, 0:1024], abT[0:8, :], grp="fl0")
                kb.mm((S_u[u][:, 0:512], u), c_ones1b[:], abrow2[:1, 0:512])
                kb.mm((S_u[u][:, 512:1024], u), c_ones1b[:],
                      abrow2[:1, 512:1024])
                kb.copy("act", U(ab_sb, u), (S_u[u][:], u))
                kb.stt(U(onorm_sb, u), (Q_u[u][:], u), onw_sb[:, 0:1],
                       U(ab_sb, u), ALU.mult, ALU.mult)
                kb.dma(ONORM[c][:, u * 1024:(u + 1) * 1024],
                       U(onorm_sb, u), grp=f"son{u}")
            glast_wcol(c)
            state_update(c)


def emit_p2d(kb, cfg, T):
    nc = kb.nc
    BF, BH = T["BF"], T["BH"]
    PSA, PSB = T["PSA"], T["PSB"]
    pd4 = [PSA[:, 0:512], PSA[:, 512:1024], PSB[:, 0:512], PSB[:, 512:1024]]
    ONORM, ATTNO, xin, wo = T["ONORM"], T["ATTNO"], T["xin"], T["wo"]
    sbuf = T["sbuf"]
    wod = [sbuf(f"wod{i}", [128, 512], F32) for i in range(2)]
    wob = [sbuf(f"wob{i}", [128, 512], BF16) for i in range(2)]
    NCH = cfg.get("n_chunks", 16)
    for tg in range(4):
        chs = [tg * 4 + i for i in range(4) if tg * 4 + i < NCH]
        if not chs:
            continue
        for i, ch in enumerate(chs):
            kb.dma(BH[1 + i][:], ONORM[ch], grp="p2d_on")
        for cq in range(4):
            for ct in range(16):
                b = ct % 2
                kb.dma(wod[b][:], wo[ct * 128:(ct + 1) * 128,
                                     cq * 512:(cq + 1) * 512], grp=f"p2dw{b}")
                kb.copy("dve", wob[b][:], wod[b][:])
                for i, ch in enumerate(chs):
                    kb.mm(pd4[i],
                          BH[1 + i][:, ct * 128:(ct + 1) * 128], wob[b][:],
                          start=(ct == 0), stop=(ct == 15))
            for i, ch in enumerate(chs):
                kb.dma(BF[4 + i][:, 0:512],
                       xin[ch * 128:(ch + 1) * 128, cq * 512:(cq + 1) * 512],
                       grp="p2dx")
                kb.tt("dve", BF[4 + i][:, 512:1024], pd4[i],
                      BF[4 + i][:, 0:512], ALU.add)
                kb.dma(ATTNO[ch * 128:(ch + 1) * 128, cq * 512:(cq + 1) * 512],
                       BF[4 + i][:, 512:1024], q="act", grp="p2do")


def emit_p3(kb, cfg, T):
    nc = kb.nc
    BF, BH = T["BF"], T["BH"]
    PSA, PSB, PSC, PSD = T["PSA"], T["PSB"], T["PSC"], T["PSD"]
    sbuf = T["sbuf"]
    XN_TM, ATTNO, out = T["XN_TM"], T["ATTNO"], T["out"]
    w1T, w2T, w3T = T["w1T"], T["w2T"], T["w3T"]
    g1r, g2r, g3r = T["g1r"], T["g2r"], T["g3r"]
    XQ1, XQ2, XQ3, HG3 = T["XQ1"], T["XQ2"], T["XQ3"], T["HG3"]
    c_ones1, c_ones1b, c_onesdb, c_ident, c_identb = (
        T["c_ones1"], T["c_ones1b"], T["c_onesdb"], T["c_ident"], T["c_identb"])
    c_onesd = T["c_onesd"]
    zi32 = sbuf("zi32", [128, 2048], I32)
    col1 = sbuf("p3_col1", [128, 1], F32)
    col2 = sbuf("p3_col2", [128, 1], F32)
    rstd = sbuf("p3_rstd", [128, 1], F32)
    inv1 = sbuf("p3_inv1", [128, 16], F32)
    inv2 = sbuf("p3_inv2", [128, 16], F32)
    h2st = sbuf("p3_h2st", [128, 16], F32)
    hmx = sbuf("p3_hmx", [128, 16], F32)
    sc3 = sbuf("p3_sc3", [128, 16], F32)
    iv3 = sbuf("p3_iv3", [128, 16], F32)
    srow = sbuf("p3_srow", [1, 8], F32)
    acol = sbuf("p3_acol", [128, 4], F32)
    thrc = sbuf("p3_thrc", [128, 4], F32)
    NCH = 16

    for wi, (wT, nblk) in enumerate(((w1T, 64), (w2T, 64), (w3T, 64))):
        wvT = wT.rearrange("a b -> (a b)").rearrange("(n p f) -> n p f",
                                                     p=128, f=2048)
        s16 = sbuf("p3_s16", [1, 16], F32) if wi == 0 else s16
        for blk in range(nblk):
            b = blk % 2
            kb.dma(BF[b][:], wvT[blk], grp=f"p3s{b}")
            kb.act(BF[2 + b][:], BF[b][:], AF.Abs)
            tgts = (PSA[:1, 0:512], PSA[:1, 512:1024],
                    PSB[:1, 0:512], PSB[:1, 512:1024])
            for q4 in range(4):
                kb.mm(tgts[q4], c_onesd[:], BF[2 + b][:, q4 * 512:(q4 + 1) * 512],
                      start=(blk == 0), stop=(blk == nblk - 1))
        for q4, tgt in enumerate((PSA[:1, 0:512], PSA[:1, 512:1024],
                                  PSB[:1, 0:512], PSB[:1, 512:1024])):
            kb.red("dve", s16[:1, q4 * 4:(q4 + 1) * 4],
                   tgt.rearrange("a (g x) -> a g x", x=128), ALU.add)
        kb.red("dve", srow[:1, wi:wi + 1], s16[:1, :], ALU.add)
    kb.ts("dve", srow[:1, 0:3], srow[:1, 0:3], 1.0 / 16777216.0, ALU.mult)
    kb.ts("dve", srow[:1, 3:6], srow[:1, 0:3], -1.0, ALU.mult, 1.0, ALU.add)
    kb.op("dve", lambda g: g.reciprocal(out=srow[:1, 3:6], in_=srow[:1, 3:6]),
          reads=[srow], writes=[srow])
    kb.ts("dve", srow[:1, 6:8], srow[:1, 0:2], 1.0, ALU.add)
    kb.tt("dve", srow[:1, 6:8], srow[:1, 6:8], srow[:1, 3:5], ALU.mult)
    kb.act(srow[:1, 6:8], srow[:1, 6:8], AF.Ln)
    kb.ts("dve", srow[:1, 6:8], srow[:1, 6:8], 0.5, ALU.mult)
    kb.ts("dve", srow[:1, 5:6], srow[:1, 2:3], 1.0, ALU.add)
    kb.tt("dve", srow[:1, 5:6], srow[:1, 5:6], srow[:1, 3:4].rearrange("a b -> a b"),
          ALU.mult) if False else None
    kb.ts("dve", srow[:1, 3:4], srow[:1, 2:3], -1.0, ALU.mult, 1.0, ALU.add)
    kb.op("dve", lambda g: g.reciprocal(out=srow[:1, 3:4], in_=srow[:1, 3:4]),
          reads=[srow], writes=[srow])
    kb.tt("dve", srow[:1, 5:6], srow[:1, 5:6], srow[:1, 3:4], ALU.mult)
    kb.act(srow[:1, 5:6], srow[:1, 5:6], AF.Ln)
    kb.ts("dve", srow[:1, 5:6], srow[:1, 5:6], 0.5, ALU.mult)
    kb.ts("dve", srow[:1, 0:3], srow[:1, 0:3], 1e-8, ALU.add,
          0.5493061443340549, ALU.mult)
    kb.mm(PSC[:, 0:8], c_ones1[:], srow[:1, 0:8])
    kb.copy("dve", thrc[:, 0:3], PSC[:, 0:3])
    kb.copy("dve", acol[:, 0:3], PSC[:, 5:8])

    def round_chain(y_f32, out_f32):
        kb.copy("dve", zi32[:], y_f32)
        kb.stt(BF[5][:], zi32[:], -1.0, y_f32, ALU.mult, ALU.add)
        kb.ts("dve", BH[14][:], BF[5][:], 0.5, ALU.is_gt)
        kb.ts("dve", BH[15][:], BF[5][:], -0.5, ALU.is_lt)
        kb.tt("pool", BF[5][:], BH[14][:], zi32[:], ALU.add)
        kb.tt("dve", out_f32, BF[5][:], BH[15][:], ALU.subtract)

    if "p3pre" in cfg["phases"]:
        for gi, grow in enumerate((g1r, g2r)):
            kb.dma(srow[:1, 0:1], grow[:1, 0:1], grp="p3g") if False else None
        g1t, g2t = BF[2], BF[3]
        grows = T["rows32"]
        for gi, (grow, gt) in enumerate(((g1r, g1t), (g2r, g2t))):
            kb.dma(grows[:1, 0:2048], grow, grp="p3g")
            for qq in range(2):
                kb.mm(PSA[:, qq * 512:(qq + 1) * 512], c_ones1[:],
                      grows[:1, qq * 512:(qq + 1) * 512])
                kb.mm(PSB[:, qq * 512:(qq + 1) * 512], c_ones1[:],
                      grows[:1, 1024 + qq * 512:1024 + (qq + 1) * 512])
            kb.copy("dve", gt[:, 0:1024], PSA[:])
            kb.copy("dve", gt[:, 1024:2048], PSB[:])
        for tb in range(NCH):
            kb.dma(BF[0][:], XN_TM[tb * 128:(tb + 1) * 128, :], grp="p3x")
            kb.act(BF[1][:], BF[0][:], AF.Square)
            kb.red("dve", col1[:], BF[1][:], ALU.add)
            kb.ts("dve", col1[:], col1[:], 1.0 / 2048, ALU.mult, 1e-8, ALU.add)
            kb.act(rstd[:], col1[:], AF.Ln)
            kb.act(rstd[:], rstd[:], AF.Exp, scale=-0.5)
            for gi, (gt, XQ, invv) in enumerate(((g1t, XQ1, inv1),
                                                 (g2t, XQ2, inv2))):
                kb.stt(BF[1][:], BF[0][:], rstd[:, 0:1], gt[:], ALU.mult,
                       ALU.mult)
                kb.red("dve", col1[:], BF[1][:], ALU.max, absval=True)
                kb.ts("dve", col1[:], col1[:], 1e-4, ALU.max)
                kb.op("dve", lambda g: g.reciprocal(out=col2[:], in_=col1[:]),
                      reads=[col1], writes=[col2])
                kb.ts("dve", invv[:, tb:tb + 1], col1[:], 1.0 / 127, ALU.mult)
                kb.ts("dve", col2[:], col2[:], 127.0, ALU.mult)
                kb.act(BF[4][:], BF[1][:], AF.Copy, scale=col2[:, 0:1])
                round_chain(BF[4][:], BF[1][:])
                for ct in range(16):
                    pb = ct % 2
                    kb.trans(PSC[:, 512 + pb * 128:512 + (pb + 1) * 128],
                             BF[1][:, ct * 128:(ct + 1) * 128], c_ident[:])
                    kb.copy("act", BH[12][:, ct * 128:(ct + 1) * 128],
                            PSC[:, 512 + pb * 128:512 + (pb + 1) * 128])
                kb.dma(XQ[:, tb * 128:(tb + 1) * 128].rearrange(
                           "(c p) t -> p c t", p=128),
                       BH[12][:], q="act", grp="p3q")

    if "p3mid" in cfg["phases"]:
        g3t = BF[6]
        grow3 = T["rows32"]
        kb.memset("dve", h2st[:], 0.0)
        kb.memset("dve", hmx[:], 0.0)
        for hb in range(16):
            kb.dma(grow3[:1, 0:512], g3r[0:1, hb * 512:(hb + 1) * 512], grp="p3g")
            kb.mm(PSC[:, 0:512], c_ones1[:], grow3[:1, 0:512])
            kb.copy("dve", g3t[:, 0:512], PSC[:, 0:512])
            for wi, (wT, wst) in enumerate(((w1T, 0), (w2T, 1))):
                for ct in range(16):
                    kb.dma(BF[0][:, 0:512], wT[ct * 128:(ct + 1) * 128,
                                               hb * 512:(hb + 1) * 512],
                           grp=f"p3w{wi}")
                    kb.ts("dve", BF[1][:, 0:512], BF[0][:, 0:512],
                          thrc[:, wi:wi + 1], ALU.is_gt)
                    kb.ts("dve", BF[1][:, 512:1024], BF[0][:, 0:512],
                          thrc[:, wi:wi + 1], ALU.is_lt) if False else None
                    kb.ts("pool", BF[4][:, 0:512], BF[0][:, 0:512],
                          0.0, ALU.subtract) if False else None
                    kb.ts("dve", BF[4][:, 0:512], BF[0][:, 0:512], -1.0, ALU.mult)
                    kb.ts("dve", BF[4][:, 512:1024], BF[4][:, 0:512],
                          thrc[:, wi:wi + 1], ALU.is_gt)
                    kb.tt("dve", BH[wst * 4 + ct // 4][:,
                          (ct % 4) * 512:(ct % 4 + 1) * 512],
                          BF[1][:, 0:512], BF[4][:, 512:1024], ALU.subtract)
            for tb in range(NCH):
                kb.dma(BH[8][:], XQ1[:, tb * 128:(tb + 1) * 128].rearrange(
                    "(c p) t -> p c t", p=128), grp="p3xs")
                kb.dma(BH[9][:], XQ2[:, tb * 128:(tb + 1) * 128].rearrange(
                    "(c p) t -> p c t", p=128), grp="p3xs")
                for ct in range(16):
                    kb.mm(PSA[:, 0:512], BH[8][:, ct * 128:(ct + 1) * 128],
                          BH[0 + ct // 4][:, (ct % 4) * 512:(ct % 4 + 1) * 512],
                          start=(ct == 0), stop=(ct == 15))
                    kb.mm(PSB[:, 0:512], BH[9][:, ct * 128:(ct + 1) * 128],
                          BH[4 + ct // 4][:, (ct % 4) * 512:(ct % 4 + 1) * 512],
                          start=(ct == 0), stop=(ct == 15))
                kb.ts("dve", col1[:], inv1[:, tb:tb + 1], 1.0, ALU.mult)
                kb.tt("dve", col1[:], col1[:], acol[:, 1:2], ALU.mult)
                kb.ts("dve", col2[:], inv2[:, tb:tb + 1], 1.0, ALU.mult)
                kb.tt("dve", col2[:], col2[:], acol[:, 2:3], ALU.mult)
                kb.act(BF[0][:, 0:512], PSA[:, 0:512], AF.Silu,
                       scale=col1[:, 0:1])
                kb.act(BF[0][:, 512:1024], PSB[:, 0:512], AF.Copy,
                       scale=col2[:, 0:1])
                kb.tt("dve", BF[1][:, 0:512], BF[0][:, 0:512],
                      BF[0][:, 512:1024], ALU.mult)
                kb.tt("dve", BH[13][:, 0:512], BF[1][:, 0:512], g3t[:, 0:512],
                      ALU.mult)
                kb.dma(HG3[tb * 128:(tb + 1) * 128, hb * 512:(hb + 1) * 512],
                       BH[13][:, 0:512], q="act", grp="p3h")
                kb.act(BF[4][:, 0:512], BF[1][:, 0:512], AF.Square,
                       accum=col1[:])
                kb.tt("dve", h2st[:, tb:tb + 1], h2st[:, tb:tb + 1], col1[:],
                      ALU.add)
                kb.red("dve", col2[:], BH[13][:, 0:512], ALU.max, absval=True)
                kb.tt("dve", hmx[:, tb:tb + 1], hmx[:, tb:tb + 1], col2[:],
                      ALU.max)

    if "p3b" in cfg["phases"]:
        kb.ts("dve", h2st[:], h2st[:], 1.0 / 8192, ALU.mult, 1e-8, ALU.add)
        kb.act(h2st[:], h2st[:], AF.Ln)
        kb.act(h2st[:], h2st[:], AF.Exp, scale=-0.5)
        kb.tt("dve", hmx[:], hmx[:], h2st[:], ALU.mult)
        kb.ts("dve", hmx[:], hmx[:], 1e-4, ALU.max)
        kb.tt("dve", iv3[:], hmx[:], acol[:, 0:1].rearrange("a b -> a b")
              if False else hmx[:], ALU.bypass) if False else None
        kb.op("dve", lambda g: g.reciprocal(out=sc3[:], in_=hmx[:]),
              reads=[hmx], writes=[sc3])
        kb.ts("dve", sc3[:], sc3[:], 127.0, ALU.mult)
        kb.tt("dve", sc3[:], sc3[:], h2st[:], ALU.mult)
        kb.ts("dve", iv3[:], hmx[:], 1.0 / 127, ALU.mult)
        for i in range(16):
            kb.tt("dve", iv3[:, i:i + 1], iv3[:, i:i + 1], acol[:, 0:1], ALU.mult)
        for tb in range(NCH):
            for hq in range(4):
                kb.dma(BH[8][:], HG3[tb * 128:(tb + 1) * 128,
                                     hq * 2048:(hq + 1) * 2048], grp="p3h2")
                kb.act(BF[1][:], BH[8][:], AF.Copy, scale=sc3[:, tb:tb + 1])
                round_chain(BF[1][:], BF[4][:])
                for ct in range(16):
                    pb = ct % 2
                    kb.trans(PSC[:, 512 + pb * 128:512 + (pb + 1) * 128],
                             BF[4][:, ct * 128:(ct + 1) * 128], c_ident[:])
                    kb.copy("act", BH[12][:, ct * 128:(ct + 1) * 128],
                            PSC[:, 512 + pb * 128:512 + (pb + 1) * 128])
                kb.dma(XQ3[hq * 2048:(hq + 1) * 2048,
                           tb * 128:(tb + 1) * 128].rearrange(
                           "(c p) t -> p c t", p=128),
                       BH[12][:], q="act", grp="p3q3")
        for c8 in range(8):
            for hh in range(16):
                for sub in range(4):
                    r0 = hh * 512 + sub * 128
                    kb.dma(BF[0][:, 0:256], w3T[r0:r0 + 128,
                                                c8 * 256:(c8 + 1) * 256],
                           grp="p3w3")
                    kb.ts("dve", BF[1][:, 0:256], BF[0][:, 0:256],
                          thrc[:, 2:3], ALU.is_gt)
                    kb.ts("dve", BF[4][:, 0:256], BF[0][:, 0:256], -1.0, ALU.mult)
                    kb.ts("dve", BF[4][:, 256:512], BF[4][:, 0:256],
                          thrc[:, 2:3], ALU.is_gt)
                    kb.tt("dve", BH[(hh % 4) * 2 + sub // 2][:,
                          (sub % 2) * 256 + ((hh // 4) % 2) * 512:
                          (sub % 2) * 256 + ((hh // 4) % 2) * 512 + 256],
                          BF[1][:, 0:256], BF[4][:, 256:512], ALU.subtract) \
                        if False else None
                    kb.tt("dve", BH[hh // 2][:, (hh % 2) * 1024 + sub * 256:
                                              (hh % 2) * 1024 + (sub + 1) * 256],
                          BF[1][:, 0:256], BF[4][:, 256:512], ALU.subtract)
            for tb in range(NCH):
                kb.dma(BH[8][:], XQ3[:, tb * 128:(tb + 1) * 128].rearrange(
                    "(c p) t -> p (c t)", p=128), grp="p3xs3") \
                    if False else None
                for hq in range(4):
                    kb.dma(BH[8 + hq][:, 0:2048], XQ3[
                        hq * 2048:(hq + 1) * 2048,
                        tb * 128:(tb + 1) * 128].rearrange(
                        "(c p) t -> p c t", p=128), grp="p3xs3")
                for hh in range(16):
                    for sub in range(4):
                        hidx = hh * 4 + sub
                        kb.mm(PSD[:, 0:256],
                              BH[8 + hidx // 16][:, (hidx % 16) * 128:
                                                 (hidx % 16 + 1) * 128],
                              BH[hh // 2][:, (hh % 2) * 1024 + sub * 256:
                                           (hh % 2) * 1024 + (sub + 1) * 256],
                              start=(hidx == 0), stop=(hidx == 63))
                kb.dma(BF[6][:, 0:256], ATTNO[tb * 128:(tb + 1) * 128,
                                              c8 * 256:(c8 + 1) * 256],
                       grp="p3at")
                kb.stt(BF[6][:, 256:512], PSD[:, 0:256], iv3[:, tb:tb + 1],
                       BF[6][:, 0:256], ALU.mult, ALU.add)
                kb.dma(out[tb * 128:(tb + 1) * 128, c8 * 256:(c8 + 1) * 256],
                       BF[6][:, 256:512], q="act", grp="p3out")



def kernel(**inputs):
    cfg = {"phases": {"p0", "p1", "p2a", "p2b", "p2c", "p2d", "p3pre",
                      "p3mid", "p3b"}, "n_chunks": 16, "expose": (), "dbg": {}}
    nc, kb = build_program(cfg)
    maps = [make_core_inputs(inputs, c, cfg) for c in range(8)]
    res = run_bass_kernel_spmd(nc, maps, core_ids=list(range(8)))
    out = np.zeros((2, 8192, 2048), np.float32)
    for c in range(8):
        b, qt = c // 4, c % 4
        out[b, qt * 2048:(qt + 1) * 2048] = np.asarray(res.results[c]["outp"])
    return out


# version 09 (final)
# speedup vs baseline: 1.3913x; 1.0650x over previous
import numpy as np
import ml_dtypes
import concourse.bass as bass
import concourse.mybir as mybir
from concourse.bass_utils import run_bass_kernel_spmd

F32 = mybir.dt.float32
BF16 = mybir.dt.bfloat16
I32 = mybir.dt.int32
I8 = mybir.dt.int8
AF = mybir.ActivationFunctionType
ALU = mybir.AluOpType
AX = mybir.AxisListType
BFnp = ml_dtypes.bfloat16

ENGS = ("pe", "act", "dve", "pool", "sp")
ENG_ATTR = {"pe": "tensor", "act": "scalar", "dve": "vector", "pool": "gpsimd",
            "sp": "sync"}

class KB:

    def __init__(self, nc):
        self.nc = nc
        self.sem = {}
        self.cnt = {e: 0 for e in ENGS}
        self.cnt["dma"] = 0
        self._es = None
        self.dep = {}
        self.grp_rr = {}
        self.know = {e: {} for e in ENGS}
        self.snap = {}
        self.hw = {}

    def init_sems(self, es):
        self._es = es
        for e in ENGS:
            self.sem[e] = es.enter_context(self.nc.semaphore(f"sem_{e}"))

    def _ensure_grp(self, g):
        if g not in self.sem:
            self.sem[g] = self._es.enter_context(self.nc.semaphore(f"sem_{g}"))
            self.cnt[g] = 0

    def eng(self, e):
        return getattr(self.nc, ENG_ATTR[e])

    def _tname(self, ap):
        if isinstance(ap, tuple):
            ap, rk = ap
            t = ap.tensor if isinstance(ap, bass.AP) else ap
            return t.name + "#" + str(rk)
        t = ap.tensor if isinstance(ap, bass.AP) else ap
        return t.name

    def _unwrap(self, ap):
        return ap[0] if isinstance(ap, tuple) else ap

    def _need(self, waiter, producer):
        peng, pcnt = producer
        if pcnt <= 0:
            return False
        return self.know[waiter].get(peng, 0) < pcnt

    def _wait(self, waiter, producer):
        peng, pcnt = producer
        self.eng(waiter).wait_ge(self.sem[peng], pcnt)
        kw = self.know[waiter]
        kw[peng] = max(kw.get(peng, 0), pcnt)
        sn = self.snap.get(peng, {}).get(pcnt)
        if sn:
            for x, c in sn.items():
                if kw.get(x, 0) < c:
                    kw[x] = c

    def op(self, e, fn, reads=(), writes=(), dma=None, n=1, win=2):
        deps = []
        def is_psum(ap):
            ap = self._unwrap(ap)
            t = ap.tensor if isinstance(ap, bass.AP) else ap
            return "PSum" in type(t).__name__
        def dep_keys(ap):
            nm = self._tname(ap)
            ks = [nm]
            if "#" in nm:
                ks.append(nm.split("#")[0])
            else:
                ks.extend(k for k in self.dep if k.startswith(nm + "#"))
            return ks
        for ap in reads:
            for k in dep_keys(ap):
                d = self.dep.get(k)
                if d:
                    deps.extend(d["w"].items())
        for ap in writes:
            for k in dep_keys(ap):
                d = self.dep.get(k)
                if d:
                    items = list(d["w"].items()) + list(d["r"].items())
                    if e == "pe" and is_psum(ap):
                        items = [(pe_, pc) for (pe_, pc) in items if pe_ != "pe"]
                    deps.extend(items)
        byeng = {}
        for (pe, pc) in deps:
            byeng[pe] = max(byeng.get(pe, 0), pc)
        if dma:
            idx = self.grp_rr.get(dma, 0)
            self.grp_rr[dma] = (idx + 1) % win
            sem_eng = f"{dma}#{idx}"
            self._ensure_grp(sem_eng)
            byeng[sem_eng] = max(byeng.get(sem_eng, 0), self.cnt[sem_eng])
        else:
            sem_eng = e
        for pe, pc in byeng.items():
            if self._need(e, (pe, pc)):
                self._wait(e, (pe, pc))
        inst = fn(self.eng(e))
        step = 16 if dma else n
        self.cnt[sem_eng] += step
        inst.then_inc(self.sem[sem_eng], step)
        stamp = (sem_eng, self.cnt[sem_eng])
        kn = dict(self.know[e])
        kn[e] = self.cnt[e] if not dma else kn.get(e, 0)
        self.snap.setdefault(sem_eng, {})[self.cnt[sem_eng]] = kn
        for ap in writes:
            nm = self._tname(ap)
            if "#" not in nm:
                for k in [k for k in self.dep if k.startswith(nm + "#")]:
                    del self.dep[k]
            d = self.dep.get(nm)
            if d is None or not dma:
                self.dep[nm] = {"w": {stamp[0]: stamp[1]}, "r": {}}
            else:
                d["w"][stamp[0]] = stamp[1]
                d["r"] = {}
        for ap in reads:
            nm = self.dep.setdefault(self._tname(ap), {"w": {}, "r": {}})
            nm["r"][stamp[0]] = stamp[1]
        return inst

    def dma(self, dst, src, q="sp", grp="dma"):
        d2, s2 = self._unwrap(dst), self._unwrap(src)
        return self.op(q, lambda g: g.dma_start(out=d2, in_=s2),
                       reads=[src], writes=[dst], dma=grp)

    def dmat(self, dst, src, q="sp", grp="dma"):
        return self.op(q, lambda g: g.dma_start_transpose(out=dst, in_=src),
                       reads=[src], writes=[dst], dma=grp)

    def mm(self, out, lhsT, rhs, start=True, stop=True):
        o2, l2, r2 = self._unwrap(out), self._unwrap(lhsT), self._unwrap(rhs)
        return self.op("pe", lambda g: g.matmul(o2, lhsT=l2, rhs=r2,
                                                start=start, stop=stop),
                       reads=[lhsT, rhs], writes=[out])

    def trans(self, out, in_, ident):
        o2, i2 = self._unwrap(out), self._unwrap(in_)
        return self.op("pe", lambda g: g.transpose(o2, in_=i2, identity=ident),
                       reads=[in_, ident], writes=[out])

    def tt(self, e, out, a, b, alu):
        o2, a2, b2 = self._unwrap(out), self._unwrap(a), self._unwrap(b)
        return self.op(e, lambda g: g.tensor_tensor(out=o2, in0=a2, in1=b2, op=alu),
                       reads=[a, b], writes=[out])

    def ts(self, e, out, a, s1, op0, s2=None, op1=ALU.bypass, accum=None):
        reads = [a] + [x for x in (s1, s2) if isinstance(x, (bass.AP, tuple))]
        writes = [out] + ([accum] if accum is not None else [])
        o2, a2 = self._unwrap(out), self._unwrap(a)
        s1u = self._unwrap(s1) if isinstance(s1, tuple) else s1
        s2u = self._unwrap(s2) if isinstance(s2, tuple) else s2
        return self.op(e, lambda g: g.tensor_scalar(
            out=o2, in0=a2, scalar1=s1u, scalar2=s2u, op0=op0, op1=op1,
            accum_out=self._unwrap(accum) if accum is not None else None),
            reads=reads, writes=writes)

    def stt(self, out, a, s, b, op0, op1, accum=None):
        reads = [a, b] + ([s] if isinstance(s, (bass.AP, tuple)) else [])
        writes = [out] + ([accum] if accum is not None else [])
        o2, a2, b2 = self._unwrap(out), self._unwrap(a), self._unwrap(b)
        su = self._unwrap(s) if isinstance(s, tuple) else s
        return self.op("dve", lambda g: g.scalar_tensor_tensor(
            out=o2, in0=a2, scalar=su, in1=b2, op0=op0, op1=op1,
            accum_out=self._unwrap(accum) if accum is not None else None),
            reads=reads, writes=writes)

    def constcol(self, val):
        if not hasattr(self, "_ccache"):
            self._ccache = {}
        if val not in self._ccache:
            t = self._es.enter_context(
                self.nc.sbuf_tensor(f"cc_{len(self._ccache)}", [128, 1], F32))
            self.memset("dve", t[:], float(val))
            self._ccache[val] = t
        return self._ccache[val][:, 0:1]

    def act(self, out, in_, func, bias=None, scale=None, accum=None):
        kw = {}
        if bias is not None:
            if isinstance(bias, float):
                bias = self.constcol(bias)
            kw["bias"] = bias
        if scale is not None:
            kw["scale"] = scale
        if accum is not None:
            kw["accum_out"] = accum
        reads = [in_] + [x for x in (bias, scale) if isinstance(x, (bass.AP, tuple))]
        writes = [out] + ([accum] if accum is not None else [])
        if "bias" in kw and isinstance(kw["bias"], tuple):
            kw["bias"] = self._unwrap(kw["bias"])
        if "scale" in kw and isinstance(kw["scale"], tuple):
            kw["scale"] = self._unwrap(kw["scale"])
        if "accum_out" in kw and isinstance(kw["accum_out"], tuple):
            kw["accum_out"] = self._unwrap(kw["accum_out"])
        o2, i2 = self._unwrap(out), self._unwrap(in_)
        return self.op("act", lambda g: g.activation(out=o2, in_=i2, func=func,
                                                     **kw),
                       reads=reads, writes=writes)

    def copy(self, e, out, in_):
        o2, i2 = self._unwrap(out), self._unwrap(in_)
        if e == "act":
            return self.op("act", lambda g: g.copy(out=o2, in_=i2),
                           reads=[in_], writes=[out])
        return self.op(e, lambda g: g.tensor_copy(out=o2, in_=i2),
                       reads=[in_], writes=[out])

    def red(self, e, out, in_, alu, axis=AX.X, absval=None, negate=None):
        o2, i2 = self._unwrap(out), self._unwrap(in_)
        return self.op(e, lambda g: g.tensor_reduce(
            out=o2, in_=i2, axis=axis, op=alu,
            apply_absolute_value=absval, negate=negate),
            reads=[in_], writes=[out])

    def ttr(self, out, a, b, op0, op1, accum, scale=1.0, seed=0.0):
        return self.op("dve", lambda g: g.tensor_tensor_reduce(
            out=out, in0=a, in1=b, scale=scale, scalar=seed, op0=op0, op1=op1,
            accum_out=accum), reads=[a, b], writes=[out, accum])

    def memset(self, e, ap, val):
        a2 = self._unwrap(ap)
        return self.op(e, lambda g: g.memset(a2, val), writes=[ap])

    def barrier_all(self):
        marks = {}
        for e in ENGS:
            eng = self.eng(e)
            i = eng.nop() if e in ("act", "pe", "sp") else eng.engine_nop()
            self.cnt[e] += 1
            i.then_inc(self.sem[e], 1)
            marks[e] = self.cnt[e]
        for e in ENGS:
            for o in ENGS:
                if o != e and self._need(e, (o, marks[o])):
                    self._wait(e, (o, marks[o]))


def softplus_dev(kb, out, in_, tmp1, tmp2):
    kb.act(tmp1, in_, AF.Abs)
    kb.act(tmp1, tmp1, AF.Exp, scale=-1.0)
    kb.act(tmp1, tmp1, AF.Ln, bias=1.0)
    kb.ts("dve", tmp2, in_, 0.0, ALU.max)
    kb.tt("dve", out, tmp2, tmp1, ALU.add)


def log_sigmoid_dev(kb, out, in_, tmp1, tmp2):
    kb.act(tmp1, in_, AF.Abs)
    kb.act(tmp1, tmp1, AF.Exp, scale=-1.0)
    kb.act(tmp1, tmp1, AF.Ln, bias=1.0)
    kb.ts("dve", tmp2, in_, 0.0, ALU.min)
    kb.tt("dve", out, tmp2, tmp1, ALU.subtract)



def build_program(cfg):
    nc = bass.Bass("TRN2", target_bir_lowering=False)
    kb = KB(nc)
    TOK, HID, NH, HD, NCH, CC = 2048, 2048, 16, 128, 16, 128

    dt = nc.dram_tensor
    xin = dt("xin", [TOK, HID], F32, kind="ExternalInput").ap()
    wq = dt("wq", [HID, HID], F32, kind="ExternalInput").ap()
    wk = dt("wk", [HID, HID], F32, kind="ExternalInput").ap()
    wv = dt("wv", [HID, HID], F32, kind="ExternalInput").ap()
    wab = dt("wab", [HID, 32], F32, kind="ExternalInput").ap()
    bab = dt("bab", [1, 32], F32, kind="ExternalInput").ap()
    lamT = dt("lamT", [HD, NH], F32, kind="ExternalInput").ap()
    onw = dt("onw", [HD, 1], F32, kind="ExternalInput").ap()
    wo = dt("wo", [HID, HID], F32, kind="ExternalInput").ap()
    w1T = dt("w1T", [HID, 8192], F32, kind="ExternalInput").ap()
    w2T = dt("w2T", [HID, 8192], F32, kind="ExternalInput").ap()
    w3T = dt("w3T", [8192, HID], F32, kind="ExternalInput").ap()
    g1r = dt("g1r", [1, HID], F32, kind="ExternalInput").ap()
    g2r = dt("g2r", [1, HID], F32, kind="ExternalInput").ap()
    g3r = dt("g3r", [1, 8192], F32, kind="ExternalInput").ap()
    ident = dt("ident", [128, 128], F32, kind="ExternalInput").ap()
    identb = dt("identb", [128, 128], BF16, kind="ExternalInput").ap()
    triu = dt("triu", [128, 128], F32, kind="ExternalInput").ap()
    tril_m = dt("tril_m", [128, 128], F32, kind="ExternalInput").ap()
    onescol = dt("onescol", [1, 128], F32, kind="ExternalInput").ap()
    onesd = dt("onesd", [128, 1], F32, kind="ExternalInput").ap()
    cmask = dt("cmask", [8, 8], F32, kind="ExternalInput").ap()
    cmaskcol = dt("cmaskcol", [8, 1], F32, kind="ExternalInput").ap()
    e127 = dt("e127", [128, 1], F32, kind="ExternalInput").ap()
    onescolb = dt("onescolb", [1, 128], BF16, kind="ExternalInput").ap()
    onesdb = dt("onesdb", [128, 1], BF16, kind="ExternalInput").ap()

    out = dt("outp", [TOK, HID], F32, kind="ExternalOutput").ap()

    ex = cfg.get("expose", ())
    kd = lambda nm: "ExternalOutput" if nm in ex else "Internal"
    XN_TM = dt("XN_TM", [TOK, HID], F32, kind=kd("XN_TM")).ap()
    XN_FM = dt("XN_FM", [HID, TOK], BF16, kind=kd("XN_FM")).ap()
    QFM = dt("QFM", [NCH, NH, HD, CC], F32, kind=kd("QFM")).ap()
    KFM = dt("KFM", [NCH, NH, HD, CC], BF16, kind=kd("KFM")).ap()
    KTM = dt("KTM", [NCH, CC, NH * HD], BF16, kind=kd("KTM")).ap()
    VTM = dt("VTM", [NCH, CC, NH * HD], BF16, kind=kd("VTM")).ap()
    VFM = dt("VFM", [NCH, NH, HD, CC], BF16, kind=kd("VFM")).ap()
    GTM = dt("GTM", [NCH, CC, NH], F32, kind=kd("GTM")).ap()
    BETA = dt("BETA", [NCH, CC, NH], F32, kind=kd("BETA")).ap()
    THR = dt("THR", [NCH, CC, NH], F32, kind=kd("THR")).ap()
    RQ = dt("RQ", [NCH, CC, NH], F32, kind=kd("RQ")).ap()
    ATTNO = dt("ATTNO", [TOK, HID], F32, kind=kd("ATTNO")).ap()
    ONORM = dt("ONORM", [NCH, HD, TOK // NCH * 16], BF16, kind=kd("ONORM")).ap()
    CIN = dt("CIN", [128, 4112], F32).ap()
    COUT = dt("COUT", [1024, 4112], F32).ap()
    XQ1 = dt("XQ1", [HID, TOK], BF16, kind=kd("XQ1")).ap()
    XQ2 = dt("XQ2", [HID, TOK], BF16, kind=kd("XQ2")).ap()
    XQ3 = dt("XQ3", [8192, TOK], BF16, kind=kd("XQ3")).ap()
    HG3 = dt("HG3", [TOK, 8192], BF16, kind=kd("HG3")).ap()
    dbg_out = {}
    for name, shape in cfg.get("dbg", {}).items():
        dbg_out[name] = dt("dbg_" + name, shape, F32, kind="ExternalOutput").ap()

    from contextlib import ExitStack
    es = ExitStack()
    kb._es = es
    sbuf = lambda n, s, d: es.enter_context(nc.sbuf_tensor(n, s, d))
    psum = lambda n, s, d: es.enter_context(nc.psum_tensor(n, s, d))
    kb.init_sems(es)

    c_ident = sbuf("c_ident", [128, 128], F32)
    c_identb = sbuf("c_identb", [128, 128], BF16)
    c_triu = sbuf("c_triu", [128, 128], F32)
    c_trilm = sbuf("c_trilm", [128, 128], F32)
    c_ones1 = sbuf("c_ones1", [1, 128], F32)
    c_ones1b = sbuf("c_ones1b", [1, 128], BF16)
    c_onesdb = sbuf("c_onesdb", [128, 1], BF16)
    c_e127 = sbuf("c_e127", [128, 1], F32)
    lamE2 = sbuf("lamE2", [128, 16], F32)
    lamdb = [sbuf(f"lamdb{h}", [128, 128], BF16) for h in range(16)]
    c_onesd = sbuf("c_onesd", [128, 1], F32)
    c_bab = sbuf("c_bab", [1, 32], F32)
    lam_sb = sbuf("lam_sb", [128, NH], F32)
    onw_sb = sbuf("onw_sb", [128, 1], F32)

    BF = [sbuf(f"BF{i}", [128, 2048], F32) for i in range(8)]
    BH = [sbuf(f"BH{i}", [128, 2048], BF16) for i in range(16)]

    PSA = psum("PSA", [128, 1024], F32)
    PSB = psum("PSB", [128, 1024], F32)
    PSC = psum("PSC", [128, 1024], F32)
    PSD = psum("PSD", [128, 1024], F32)
    psP = [PSA[:, 0:512], PSA[:, 512:1024], PSB[:, 0:512], PSB[:, 512:1024]]
    psZ = PSC[:, 0:512]
    psT = [PSC[:, 512:640], PSC[:, 640:768]]

    with es:
        kb.dma(c_ident[:], ident)
        kb.dma(c_identb[:], identb)
        kb.dma(c_triu[:], triu)
        kb.dma(c_trilm[:], tril_m)
        kb.dma(c_ones1[:], onescol)
        kb.dma(c_onesd[:], onesd)
        kb.dma(c_e127[:], e127)
        kb.copy("dve", c_ones1b[:], c_ones1[:])
        kb.copy("dve", c_onesdb[:], c_onesd[:])
        kb.dma(c_bab[:], bab)
        kb.dma(onw_sb[:], onw)
        t16a = sbuf("t16a", [128, NH], F32)
        t16b = sbuf("t16b", [128, NH], F32)
        kb.dma(lam_sb[:], lamT)
        softplus_dev(kb, lam_sb[:], lam_sb[:], t16a[:], t16b[:])
        kb.ts("dve", lam_sb[:], lam_sb[:], 0.25, ALU.add)
        kb.ts("dve", lamE2[:], lam_sb[:], 1e-8, ALU.add)
        for h in range(16):
            kb.ts("dve", lamdb[h][:], c_ident[:], lam_sb[:, h:h + 1], ALU.mult)

        if "p0" in cfg["phases"]:
            sx = [BF[0], BF[1]]
            xnt = [BF[2], BF[3]]
            stats = sbuf("p0_stats", [128, 24], F32)
            mv = sbuf("p0_mv", [128, 2], F32)
            rstd = sbuf("p0_rstd", [128, 1], F32)
            xfmb = sbuf("p0_xfmb", [128, 128], BF16)
            pT = psT
            for tt in range(16):
                b = tt % 2
                kb.dma(sx[b][:], xin[tt * 128:(tt + 1) * 128, :], grp=f"dx{b}")
                for j in range(4):
                    kb.op("dve", lambda g, j=j, b=b: g.bn_stats(
                        out=stats[:, j * 6:(j + 1) * 6], in_=sx[b][:, j * 512:(j + 1) * 512]),
                        reads=[sx[b]], writes=[stats])
                kb.op("dve", lambda g: g.bn_aggr(out=mv[:], in_=stats[:]),
                      reads=[stats], writes=[mv])
                kb.act(rstd[:], mv[:, 1:2], AF.Ln, bias=1e-5)
                kb.act(rstd[:], rstd[:], AF.Exp, scale=-0.5)
                kb.ts("dve", xnt[b][:], sx[b][:], mv[:, 0:1], ALU.subtract,
                      rstd[:, 0:1], ALU.mult)
                kb.dma(XN_TM[tt * 128:(tt + 1) * 128, :], xnt[b][:], q="act", grp="sx")
                for ct in range(16):
                    pb = ct % 2
                    kb.trans(pT[pb], xnt[b][:, ct * 128:(ct + 1) * 128], c_ident[:])
                    kb.copy("act", xfmb[:], pT[pb])
                    kb.dma(XN_FM[ct * 128:(ct + 1) * 128, tt * 128:(tt + 1) * 128],
                           xfmb[:], q="act", grp="sfm")

        if "p1" in cfg["phases"]:
            xnfm_sv = [BH[1], BH[2], BH[3], BH[4]]
            class _XN:
                def __getitem__(self, sl):
                    raise RuntimeError("use xnfm_at")
            def xnfm_at(ct, lo, hi):
                return xnfm_sv[ct // 4][:, (ct % 4) * 512 + lo:(ct % 4) * 512 + hi]
            wfp = [sbuf(f"p1_wfp{i}", [128, 512], F32) for i in range(2)]
            wbf = [sbuf(f"p1_wbf{i}", [128, 512], BF16) for i in range(2)]
            qtm = [BF[0], BF[1], BF[2], BF[3]]
            sq2 = BF[4]
            n2 = sbuf("p1_n2", [128, NH], F32)
            scal = sbuf("p1_scal", [128, NH], F32)
            ktb = BH[0]
            gb = sbuf("p1_gb", [128, 32], F32)
            gsm = sbuf("p1_gsm", [128, NH], F32)
            t16c = sbuf("p1_t16c", [128, NH], F32)
            fmb = sbuf("p1_fmb", [128, 128], BF16)
            fmf = sbuf("p1_fmf", [128, 128], F32)

            def proj_assemble(Wap, sblk):
                t0 = sblk * 512
                for nt in range(4):
                    for ct in range(16):
                        b = ct % 2
                        kb.dma(wfp[b][:], Wap[ct * 128:(ct + 1) * 128,
                                              nt * 512:(nt + 1) * 512], grp=f"dw{b}")
                        kb.copy("dve", wbf[b][:], wfp[b][:])
                        for tb in range(4):
                            kb.mm(psP[tb], xnfm_at(ct, tb * 128, (tb + 1) * 128),
                                  wbf[b][:], start=(ct == 0), stop=(ct == 15))
                    for tb in range(4):
                        kb.act(qtm[tb][:, nt * 512:(nt + 1) * 512], psP[tb], AF.Silu)

            for sblk in range(4):
                t0 = sblk * 512
                for ct in range(16):
                    kb.dma(xnfm_at(ct, 0, 512),
                           XN_FM[ct * 128:(ct + 1) * 128, t0:t0 + 512], grp="dxf")
                proj_assemble(wq, sblk)
                for tb in range(4):
                    ch = sblk * 4 + tb
                    q3 = qtm[tb][:].rearrange("p (h d) -> p h d", d=HD)
                    kb.act(sq2[:], qtm[tb][:], AF.Square)
                    kb.red("dve", n2[:], sq2[:].rearrange("p (h d) -> p h d", d=HD),
                           ALU.add)
                    kb.ts("dve", scal[:], n2[:], 1e-24, ALU.max, 1e-10, ALU.mult)
                    kb.dma(THR[ch], scal[:], q="act", grp="sth")
                    kb.act(t16c[:], n2[:], AF.Ln)
                    kb.act(t16c[:], t16c[:], AF.Exp, scale=0.5)
                    kb.ts("dve", t16c[:], t16c[:], 1e-12, ALU.max)
                    kb.op("dve", lambda g: g.reciprocal(out=t16c[:], in_=t16c[:]),
                          reads=[t16c], writes=[t16c])
                    kb.dma(RQ[ch], t16c[:], q="act", grp="sth")
                    for h in range(NH):
                        pb = h % 2
                        kb.trans(psT[pb], qtm[tb][:, h * HD:(h + 1) * HD], c_ident[:])
                        kb.copy("dve", fmf[:], psT[pb])
                        kb.dma(QFM[ch, h], fmf[:], q="act", grp="sq")
                proj_assemble(wk, sblk)
                for tb in range(4):
                    ch = sblk * 4 + tb
                    kb.act(sq2[:], qtm[tb][:], AF.Square)
                    kb.red("dve", n2[:], sq2[:].rearrange("p (h d) -> p h d", d=HD),
                           ALU.add)
                    kb.act(scal[:], n2[:], AF.Ln)
                    kb.act(scal[:], scal[:], AF.Exp, scale=0.5)
                    kb.ts("dve", scal[:], scal[:], 1e-12, ALU.max)
                    kb.op("dve", lambda g: g.reciprocal(out=scal[:], in_=scal[:]),
                          reads=[scal], writes=[scal])
                    for h in range(NH):
                        kb.ts("dve", sq2[:, h * HD:(h + 1) * HD],
                              qtm[tb][:, h * HD:(h + 1) * HD], scal[:, h:h + 1],
                              ALU.mult)
                    kb.copy("act", ktb[:], sq2[:])
                    kb.dma(KTM[ch], ktb[:], q="act", grp="sk")
                    for h in range(NH):
                        pb = h % 2
                        kb.trans(psT[pb], sq2[:, h * HD:(h + 1) * HD], c_ident[:])
                        kb.copy("dve", fmb[:], psT[pb])
                        kb.dma(KFM[ch, h], fmb[:], q="act", grp="sk")
                proj_assemble(wv, sblk)
                for tb in range(4):
                    ch = sblk * 4 + tb
                    kb.copy("dve", ktb[:], qtm[tb][:])
                    kb.dma(VTM[ch], ktb[:], q="act", grp="sv")
                    for h in range(NH):
                        pb = h % 2
                        kb.trans(psT[pb], qtm[tb][:, h * HD:(h + 1) * HD],
                                 c_ident[:])
                        kb.copy("dve", fmb[:], psT[pb])
                        kb.dma(VFM[ch, h], fmb[:], q="act", grp="sv")
                for tb in range(4):
                    ch = sblk * 4 + tb
                    for ct in range(16):
                        b = ct % 2
                        kb.dma(wfp[b][:, :32], wab[ct * 128:(ct + 1) * 128, :],
                               grp=f"dw{b}")
                        kb.copy("dve", wbf[b][:, :32], wfp[b][:, :32])
                        kb.mm(psZ[:, 0:32], xnfm_at(ct, tb * 128, (tb + 1) * 128),
                              wbf[b][:, :32], start=(ct == 0), stop=False)
                    kb.copy("dve", wbf[0][:1, :32], c_bab[:])
                    kb.mm(psZ[:, 0:32], c_ones1b[:], wbf[0][:1, :32],
                          start=False, stop=True)
                    kb.copy("dve", gb[:], psZ[:, 0:32])
                    kb.act(t16c[:], gb[:, 16:32], AF.Sigmoid)
                    kb.dma(BETA[ch], t16c[:], q="act", grp="sg")
                    log_sigmoid_dev(kb, gsm[:], gb[:, :16], t16c[:], n2[:])
                    kb.mm(psZ[:, 64:80], c_triu[:], gsm[:])
                    kb.copy("dve", t16c[:], psZ[:, 64:80])
                    kb.dma(GTM[ch], t16c[:], q="act", grp="sg")

        T = dict(sbuf=sbuf, psum=psum, BF=BF, BH=BH, KTM=KTM, VTM=VTM, KFM=KFM, VFM=VFM,
                 QFM=QFM, GTM=GTM, BETA=BETA, THR=THR, RQ=RQ, CIN=CIN,
                 COUT=COUT, ONORM=ONORM, ATTNO=ATTNO, XN_TM=XN_TM, xin=xin,
                 wo=wo, c_ident=c_ident, c_trilm=c_trilm, c_ones1=c_ones1,
                 c_ones1b=c_ones1b, c_onesd=c_onesd, c_onesdb=c_onesdb,
                 c_e127=c_e127, c_identb=c_identb, lam_sb=lam_sb, lamdb=lamdb, lamE2=lamE2,
                 onw_sb=onw_sb, PSA=PSA, PSB=PSB, PSC=PSC, PSD=PSD,
                 cmask=cmask, cmaskcol=cmaskcol, out=out,
                 w1T=w1T, w2T=w2T, w3T=w3T, g1r=g1r, g2r=g2r, g3r=g3r,
                 XQ1=XQ1, XQ2=XQ2, XQ3=XQ3, HG3=HG3)
        cfg["dumps"] = dbg_out
        if ("p2a" in cfg["phases"] or "p2c" in cfg["phases"]
                or "p2b" in cfg["phases"]):
            emit_p2(kb, cfg, T)
        if "p2d" in cfg["phases"]:
            emit_p2d(kb, cfg, T)
        if {"p3pre", "p3mid", "p3b"} & cfg["phases"]:
            emit_p3(kb, cfg, T)

        for e in ("sp",):
            for gname, cnt in list(kb.cnt.items()):
                if gname in ENGS or cnt == 0:
                    continue
                kb.eng("sp").wait_ge(kb.sem[gname], cnt)

    return nc, kb


def make_core_inputs(inp, core, cfg):
    b, qt = core // 4, core % 4
    TOK = 2048
    x = np.ascontiguousarray(inp["L0_attn_x"][b, qt * TOK:(qt + 1) * TOK, :], )
    ii = np.arange(128)
    m = {
        "xin": x.astype(np.float32),
        "wq": inp["L0_attn_Wq"], "wk": inp["L0_attn_Wk"], "wv": inp["L0_attn_Wv"],
        "wab": np.ascontiguousarray(np.concatenate(
            [inp["L0_attn_Wa"], inp["L0_attn_Wb"]], 1)),
        "bab": np.concatenate([inp["L0_attn_ba"], inp["L0_attn_bb"]])[None, :],
        "lamT": np.ascontiguousarray(inp["L0_attn_lambda_params"].reshape(16, 128).T),
        "onw": inp["L0_attn_o_norm_weight"][:, None],
        "wo": inp["L0_attn_Wo"],
        "w1T": np.ascontiguousarray(inp["L0_ffn_w1"].T),
        "w2T": np.ascontiguousarray(inp["L0_ffn_w2"].T),
        "w3T": np.ascontiguousarray(inp["L0_ffn_w3"].T),
        "g1r": inp["L0_ffn_g1"][None, :], "g2r": inp["L0_ffn_g2"][None, :],
        "g3r": inp["L0_ffn_g3"][None, :],
        "ident": np.eye(128, dtype=np.float32),
        "identb": np.eye(128).astype(BFnp),
        "triu": (ii[:, None] <= ii[None, :]).astype(np.float32),
        "tril_m": (ii[:, None] <= ii[None, :]).astype(np.float32),
        "onescol": np.ones((1, 128), np.float32),
        "onesd": np.ones((128, 1), np.float32),
        "cmask": np.zeros((8, 8), np.float32),
        "cmaskcol": np.zeros((8, 1), np.float32),
        "e127": np.eye(128, dtype=np.float32)[:, 127:128],
        "onescolb": np.ones((1, 128)).astype(BFnp),
        "onesdb": np.ones((128, 1)).astype(BFnp),
    }
    b_, qt_ = core // 4, core % 4
    preds = [r for r in range(b_ * 4, b_ * 4 + qt_)]
    cmc = np.zeros((8, 1), np.float32)
    cm2 = np.zeros((8, 8), np.float32)
    for r in preds:
        cmc[r, 0] = 1.0
        for rp in range(r + 1, b_ * 4 + qt_):
            cm2[rp, r] = 1.0
    m["cmask"] = cm2
    m["cmaskcol"] = cmc
    return {k: np.ascontiguousarray(v.astype(np.float32) if v.dtype == np.float64
                                    else v) for k, v in m.items()}


def emit_p2(kb, cfg, T):
    nc = kb.nc
    NH, HD, CC = 16, 128, 128
    NCH = cfg.get("n_chunks", 16)
    sbuf, psum = T["sbuf"], T["psum"]
    (KTM, VTM, KFM, VFM, QFM, GTM, BETA, THR, RQ, CIN, COUT, ONORM) = (
        T["KTM"], T["VTM"], T["KFM"], T["VFM"], T["QFM"], T["GTM"], T["BETA"],
        T["THR"], T["RQ"], T["CIN"], T["COUT"], T["ONORM"])
    c_ident, c_trilm, c_ones1, c_ones1b, c_onesd, c_onesdb, c_e127 = (
        T["c_ident"], T["c_trilm"], T["c_ones1"], T["c_ones1b"], T["c_onesd"],
        T["c_onesdb"], T["c_e127"])
    c_identb = T["c_identb"]
    lam_sb, lamE2, lamdb = T["lam_sb"], T["lamE2"], T["lamdb"]
    onw_sb = T["onw_sb"]
    PSA, PSB, PSC, PSD = T["PSA"], T["PSB"], T["PSC"], T["PSD"]
    S_u = (PSA, PSC)
    Q_u = (PSB, PSD)
    BF, BH = T["BF"], T["BH"]

    A0st = [sbuf(f"A0st{h}", [128, 128], F32) for h in range(NH)]
    HKst = [sbuf(f"HKst{h}", [128, 128], F32) for h in range(NH)]
    ktm_sb = BH[1]
    vtm_sb = BH[2]
    kfm_sb = BH[3]
    vfm_sb = BH[4]
    qfm_sb = BF[7]
    Gt = sbuf("Gt", [128, NH], F32)
    Bt = sbuf("Bt", [128, NH], F32)
    Tt = sbuf("Tt", [128, NH], F32)
    Rqt = sbuf("Rqt", [128, NH], F32)
    wcol = sbuf("wcol", [128, NH], F32)
    eGlb = sbuf("eGlb", [128, NH], F32)
    glrow = sbuf("glrow", [1, NH], F32)
    dgacc = sbuf("dgacc", [1, NH], F32)
    kwb = sbuf("kwb", [128, 128], BF16)
    vwb = sbuf("vwb", [128, 128], BF16)
    t16a = sbuf("p2_t16a", [128, NH], F32)
    t16b = sbuf("p2_t16b", [128, NH], F32)
    t16c = sbuf("p2_t16c", [128, NH], F32)
    t16d = sbuf("p2_t16d", [128, NH], F32)

    def glast_wcol(c):
        kb.mm(PSC[:1, 0:16], c_e127[:], Gt[:])
        kb.copy("dve", glrow[:], PSC[:1, 0:16])
        kb.mm(PSC[:, 16:32], c_ones1[:], glrow[:])
        kb.tt("dve", t16a[:], PSC[:, 16:32], Gt[:], ALU.subtract)
        kb.ts("dve", t16a[:], t16a[:], 0.0, ALU.min)
        kb.act(t16b[:], t16a[:], AF.Exp)
        kb.tt("dve", wcol[:], t16b[:], Bt[:], ALU.mult)
        kb.act(eGlb[:], PSC[:, 16:32], AF.Exp)

    def state_update(c):
        for h in range(NH):
            u, hh = h // 8, h % 8
            kb.ts("dve", kwb[:], ktm_sb[:, h * HD:(h + 1) * HD],
                  wcol[:, h:h + 1], ALU.mult)
            kb.ts("dve", vwb[:], vtm_sb[:, h * HD:(h + 1) * HD],
                  wcol[:, h:h + 1], ALU.mult)
            kb.mm(S_u[u][:, hh * 128:(hh + 1) * 128], kwb[:],
                  ktm_sb[:, h * HD:(h + 1) * HD])
            kb.mm(Q_u[u][:, hh * 128:(hh + 1) * 128], vwb[:],
                  ktm_sb[:, h * HD:(h + 1) * HD])
            kb.stt(A0st[h][:], A0st[h][:], eGlb[:, h:h + 1],
                   S_u[u][:, hh * 128:(hh + 1) * 128], ALU.mult, ALU.add)
            kb.stt(HKst[h][:], HKst[h][:], eGlb[:, h:h + 1],
                   Q_u[u][:, hh * 128:(hh + 1) * 128], ALU.mult, ALU.add)

    if "p2a" in cfg["phases"]:
        for h in range(NH):
            kb.memset("dve", A0st[h][:], 0.0)
            kb.memset("dve", HKst[h][:], 0.0)
        kb.memset("dve", dgacc[:], 0.0)
        for c in range(NCH):
            kb.dma(ktm_sb[:], KTM[c], grp="p2k")
            kb.dma(vtm_sb[:], VTM[c], grp="p2v")
            kb.dma(Gt[:], GTM[c], grp="p2g")
            kb.dma(Bt[:], BETA[c], grp="p2g")
            glast_wcol(c)
            kb.tt("dve", dgacc[:], dgacc[:], glrow[:], ALU.add)
            state_update(c)
        for h in range(NH):
            kb.dma(CIN[:, h * 128:(h + 1) * 128], A0st[h][:], grp="cinw")
            kb.dma(CIN[:, 2048 + h * 128:2048 + (h + 1) * 128], HKst[h][:],
                   grp="cinw")
        kb.memset("dve", t16d[:], 0.0)
        kb.copy("dve", t16d[0:1, :], dgacc[:])
        kb.dma(CIN[:, 4096:4112], t16d[:], grp="cinw")
        kb.op("pool", lambda g: g.collective_compute(
            "AllGather", ALU.bypass, replica_groups=[list(range(8))],
            ins=[CIN], outs=[COUT]), reads=[CIN], writes=[COUT])

    if "p2b" in cfg["phases"]:
        cmsk = sbuf("cmsk", [8, 8], F32)
        cmcol = sbuf("cmcol", [8, 1], F32)
        dgall = sbuf("dgall", [8, NH], F32)
        wt8 = sbuf("wt8", [8, NH], F32)
        wflat = sbuf("wflat", [1, 128], F32)
        wBs = sbuf("wBs", [128, 128], F32)
        gsbA = BF[0]
        gsbB = BF[1]
        kb.dma(cmsk[:], T["cmask"])
        kb.dma(cmcol[:], T["cmaskcol"])
        for r in range(8):
            kb.dma(dgall[r:r + 1, :], COUT[r * 128:r * 128 + 1, 4096:4112],
                   grp="p2b")
        kb.mm(PSC[:8, 0:16], cmsk[:], dgall[:])
        kb.act(wt8[:], PSC[:8, 0:16], AF.Exp)
        kb.ts("dve", wt8[:], wt8[:], cmcol[:, 0:1], ALU.mult)
        kb.dma(wflat[:], wt8[:], grp="p2b")
        kb.mm(PSC[:, 32:160], c_ones1[:], wflat[:])
        kb.copy("dve", wBs[:], PSC[:, 32:160])
        for h in range(NH):
            kb.memset("dve", A0st[h][:], 0.0)
            kb.memset("dve", HKst[h][:], 0.0)
        for r in range(8):
            kb.dma(gsbA[:], COUT[r * 128:(r + 1) * 128, 0:2048], grp="p2ga")
            kb.dma(gsbB[:], COUT[r * 128:(r + 1) * 128, 2048:4096], grp="p2gb")
            for h in range(NH):
                kb.stt(A0st[h][:], gsbA[:, h * 128:(h + 1) * 128],
                       wBs[:, r * 16 + h:r * 16 + h + 1], A0st[h][:],
                       ALU.mult, ALU.add)
                kb.stt(HKst[h][:], gsbB[:, h * 128:(h + 1) * 128],
                       wBs[:, r * 16 + h:r * 16 + h + 1], HKst[h][:],
                       ALU.mult, ALU.add)

    if "p2c" in cfg["phases"]:
        Xcg = BF[0]
        Rcg = BF[1]
        Pcg = BF[2]
        Pb = BH[5]
        Psc = BH[6]
        swb = BH[7]
        pqb = BH[8]
        rrb = BH[9]
        wm = BH[10]
        k2tm = BH[11]
        A0p = BH[12]
        hkvb = BH[13]
        expGb = BF[3]
        rqb = BF[4]
        egrqb = BF[5]
        ab_sb = BH[14]
        dn_sb = BF[6]
        onorm_sb = BH[15]
        gtr2 = sbuf("gtr2", [128, 32], F32)
        tr32 = sbuf("tr32", [32, 128], F32)
        rows32 = sbuf("rows32", [1, 4096], F32)
        T["rows32"] = rows32
        dgtile = sbuf("dgtile", [128, 128], F32)
        bmtile = sbuf("bmtile", [128, 128], F32)
        wexp = sbuf("wexp", [128, 128], F32)
        datile = sbuf("datile", [128, 128], F32)
        dsc = sbuf("dsc", [128, NH], F32)
        dnsc = sbuf("dnsc", [128, NH], F32)
        pqsc = sbuf("pqsc", [128, NH], F32)
        donec = sbuf("donec", [128, NH], F32)
        ndone = sbuf("ndone", [128, NH], F32)
        alsc = sbuf("alsc", [128, NH], F32)
        besc = sbuf("besc", [128, NH], F32)
        s16a = sbuf("s16a", [128, NH], F32)
        s16b = sbuf("s16b", [128, NH], F32)
        abT = sbuf("abT", [16, 128], BF16)
        abrow2 = sbuf("abrow2", [1, 2048], BF16)

        def U(t, u):
            return (t[:, u * 1024:(u + 1) * 1024], u)

        def Uh(t, u, hh, w=128):
            return (t[:, u * 1024 + hh * w:u * 1024 + (hh + 1) * w], u)

        def Us(t, u):
            return (t[:, u * 8:(u + 1) * 8], u)

        def Ush(t, u, hh):
            return (t[:, u * 8 + hh:u * 8 + hh + 1], u)

        def mv_unit(u, Pb_src, Psc_src):
            for hh in range(8):
                h = u * 8 + hh
                kb.mm((S_u[u][:, hh * 128:(hh + 1) * 128], u),
                      Uh(kfm_sb, u, hh), Uh(Pb_src, u, hh))
            kb.tt("dve", U(swb, u), U(wm, u), (S_u[u][:], u), ALU.mult)
            for hh in range(8):
                h = u * 8 + hh
                kb.mm((Q_u[u][:, hh * 128:(hh + 1) * 128], u),
                      Uh(ktm_sb, u, hh), Uh(swb, u, hh),
                      start=True, stop=False)
                kb.mm((Q_u[u][:, hh * 128:(hh + 1) * 128], u),
                      Uh(A0p, u, hh), Uh(Psc_src, u, hh),
                      start=False, stop=False)
                kb.mm((Q_u[u][:, hh * 128:(hh + 1) * 128], u),
                      lamdb[h][:], Uh(Pb_src, u, hh),
                      start=False, stop=True)

        def colreduce_u(u, src, lo):
            for hh in range(8):
                kb.mm((S_u[u][:, lo + hh:lo + hh + 1], u),
                      Uh(src, u, hh), c_onesdb[:])


        NIT = cfg.get("n_iter", 30)
        dumps = cfg.get("dumps", {})
        def dump(name, ap):
            if name in dumps:
                kb.dma(dumps[name], ap, q="pool", grp="dmp")
        if "p2b" not in cfg["phases"]:
            for h in range(NH):
                kb.memset("dve", A0st[h][:], 0.0)
                kb.memset("dve", HKst[h][:], 0.0)
        for c in range(NCH):
            kb.dma(ktm_sb[:], KTM[c], grp="p2k")
            kb.dma(vtm_sb[:], VTM[c], grp="p2v")
            kfm_ap = KFM[c].rearrange("h d t -> d h t")
            vfm_ap = VFM[c].rearrange("h d t -> d h t")
            qfm_ap = QFM[c].rearrange("h d t -> d h t")
            kb.dma(kfm_sb[:], kfm_ap, grp="p2k")
            kb.dma(vfm_sb[:], vfm_ap, grp="p2v")
            kb.dma(qfm_sb[:], qfm_ap, grp="p2q")
            kb.dma(Gt[:], GTM[c], grp="p2g")
            kb.dma(Bt[:], BETA[c], grp="p2g")
            kb.dma(Tt[:], THR[c], grp="p2g")
            kb.dma(Rqt[:], RQ[c], grp="p2g")

            kb.copy("dve", gtr2[:, 0:16], Gt[:])
            kb.copy("dve", gtr2[:, 16:32], Rqt[:])
            kb.trans(PSC[:32, 128:256], gtr2[:], c_ident[:])
            kb.copy("dve", tr32[:], PSC[:32, 128:256])
            kb.dma(rows32[:], tr32[:], grp="fl0")
            for h in range(NH):
                u, hh = h // 8, h % 8
                kb.mm((S_u[u][:, 384:512], u), c_ones1[:],
                      rows32[:1, h * 128:(h + 1) * 128])
                kb.ts("dve", dgtile[:], (S_u[u][:, 384:512], u), Gt[:, h:h + 1],
                      ALU.subtract, 0.0, ALU.min)
                kb.act(wexp[:], dgtile[:], AF.Exp)
                kb.ts("dve", bmtile[:], c_trilm[:], Bt[:, h:h + 1], ALU.mult)
                kb.tt("dve", Uh(wm, u, hh), wexp[:], bmtile[:], ALU.mult)
            for u in range(2):
                for qq in range(2):
                    kb.mm((S_u[u][:, qq * 512:(qq + 1) * 512], u), c_ones1[:],
                          rows32[:1, 2048 + u * 1024 + qq * 512:
                                 2048 + u * 1024 + (qq + 1) * 512])
                kb.copy("dve", U(rqb, u), (S_u[u][:], u))
            kb.act(rows32[:1, 0:2048], rows32[:1, 0:2048], AF.Exp)
            for u in range(2):
                for qq in range(2):
                    kb.mm((S_u[u][:, qq * 512:(qq + 1) * 512], u), c_ones1[:],
                          rows32[:1, u * 1024 + qq * 512:
                                 u * 1024 + (qq + 1) * 512])
                kb.copy("dve", U(expGb, u), (S_u[u][:], u))
                kb.tt("pool", U(egrqb, u), U(expGb, u), U(rqb, u), ALU.mult)
            kb.act(k2tm[:], ktm_sb[:], AF.Square)
            for h in range(NH):
                u, hh = h // 8, h % 8
                kb.copy("act", Uh(A0p, u, hh), A0st[h][:])
                kb.copy("act", Uh(hkvb, u, hh), HKst[h][:])
            for h in range(NH):
                u, hh = h // 8, h % 8
                kb.mm((Q_u[u][:, hh * 128:(hh + 1) * 128], u),
                      Uh(k2tm, u, hh), Uh(wm, u, hh), start=True, stop=False)
                kb.tt("dve", datile[:], A0st[h][:], c_ident[:], ALU.mult)
                kb.mm((S_u[u][:1, 0:128], u), c_onesd[:], datile[:])
                kb.copy("dve", (rows32[:1, 2048 + h * 128:2048 + (h + 1) * 128],
                                u), (S_u[u][:1, 0:128], u))
                kb.mm((Q_u[u][:, hh * 128:(hh + 1) * 128], u),
                      (rows32[:1, 2048 + h * 128:2048 + (h + 1) * 128], u),
                      (rows32[:1, h * 128:(h + 1) * 128], "ro"),
                      start=False, stop=True)
            for h in range(NH):
                u, hh = h // 8, h % 8
                kb.act(Uh(dn_sb, u, hh), (Q_u[u][:, hh * 128:(hh + 1) * 128], u),
                       AF.Ln, bias=lamE2[:, h:h + 1])
            for u in range(2):
                kb.act(U(dn_sb, u), U(dn_sb, u), AF.Exp, scale=-1.0)
                kb.tt("dve", U(Xcg, u), U(qfm_sb, u), U(dn_sb, u), ALU.mult)
                kb.copy("act", U(Pb, u), U(Xcg, u))
                kb.tt("pool", U(Psc, u), U(Xcg, u), U(expGb, u), ALU.mult)
            for u in range(2):
                mv_unit(u, Pb, Psc)
            for u in range(2):
                kb.tt("dve", U(Rcg, u), U(qfm_sb, u), (Q_u[u][:], u),
                      ALU.subtract)
                kb.copy("dve", U(Pcg, u), U(Rcg, u))
                kb.act(U(rrb, u), U(Rcg, u), AF.Square)
                colreduce_u(u, rrb, 0)
                kb.copy("dve", Us(dsc, u), (S_u[u][:, 0:8], u))
                kb.memset("dve", Us(donec, u), 0.0)
            if c == 0:
                dump("X0", Xcg[:])

            def route(src16, evac=True):
                kb.trans((S_u[0][0:16, 128:256], 0), src16, c_ident[:])
                kb.copy("dve", abT[:], (S_u[0][0:16, 128:256], 0))
                kb.dma(abrow2[:], abT[:], grp="fl0")
                for u in range(2):
                    kb.mm((S_u[u][:, 0:512], u), c_ones1b[:],
                          abrow2[:1, u * 1024:u * 1024 + 512])
                    kb.mm((S_u[u][:, 512:1024], u), c_ones1b[:],
                          abrow2[:1, u * 1024 + 512:(u + 1) * 1024])
                    if evac:
                        kb.copy("act", U(ab_sb, u), (S_u[u][:], u))

            def seg1(u):
                kb.copy("act", U(Pb, u), U(Pcg, u))
                kb.tt("pool", U(Psc, u), U(Pcg, u), U(expGb, u), ALU.mult)
                mv_unit(u, Pb, Psc)
                kb.tt("dve", U(pqb, u), U(Pcg, u), (Q_u[u][:], u), ALU.mult)
                colreduce_u(u, pqb, 0)
                kb.copy("dve", Us(pqsc, u), (S_u[u][:, 0:8], u))

            def smalla():
                kb.tt("dve", s16a[:], dsc[:], Tt[:], ALU.is_lt)
                kb.tt("dve", donec[:], donec[:], s16a[:], ALU.max)
                kb.act(s16a[:], pqsc[:], AF.Abs)
                kb.tt("dve", s16a[:], s16a[:], Tt[:], ALU.is_lt)
                kb.tt("dve", donec[:], donec[:], s16a[:], ALU.max)
                kb.ts("dve", ndone[:], donec[:], -1.0, ALU.mult, 1.0, ALU.add)
                kb.tt("dve", s16a[:], pqsc[:], ndone[:], ALU.mult)
                kb.tt("dve", s16a[:], s16a[:], donec[:], ALU.add)
                kb.op("dve", lambda g: g.reciprocal(out=s16b[:], in_=s16a[:]),
                      reads=[s16a], writes=[s16b])
                kb.tt("dve", alsc[:], dsc[:], s16b[:], ALU.mult)
                kb.tt("dve", alsc[:], alsc[:], ndone[:], ALU.mult)

            def upd(u):
                kb.tt("pool", U(pqb, u), U(ab_sb, u), U(Pcg, u), ALU.mult)
                kb.tt("pool", U(Xcg, u), U(Xcg, u), U(pqb, u), ALU.add)
                kb.tt("dve", U(rrb, u), U(ab_sb, u), (Q_u[u][:], u), ALU.mult)
                kb.tt("dve", U(Rcg, u), U(Rcg, u), U(rrb, u), ALU.subtract)
                kb.act(U(rrb, u), U(Rcg, u), AF.Square)
                colreduce_u(u, rrb, 8)
                kb.copy("dve", Us(dnsc, u), (S_u[u][:, 8:16], u))

            def smallb():
                kb.tt("dve", s16a[:], dnsc[:], dsc[:], ALU.subtract)
                kb.tt("dve", s16a[:], s16a[:], ndone[:], ALU.mult)
                kb.tt("dve", dnsc[:], dsc[:], s16a[:], ALU.add)
                kb.tt("dve", s16a[:], dsc[:], ndone[:], ALU.mult)
                kb.tt("dve", s16a[:], s16a[:], donec[:], ALU.add)
                kb.op("dve", lambda g: g.reciprocal(out=s16b[:], in_=s16a[:]),
                      reads=[s16a], writes=[s16b])
                kb.tt("dve", besc[:], dnsc[:], s16b[:], ALU.mult)
                kb.tt("dve", besc[:], besc[:], ndone[:], ALU.mult)
                kb.copy("dve", dsc[:], dnsc[:])

            def pupd(u):
                kb.tt("dve", U(pqb, u), U(Pcg, u), (S_u[u][:], u), ALU.mult)
                kb.tt("pool", U(Pcg, u), U(Rcg, u), U(pqb, u), ALU.add)

            for it in range(NIT):
                seg1(0)
                seg1(1)
                smalla()
                route(alsc[:])
                upd(0)
                upd(1)
                smallb()
                route(besc[:], evac=False)
                pupd(0)
                pupd(1)

            for u in range(2):
                kb.tt("dve", U(pqb, u), U(Xcg, u), U(rqb, u), ALU.mult)
                kb.tt("pool", U(rrb, u), U(Xcg, u), U(egrqb, u), ALU.mult)
                for hh in range(8):
                    kb.mm((S_u[u][:, hh * 128:(hh + 1) * 128], u),
                          Uh(vfm_sb, u, hh), Uh(pqb, u, hh))
                kb.tt("dve", U(swb, u), U(wm, u), (S_u[u][:], u), ALU.mult)
                for hh in range(8):
                    h = u * 8 + hh
                    kb.mm((Q_u[u][:, hh * 128:(hh + 1) * 128], u),
                          Uh(ktm_sb, u, hh), Uh(swb, u, hh),
                          start=True, stop=False)
                    kb.mm((Q_u[u][:, hh * 128:(hh + 1) * 128], u),
                          Uh(hkvb, u, hh), Uh(rrb, u, hh),
                          start=False, stop=True)
                kb.act(U(pqb, u), (Q_u[u][:], u), AF.Square)
                colreduce_u(u, pqb, 0)
                kb.ts("dve", Us(s16a, u), (S_u[u][:, 0:8], u), 1.0 / 128,
                      ALU.mult, 1e-6, ALU.add)
                kb.act(Us(s16a, u), Us(s16a, u), AF.Ln)
                kb.act(Us(s16a, u), Us(s16a, u), AF.Exp, scale=-0.5)
                kb.trans((S_u[u][0:8, 128:256], u), Us(s16a, u), c_ident[:])
                kb.copy("dve", abT[0:8, :], (S_u[u][0:8, 128:256], u))
                kb.dma(abrow2[:1, 0:1024], abT[0:8, :], grp="fl0")
                kb.mm((S_u[u][:, 0:512], u), c_ones1b[:], abrow2[:1, 0:512])
                kb.mm((S_u[u][:, 512:1024], u), c_ones1b[:],
                      abrow2[:1, 512:1024])
                kb.copy("act", U(ab_sb, u), (S_u[u][:], u))
                kb.stt(U(onorm_sb, u), (Q_u[u][:], u), onw_sb[:, 0:1],
                       U(ab_sb, u), ALU.mult, ALU.mult)
                kb.dma(ONORM[c][:, u * 1024:(u + 1) * 1024],
                       U(onorm_sb, u), grp=f"son{u}")
            glast_wcol(c)
            state_update(c)


def emit_p2d(kb, cfg, T):
    nc = kb.nc
    BF, BH = T["BF"], T["BH"]
    PSA, PSB = T["PSA"], T["PSB"]
    pd4 = [PSA[:, 0:512], PSA[:, 512:1024], PSB[:, 0:512], PSB[:, 512:1024]]
    ONORM, ATTNO, xin, wo = T["ONORM"], T["ATTNO"], T["xin"], T["wo"]
    sbuf = T["sbuf"]
    wod = [sbuf(f"wod{i}", [128, 512], F32) for i in range(2)]
    wob = [sbuf(f"wob{i}", [128, 512], BF16) for i in range(2)]
    NCH = cfg.get("n_chunks", 16)
    for tg in range(4):
        chs = [tg * 4 + i for i in range(4) if tg * 4 + i < NCH]
        if not chs:
            continue
        for i, ch in enumerate(chs):
            kb.dma(BH[1 + i][:], ONORM[ch], grp="p2d_on")
        for cq in range(4):
            for ct in range(16):
                b = ct % 2
                kb.dma(wod[b][:], wo[ct * 128:(ct + 1) * 128,
                                     cq * 512:(cq + 1) * 512], grp=f"p2dw{b}")
                kb.copy("dve", wob[b][:], wod[b][:])
                for i, ch in enumerate(chs):
                    kb.mm(pd4[i],
                          BH[1 + i][:, ct * 128:(ct + 1) * 128], wob[b][:],
                          start=(ct == 0), stop=(ct == 15))
            for i, ch in enumerate(chs):
                kb.dma(BF[4 + i][:, 0:512],
                       xin[ch * 128:(ch + 1) * 128, cq * 512:(cq + 1) * 512],
                       grp="p2dx")
                kb.tt("dve", BF[4 + i][:, 512:1024], pd4[i],
                      BF[4 + i][:, 0:512], ALU.add)
                kb.dma(ATTNO[ch * 128:(ch + 1) * 128, cq * 512:(cq + 1) * 512],
                       BF[4 + i][:, 512:1024], q="act", grp="p2do")


def emit_p3(kb, cfg, T):
    nc = kb.nc
    BF, BH = T["BF"], T["BH"]
    PSA, PSB, PSC, PSD = T["PSA"], T["PSB"], T["PSC"], T["PSD"]
    sbuf = T["sbuf"]
    XN_TM, ATTNO, out = T["XN_TM"], T["ATTNO"], T["out"]
    w1T, w2T, w3T = T["w1T"], T["w2T"], T["w3T"]
    g1r, g2r, g3r = T["g1r"], T["g2r"], T["g3r"]
    XQ1, XQ2, XQ3, HG3 = T["XQ1"], T["XQ2"], T["XQ3"], T["HG3"]
    c_ones1, c_ones1b, c_onesdb, c_ident, c_identb = (
        T["c_ones1"], T["c_ones1b"], T["c_onesdb"], T["c_ident"], T["c_identb"])
    c_onesd = T["c_onesd"]
    zi32 = sbuf("zi32", [128, 2048], I32)
    col1 = sbuf("p3_col1", [128, 1], F32)
    col2 = sbuf("p3_col2", [128, 1], F32)
    rstd = sbuf("p3_rstd", [128, 1], F32)
    inv1 = sbuf("p3_inv1", [128, 16], F32)
    inv2 = sbuf("p3_inv2", [128, 16], F32)
    h2st = sbuf("p3_h2st", [128, 16], F32)
    hmx = sbuf("p3_hmx", [128, 16], F32)
    sc3 = sbuf("p3_sc3", [128, 16], F32)
    iv3 = sbuf("p3_iv3", [128, 16], F32)
    srow = sbuf("p3_srow", [1, 8], F32)
    acol = sbuf("p3_acol", [128, 4], F32)
    thrc = sbuf("p3_thrc", [128, 4], F32)
    NCH = 16

    for wi, (wT, nblk) in enumerate(((w1T, 64), (w2T, 64), (w3T, 64))):
        wvT = wT.rearrange("a b -> (a b)").rearrange("(n p f) -> n p f",
                                                     p=128, f=2048)
        s16 = sbuf("p3_s16", [1, 16], F32) if wi == 0 else s16
        for blk in range(nblk):
            b = blk % 2
            kb.dma(BF[b][:], wvT[blk], grp=f"p3s{b}")
            kb.act(BF[2 + b][:], BF[b][:], AF.Abs)
            tgts = (PSA[:1, 0:512], PSA[:1, 512:1024],
                    PSB[:1, 0:512], PSB[:1, 512:1024])
            for q4 in range(4):
                kb.mm(tgts[q4], c_onesd[:], BF[2 + b][:, q4 * 512:(q4 + 1) * 512],
                      start=(blk == 0), stop=(blk == nblk - 1))
        for q4, tgt in enumerate((PSA[:1, 0:512], PSA[:1, 512:1024],
                                  PSB[:1, 0:512], PSB[:1, 512:1024])):
            kb.red("dve", s16[:1, q4 * 4:(q4 + 1) * 4],
                   tgt.rearrange("a (g x) -> a g x", x=128), ALU.add)
        kb.red("dve", srow[:1, wi:wi + 1], s16[:1, :], ALU.add)
    kb.ts("dve", srow[:1, 0:3], srow[:1, 0:3], 1.0 / 16777216.0, ALU.mult)
    kb.ts("dve", srow[:1, 3:6], srow[:1, 0:3], -1.0, ALU.mult, 1.0, ALU.add)
    kb.op("dve", lambda g: g.reciprocal(out=srow[:1, 3:6], in_=srow[:1, 3:6]),
          reads=[srow], writes=[srow])
    kb.ts("dve", srow[:1, 6:8], srow[:1, 0:2], 1.0, ALU.add)
    kb.tt("dve", srow[:1, 6:8], srow[:1, 6:8], srow[:1, 3:5], ALU.mult)
    kb.act(srow[:1, 6:8], srow[:1, 6:8], AF.Ln)
    kb.ts("dve", srow[:1, 6:8], srow[:1, 6:8], 0.5, ALU.mult)
    kb.ts("dve", srow[:1, 5:6], srow[:1, 2:3], 1.0, ALU.add)
    kb.tt("dve", srow[:1, 5:6], srow[:1, 5:6], srow[:1, 3:4].rearrange("a b -> a b"),
          ALU.mult) if False else None
    kb.ts("dve", srow[:1, 3:4], srow[:1, 2:3], -1.0, ALU.mult, 1.0, ALU.add)
    kb.op("dve", lambda g: g.reciprocal(out=srow[:1, 3:4], in_=srow[:1, 3:4]),
          reads=[srow], writes=[srow])
    kb.tt("dve", srow[:1, 5:6], srow[:1, 5:6], srow[:1, 3:4], ALU.mult)
    kb.act(srow[:1, 5:6], srow[:1, 5:6], AF.Ln)
    kb.ts("dve", srow[:1, 5:6], srow[:1, 5:6], 0.5, ALU.mult)
    kb.ts("dve", srow[:1, 0:3], srow[:1, 0:3], 1e-8, ALU.add,
          0.5493061443340549, ALU.mult)
    kb.mm(PSC[:, 0:8], c_ones1[:], srow[:1, 0:8])
    kb.copy("dve", thrc[:, 0:3], PSC[:, 0:3])
    kb.copy("dve", acol[:, 0:3], PSC[:, 5:8])

    def round_chain(y_f32, out_f32):
        kb.copy("dve", zi32[:], y_f32)
        kb.stt(BF[5][:], zi32[:], -1.0, y_f32, ALU.mult, ALU.add)
        kb.ts("dve", BH[14][:], BF[5][:], 0.5, ALU.is_gt)
        kb.ts("dve", BH[15][:], BF[5][:], -0.5, ALU.is_lt)
        kb.tt("pool", BF[5][:], BH[14][:], zi32[:], ALU.add)
        kb.tt("dve", out_f32, BF[5][:], BH[15][:], ALU.subtract)

    if "p3pre" in cfg["phases"]:
        for gi, grow in enumerate((g1r, g2r)):
            kb.dma(srow[:1, 0:1], grow[:1, 0:1], grp="p3g") if False else None
        g1t, g2t = BF[2], BF[3]
        grows = T["rows32"]
        for gi, (grow, gt) in enumerate(((g1r, g1t), (g2r, g2t))):
            kb.dma(grows[:1, 0:2048], grow, grp="p3g")
            for qq in range(2):
                kb.mm(PSA[:, qq * 512:(qq + 1) * 512], c_ones1[:],
                      grows[:1, qq * 512:(qq + 1) * 512])
                kb.mm(PSB[:, qq * 512:(qq + 1) * 512], c_ones1[:],
                      grows[:1, 1024 + qq * 512:1024 + (qq + 1) * 512])
            kb.copy("dve", gt[:, 0:1024], PSA[:])
            kb.copy("dve", gt[:, 1024:2048], PSB[:])
        for tb in range(NCH):
            kb.dma(BF[0][:], XN_TM[tb * 128:(tb + 1) * 128, :], grp="p3x")
            kb.act(BF[1][:], BF[0][:], AF.Square)
            kb.red("dve", col1[:], BF[1][:], ALU.add)
            kb.ts("dve", col1[:], col1[:], 1.0 / 2048, ALU.mult, 1e-8, ALU.add)
            kb.act(rstd[:], col1[:], AF.Ln)
            kb.act(rstd[:], rstd[:], AF.Exp, scale=-0.5)
            for gi, (gt, XQ, invv) in enumerate(((g1t, XQ1, inv1),
                                                 (g2t, XQ2, inv2))):
                kb.stt(BF[1][:], BF[0][:], rstd[:, 0:1], gt[:], ALU.mult,
                       ALU.mult)
                kb.red("dve", col1[:], BF[1][:], ALU.max, absval=True)
                kb.ts("dve", col1[:], col1[:], 1e-4, ALU.max)
                kb.op("dve", lambda g: g.reciprocal(out=col2[:], in_=col1[:]),
                      reads=[col1], writes=[col2])
                kb.ts("dve", invv[:, tb:tb + 1], col1[:], 1.0 / 127, ALU.mult)
                kb.ts("dve", col2[:], col2[:], 127.0, ALU.mult)
                kb.act(BF[4][:], BF[1][:], AF.Copy, scale=col2[:, 0:1])
                round_chain(BF[4][:], BF[1][:])
                for ct in range(16):
                    pb = ct % 2
                    kb.trans(PSC[:, 512 + pb * 128:512 + (pb + 1) * 128],
                             BF[1][:, ct * 128:(ct + 1) * 128], c_ident[:])
                    kb.copy("act", BH[12][:, ct * 128:(ct + 1) * 128],
                            PSC[:, 512 + pb * 128:512 + (pb + 1) * 128])
                kb.dma(XQ[:, tb * 128:(tb + 1) * 128].rearrange(
                           "(c p) t -> p c t", p=128),
                       BH[12][:], q="act", grp="p3q")

    if "p3mid" in cfg["phases"]:
        g3t = BF[6]
        grow3 = T["rows32"]
        kb.memset("dve", h2st[:], 0.0)
        kb.memset("dve", hmx[:], 0.0)
        for hb in range(16):
            kb.dma(grow3[:1, 0:512], g3r[0:1, hb * 512:(hb + 1) * 512], grp="p3g")
            kb.mm(PSC[:, 0:512], c_ones1[:], grow3[:1, 0:512])
            kb.copy("dve", g3t[:, 0:512], PSC[:, 0:512])
            for wi, (wT, wst) in enumerate(((w1T, 0), (w2T, 1))):
                for ct in range(16):
                    kb.dma(BF[0][:, 0:512], wT[ct * 128:(ct + 1) * 128,
                                               hb * 512:(hb + 1) * 512],
                           grp=f"p3w{wi}")
                    kb.ts("dve", BF[1][:, 0:512], BF[0][:, 0:512],
                          thrc[:, wi:wi + 1], ALU.is_gt)
                    kb.ts("dve", BF[1][:, 512:1024], BF[0][:, 0:512],
                          thrc[:, wi:wi + 1], ALU.is_lt) if False else None
                    kb.ts("pool", BF[4][:, 0:512], BF[0][:, 0:512],
                          0.0, ALU.subtract) if False else None
                    kb.ts("dve", BF[4][:, 0:512], BF[0][:, 0:512], -1.0, ALU.mult)
                    kb.ts("dve", BF[4][:, 512:1024], BF[4][:, 0:512],
                          thrc[:, wi:wi + 1], ALU.is_gt)
                    kb.tt("dve", BH[wst * 4 + ct // 4][:,
                          (ct % 4) * 512:(ct % 4 + 1) * 512],
                          BF[1][:, 0:512], BF[4][:, 512:1024], ALU.subtract)
            def ld_stripe(tb, sl):
                kb.dma(BH[8 + sl][:], XQ1[:, tb * 128:(tb + 1) * 128].rearrange(
                    "(c p) t -> p c t", p=128), grp=f"p3xs{sl}")
                kb.dma(BH[10 + sl][:], XQ2[:, tb * 128:(tb + 1) * 128].rearrange(
                    "(c p) t -> p c t", p=128), grp=f"p3xs{sl}")
            ld_stripe(0, 0)
            for tb in range(NCH):
                sl = tb % 2
                if tb + 1 < NCH:
                    ld_stripe(tb + 1, (tb + 1) % 2)
                for ct in range(16):
                    kb.mm(PSA[:, 0:512], BH[8 + sl][:, ct * 128:(ct + 1) * 128],
                          BH[0 + ct // 4][:, (ct % 4) * 512:(ct % 4 + 1) * 512],
                          start=(ct == 0), stop=(ct == 15))
                    kb.mm(PSB[:, 0:512], BH[10 + sl][:, ct * 128:(ct + 1) * 128],
                          BH[4 + ct // 4][:, (ct % 4) * 512:(ct % 4 + 1) * 512],
                          start=(ct == 0), stop=(ct == 15))
                kb.ts("dve", col1[:], inv1[:, tb:tb + 1], 1.0, ALU.mult)
                kb.tt("dve", col1[:], col1[:], acol[:, 1:2], ALU.mult)
                kb.ts("dve", col2[:], inv2[:, tb:tb + 1], 1.0, ALU.mult)
                kb.tt("dve", col2[:], col2[:], acol[:, 2:3], ALU.mult)
                kb.act(BF[0][:, 0:512], PSA[:, 0:512], AF.Silu,
                       scale=col1[:, 0:1])
                kb.act(BF[0][:, 512:1024], PSB[:, 0:512], AF.Copy,
                       scale=col2[:, 0:1])
                kb.tt("dve", BF[1][:, 0:512], BF[0][:, 0:512],
                      BF[0][:, 512:1024], ALU.mult)
                kb.tt("dve", BH[13][:, 0:512], BF[1][:, 0:512], g3t[:, 0:512],
                      ALU.mult)
                kb.dma(HG3[tb * 128:(tb + 1) * 128, hb * 512:(hb + 1) * 512],
                       BH[13][:, 0:512], q="act", grp="p3h")
                kb.act(BF[4][:, 0:512], BF[1][:, 0:512], AF.Square,
                       accum=col1[:])
                kb.tt("dve", h2st[:, tb:tb + 1], h2st[:, tb:tb + 1], col1[:],
                      ALU.add)
                kb.red("dve", col2[:], BH[13][:, 0:512], ALU.max, absval=True)
                kb.tt("dve", hmx[:, tb:tb + 1], hmx[:, tb:tb + 1], col2[:],
                      ALU.max)

    if "p3b" in cfg["phases"]:
        kb.ts("dve", h2st[:], h2st[:], 1.0 / 8192, ALU.mult, 1e-8, ALU.add)
        kb.act(h2st[:], h2st[:], AF.Ln)
        kb.act(h2st[:], h2st[:], AF.Exp, scale=-0.5)
        kb.tt("dve", hmx[:], hmx[:], h2st[:], ALU.mult)
        kb.ts("dve", hmx[:], hmx[:], 1e-4, ALU.max)
        kb.tt("dve", iv3[:], hmx[:], acol[:, 0:1].rearrange("a b -> a b")
              if False else hmx[:], ALU.bypass) if False else None
        kb.op("dve", lambda g: g.reciprocal(out=sc3[:], in_=hmx[:]),
              reads=[hmx], writes=[sc3])
        kb.ts("dve", sc3[:], sc3[:], 127.0, ALU.mult)
        kb.tt("dve", sc3[:], sc3[:], h2st[:], ALU.mult)
        kb.ts("dve", iv3[:], hmx[:], 1.0 / 127, ALU.mult)
        for i in range(16):
            kb.tt("dve", iv3[:, i:i + 1], iv3[:, i:i + 1], acol[:, 0:1], ALU.mult)
        for tb in range(NCH):
            for hq in range(4):
                kb.dma(BH[8][:], HG3[tb * 128:(tb + 1) * 128,
                                     hq * 2048:(hq + 1) * 2048], grp="p3h2")
                kb.act(BF[1][:], BH[8][:], AF.Copy, scale=sc3[:, tb:tb + 1])
                round_chain(BF[1][:], BF[4][:])
                for ct in range(16):
                    pb = ct % 2
                    kb.trans(PSC[:, 512 + pb * 128:512 + (pb + 1) * 128],
                             BF[4][:, ct * 128:(ct + 1) * 128], c_ident[:])
                    kb.copy("act", BH[12][:, ct * 128:(ct + 1) * 128],
                            PSC[:, 512 + pb * 128:512 + (pb + 1) * 128])
                kb.dma(XQ3[hq * 2048:(hq + 1) * 2048,
                           tb * 128:(tb + 1) * 128].rearrange(
                           "(c p) t -> p c t", p=128),
                       BH[12][:], q="act", grp="p3q3")
        for c8 in range(8):
            for hh in range(16):
                for sub in range(4):
                    r0 = hh * 512 + sub * 128
                    kb.dma(BF[0][:, 0:256], w3T[r0:r0 + 128,
                                                c8 * 256:(c8 + 1) * 256],
                           grp="p3w3")
                    kb.ts("dve", BF[1][:, 0:256], BF[0][:, 0:256],
                          thrc[:, 2:3], ALU.is_gt)
                    kb.ts("dve", BF[4][:, 0:256], BF[0][:, 0:256], -1.0, ALU.mult)
                    kb.ts("dve", BF[4][:, 256:512], BF[4][:, 0:256],
                          thrc[:, 2:3], ALU.is_gt)
                    kb.tt("dve", BH[(hh % 4) * 2 + sub // 2][:,
                          (sub % 2) * 256 + ((hh // 4) % 2) * 512:
                          (sub % 2) * 256 + ((hh // 4) % 2) * 512 + 256],
                          BF[1][:, 0:256], BF[4][:, 256:512], ALU.subtract) \
                        if False else None
                    kb.tt("dve", BH[hh // 2][:, (hh % 2) * 1024 + sub * 256:
                                              (hh % 2) * 1024 + (sub + 1) * 256],
                          BF[1][:, 0:256], BF[4][:, 256:512], ALU.subtract)
            def ld3(tb, base):
                for hq in range(4):
                    kb.dma(BH[base + hq][:, 0:2048], XQ3[
                        hq * 2048:(hq + 1) * 2048,
                        tb * 128:(tb + 1) * 128].rearrange(
                        "(c p) t -> p c t", p=128), grp=f"p3xs{base}")
            if c8 == 0:
                ld3(0, 8)
            for tb in range(NCH):
                base = 8 + 4 * (tb % 2)
                nxt = tb + 1 if tb + 1 < NCH else (0 if c8 + 1 < 8 else None)
                if nxt is not None:
                    ld3(nxt, 8 + 4 * ((tb + 1) % 2))
                for hh in range(16):
                    for sub in range(4):
                        hidx = hh * 4 + sub
                        kb.mm(PSD[:, 0:256],
                              BH[base + hidx // 16][:, (hidx % 16) * 128:
                                                    (hidx % 16 + 1) * 128],
                              BH[hh // 2][:, (hh % 2) * 1024 + sub * 256:
                                           (hh % 2) * 1024 + (sub + 1) * 256],
                              start=(hidx == 0), stop=(hidx == 63))
                kb.dma(BF[6][:, 0:256], ATTNO[tb * 128:(tb + 1) * 128,
                                              c8 * 256:(c8 + 1) * 256],
                       grp="p3at")
                kb.stt(BF[6][:, 256:512], PSD[:, 0:256], iv3[:, tb:tb + 1],
                       BF[6][:, 0:256], ALU.mult, ALU.add)
                kb.dma(out[tb * 128:(tb + 1) * 128, c8 * 256:(c8 + 1) * 256],
                       BF[6][:, 256:512], q="act", grp="p3out")




def kernel(**inputs):
    cfg = {"phases": {"p0", "p1", "p2a", "p2b", "p2c", "p2d", "p3pre",
                      "p3mid", "p3b"}, "n_chunks": 16, "expose": (), "dbg": {}}
    nc, kb = build_program(cfg)
    maps = [make_core_inputs(inputs, c, cfg) for c in range(8)]
    res = run_bass_kernel_spmd(nc, maps, core_ids=list(range(8)))
    out = np.zeros((2, 8192, 2048), np.float32)
    for c in range(8):
        b, qt = c // 4, c % 4
        out[b, qt * 2048:(qt + 1) * 2048] = np.asarray(res.results[c]["outp"])
    return out


# version 10
# speedup vs baseline: 1.3935x; 1.0016x over previous
import numpy as np
import ml_dtypes
import concourse.bass as bass
import concourse.mybir as mybir
from concourse.bass_utils import run_bass_kernel_spmd

F32 = mybir.dt.float32
BF16 = mybir.dt.bfloat16
I32 = mybir.dt.int32
I8 = mybir.dt.int8
AF = mybir.ActivationFunctionType
ALU = mybir.AluOpType
AX = mybir.AxisListType
BFnp = ml_dtypes.bfloat16

ENGS = ("pe", "act", "dve", "pool", "sp")
ENG_ATTR = {"pe": "tensor", "act": "scalar", "dve": "vector", "pool": "gpsimd",
            "sp": "sync"}

class KB:

    def __init__(self, nc):
        self.nc = nc
        self.sem = {}
        self.cnt = {e: 0 for e in ENGS}
        self.cnt["dma"] = 0
        self._es = None
        self.dep = {}
        self.grp_rr = {}
        self.know = {e: {} for e in ENGS}
        self.snap = {}
        self.hw = {}

    def init_sems(self, es):
        self._es = es
        for e in ENGS:
            self.sem[e] = es.enter_context(self.nc.semaphore(f"sem_{e}"))

    def _ensure_grp(self, g):
        if g not in self.sem:
            self.sem[g] = self._es.enter_context(self.nc.semaphore(f"sem_{g}"))
            self.cnt[g] = 0

    def eng(self, e):
        return getattr(self.nc, ENG_ATTR[e])

    def _tname(self, ap):
        if isinstance(ap, tuple):
            ap, rk = ap
            t = ap.tensor if isinstance(ap, bass.AP) else ap
            return t.name + "#" + str(rk)
        t = ap.tensor if isinstance(ap, bass.AP) else ap
        return t.name

    def _unwrap(self, ap):
        return ap[0] if isinstance(ap, tuple) else ap

    def _need(self, waiter, producer):
        peng, pcnt = producer
        if pcnt <= 0:
            return False
        return self.know[waiter].get(peng, 0) < pcnt

    def _wait(self, waiter, producer):
        peng, pcnt = producer
        self.eng(waiter).wait_ge(self.sem[peng], pcnt)
        kw = self.know[waiter]
        kw[peng] = max(kw.get(peng, 0), pcnt)
        sn = self.snap.get(peng, {}).get(pcnt)
        if sn:
            for x, c in sn.items():
                if kw.get(x, 0) < c:
                    kw[x] = c

    def op(self, e, fn, reads=(), writes=(), dma=None, n=1, win=2):
        deps = []
        def is_psum(ap):
            ap = self._unwrap(ap)
            t = ap.tensor if isinstance(ap, bass.AP) else ap
            return "PSum" in type(t).__name__
        def dep_keys(ap):
            nm = self._tname(ap)
            ks = [nm]
            if "#" in nm:
                ks.append(nm.split("#")[0])
            else:
                ks.extend(k for k in self.dep if k.startswith(nm + "#"))
            return ks
        for ap in reads:
            for k in dep_keys(ap):
                d = self.dep.get(k)
                if d:
                    deps.extend(d["w"].items())
        for ap in writes:
            for k in dep_keys(ap):
                d = self.dep.get(k)
                if d:
                    items = list(d["w"].items()) + list(d["r"].items())
                    if e == "pe" and is_psum(ap):
                        items = [(pe_, pc) for (pe_, pc) in items if pe_ != "pe"]
                    deps.extend(items)
        byeng = {}
        for (pe, pc) in deps:
            byeng[pe] = max(byeng.get(pe, 0), pc)
        if dma:
            idx = self.grp_rr.get(dma, 0)
            self.grp_rr[dma] = (idx + 1) % win
            sem_eng = f"{dma}#{idx}"
            self._ensure_grp(sem_eng)
            byeng[sem_eng] = max(byeng.get(sem_eng, 0), self.cnt[sem_eng])
        else:
            sem_eng = e
        for pe, pc in byeng.items():
            if self._need(e, (pe, pc)):
                self._wait(e, (pe, pc))
        inst = fn(self.eng(e))
        step = 16 if dma else n
        self.cnt[sem_eng] += step
        inst.then_inc(self.sem[sem_eng], step)
        stamp = (sem_eng, self.cnt[sem_eng])
        kn = dict(self.know[e])
        kn[e] = self.cnt[e] if not dma else kn.get(e, 0)
        self.snap.setdefault(sem_eng, {})[self.cnt[sem_eng]] = kn
        for ap in writes:
            nm = self._tname(ap)
            if "#" not in nm:
                for k in [k for k in self.dep if k.startswith(nm + "#")]:
                    del self.dep[k]
            d = self.dep.get(nm)
            if d is None or not dma:
                self.dep[nm] = {"w": {stamp[0]: stamp[1]}, "r": {}}
            else:
                d["w"][stamp[0]] = stamp[1]
                d["r"] = {}
        for ap in reads:
            nm = self.dep.setdefault(self._tname(ap), {"w": {}, "r": {}})
            nm["r"][stamp[0]] = stamp[1]
        return inst

    def dma(self, dst, src, q="sp", grp="dma"):
        d2, s2 = self._unwrap(dst), self._unwrap(src)
        return self.op(q, lambda g: g.dma_start(out=d2, in_=s2),
                       reads=[src], writes=[dst], dma=grp)

    def dmat(self, dst, src, q="sp", grp="dma"):
        return self.op(q, lambda g: g.dma_start_transpose(out=dst, in_=src),
                       reads=[src], writes=[dst], dma=grp)

    def mm(self, out, lhsT, rhs, start=True, stop=True):
        o2, l2, r2 = self._unwrap(out), self._unwrap(lhsT), self._unwrap(rhs)
        return self.op("pe", lambda g: g.matmul(o2, lhsT=l2, rhs=r2,
                                                start=start, stop=stop),
                       reads=[lhsT, rhs], writes=[out])

    def trans(self, out, in_, ident):
        o2, i2 = self._unwrap(out), self._unwrap(in_)
        return self.op("pe", lambda g: g.transpose(o2, in_=i2, identity=ident),
                       reads=[in_, ident], writes=[out])

    def tt(self, e, out, a, b, alu):
        o2, a2, b2 = self._unwrap(out), self._unwrap(a), self._unwrap(b)
        return self.op(e, lambda g: g.tensor_tensor(out=o2, in0=a2, in1=b2, op=alu),
                       reads=[a, b], writes=[out])

    def ts(self, e, out, a, s1, op0, s2=None, op1=ALU.bypass, accum=None):
        reads = [a] + [x for x in (s1, s2) if isinstance(x, (bass.AP, tuple))]
        writes = [out] + ([accum] if accum is not None else [])
        o2, a2 = self._unwrap(out), self._unwrap(a)
        s1u = self._unwrap(s1) if isinstance(s1, tuple) else s1
        s2u = self._unwrap(s2) if isinstance(s2, tuple) else s2
        return self.op(e, lambda g: g.tensor_scalar(
            out=o2, in0=a2, scalar1=s1u, scalar2=s2u, op0=op0, op1=op1,
            accum_out=self._unwrap(accum) if accum is not None else None),
            reads=reads, writes=writes)

    def stt(self, out, a, s, b, op0, op1, accum=None):
        reads = [a, b] + ([s] if isinstance(s, (bass.AP, tuple)) else [])
        writes = [out] + ([accum] if accum is not None else [])
        o2, a2, b2 = self._unwrap(out), self._unwrap(a), self._unwrap(b)
        su = self._unwrap(s) if isinstance(s, tuple) else s
        return self.op("dve", lambda g: g.scalar_tensor_tensor(
            out=o2, in0=a2, scalar=su, in1=b2, op0=op0, op1=op1,
            accum_out=self._unwrap(accum) if accum is not None else None),
            reads=reads, writes=writes)

    def constcol(self, val):
        if not hasattr(self, "_ccache"):
            self._ccache = {}
        if val not in self._ccache:
            t = self._es.enter_context(
                self.nc.sbuf_tensor(f"cc_{len(self._ccache)}", [128, 1], F32))
            self.memset("dve", t[:], float(val))
            self._ccache[val] = t
        return self._ccache[val][:, 0:1]

    def act(self, out, in_, func, bias=None, scale=None, accum=None):
        kw = {}
        if bias is not None:
            if isinstance(bias, float):
                bias = self.constcol(bias)
            kw["bias"] = bias
        if scale is not None:
            kw["scale"] = scale
        if accum is not None:
            kw["accum_out"] = accum
        reads = [in_] + [x for x in (bias, scale) if isinstance(x, (bass.AP, tuple))]
        writes = [out] + ([accum] if accum is not None else [])
        if "bias" in kw and isinstance(kw["bias"], tuple):
            kw["bias"] = self._unwrap(kw["bias"])
        if "scale" in kw and isinstance(kw["scale"], tuple):
            kw["scale"] = self._unwrap(kw["scale"])
        if "accum_out" in kw and isinstance(kw["accum_out"], tuple):
            kw["accum_out"] = self._unwrap(kw["accum_out"])
        o2, i2 = self._unwrap(out), self._unwrap(in_)
        return self.op("act", lambda g: g.activation(out=o2, in_=i2, func=func,
                                                     **kw),
                       reads=reads, writes=writes)

    def copy(self, e, out, in_):
        o2, i2 = self._unwrap(out), self._unwrap(in_)
        if e == "act":
            return self.op("act", lambda g: g.copy(out=o2, in_=i2),
                           reads=[in_], writes=[out])
        return self.op(e, lambda g: g.tensor_copy(out=o2, in_=i2),
                       reads=[in_], writes=[out])

    def red(self, e, out, in_, alu, axis=AX.X, absval=None, negate=None):
        o2, i2 = self._unwrap(out), self._unwrap(in_)
        return self.op(e, lambda g: g.tensor_reduce(
            out=o2, in_=i2, axis=axis, op=alu,
            apply_absolute_value=absval, negate=negate),
            reads=[in_], writes=[out])

    def ttr(self, out, a, b, op0, op1, accum, scale=1.0, seed=0.0):
        return self.op("dve", lambda g: g.tensor_tensor_reduce(
            out=out, in0=a, in1=b, scale=scale, scalar=seed, op0=op0, op1=op1,
            accum_out=accum), reads=[a, b], writes=[out, accum])

    def memset(self, e, ap, val):
        a2 = self._unwrap(ap)
        return self.op(e, lambda g: g.memset(a2, val), writes=[ap])

    def barrier_all(self):
        marks = {}
        for e in ENGS:
            eng = self.eng(e)
            i = eng.nop() if e in ("act", "pe", "sp") else eng.engine_nop()
            self.cnt[e] += 1
            i.then_inc(self.sem[e], 1)
            marks[e] = self.cnt[e]
        for e in ENGS:
            for o in ENGS:
                if o != e and self._need(e, (o, marks[o])):
                    self._wait(e, (o, marks[o]))


def softplus_dev(kb, out, in_, tmp1, tmp2):
    kb.act(tmp1, in_, AF.Abs)
    kb.act(tmp1, tmp1, AF.Exp, scale=-1.0)
    kb.act(tmp1, tmp1, AF.Ln, bias=1.0)
    kb.ts("dve", tmp2, in_, 0.0, ALU.max)
    kb.tt("dve", out, tmp2, tmp1, ALU.add)


def log_sigmoid_dev(kb, out, in_, tmp1, tmp2):
    kb.act(tmp1, in_, AF.Abs)
    kb.act(tmp1, tmp1, AF.Exp, scale=-1.0)
    kb.act(tmp1, tmp1, AF.Ln, bias=1.0)
    kb.ts("dve", tmp2, in_, 0.0, ALU.min)
    kb.tt("dve", out, tmp2, tmp1, ALU.subtract)



def build_program(cfg):
    nc = bass.Bass("TRN2", target_bir_lowering=False)
    kb = KB(nc)
    TOK, HID, NH, HD, NCH, CC = 2048, 2048, 16, 128, 16, 128

    dt = nc.dram_tensor
    xin = dt("xin", [TOK, HID], F32, kind="ExternalInput").ap()
    wq = dt("wq", [HID, HID], F32, kind="ExternalInput").ap()
    wk = dt("wk", [HID, HID], F32, kind="ExternalInput").ap()
    wv = dt("wv", [HID, HID], F32, kind="ExternalInput").ap()
    wab = dt("wab", [HID, 32], F32, kind="ExternalInput").ap()
    bab = dt("bab", [1, 32], F32, kind="ExternalInput").ap()
    lamT = dt("lamT", [HD, NH], F32, kind="ExternalInput").ap()
    onw = dt("onw", [HD, 1], F32, kind="ExternalInput").ap()
    wo = dt("wo", [HID, HID], F32, kind="ExternalInput").ap()
    w1T = dt("w1T", [HID, 8192], F32, kind="ExternalInput").ap()
    w2T = dt("w2T", [HID, 8192], F32, kind="ExternalInput").ap()
    w3T = dt("w3T", [8192, HID], F32, kind="ExternalInput").ap()
    g1r = dt("g1r", [1, HID], F32, kind="ExternalInput").ap()
    g2r = dt("g2r", [1, HID], F32, kind="ExternalInput").ap()
    g3r = dt("g3r", [1, 8192], F32, kind="ExternalInput").ap()
    ident = dt("ident", [128, 128], F32, kind="ExternalInput").ap()
    identb = dt("identb", [128, 128], BF16, kind="ExternalInput").ap()
    triu = dt("triu", [128, 128], F32, kind="ExternalInput").ap()
    tril_m = dt("tril_m", [128, 128], F32, kind="ExternalInput").ap()
    onescol = dt("onescol", [1, 128], F32, kind="ExternalInput").ap()
    onesd = dt("onesd", [128, 1], F32, kind="ExternalInput").ap()
    cmask = dt("cmask", [8, 8], F32, kind="ExternalInput").ap()
    cmaskcol = dt("cmaskcol", [8, 1], F32, kind="ExternalInput").ap()
    e127 = dt("e127", [128, 1], F32, kind="ExternalInput").ap()
    onescolb = dt("onescolb", [1, 128], BF16, kind="ExternalInput").ap()
    onesdb = dt("onesdb", [128, 1], BF16, kind="ExternalInput").ap()

    out = dt("outp", [TOK, HID], F32, kind="ExternalOutput").ap()

    ex = cfg.get("expose", ())
    kd = lambda nm: "ExternalOutput" if nm in ex else "Internal"
    XN_TM = dt("XN_TM", [TOK, HID], F32, kind=kd("XN_TM")).ap()
    XN_FM = dt("XN_FM", [HID, TOK], BF16, kind=kd("XN_FM")).ap()
    QFM = dt("QFM", [NCH, NH, HD, CC], F32, kind=kd("QFM")).ap()
    KFM = dt("KFM", [NCH, NH, HD, CC], BF16, kind=kd("KFM")).ap()
    KTM = dt("KTM", [NCH, CC, NH * HD], BF16, kind=kd("KTM")).ap()
    VTM = dt("VTM", [NCH, CC, NH * HD], BF16, kind=kd("VTM")).ap()
    VFM = dt("VFM", [NCH, NH, HD, CC], BF16, kind=kd("VFM")).ap()
    GTM = dt("GTM", [NCH, CC, NH], F32, kind=kd("GTM")).ap()
    BETA = dt("BETA", [NCH, CC, NH], F32, kind=kd("BETA")).ap()
    THR = dt("THR", [NCH, CC, NH], F32, kind=kd("THR")).ap()
    RQ = dt("RQ", [NCH, CC, NH], F32, kind=kd("RQ")).ap()
    ATTNO = dt("ATTNO", [TOK, HID], F32, kind=kd("ATTNO")).ap()
    ONORM = dt("ONORM", [NCH, HD, TOK // NCH * 16], BF16, kind=kd("ONORM")).ap()
    CIN = dt("CIN", [128, 4112], F32).ap()
    COUT = dt("COUT", [1024, 4112], F32).ap()
    XQ1 = dt("XQ1", [HID, TOK], BF16, kind=kd("XQ1")).ap()
    XQ2 = dt("XQ2", [HID, TOK], BF16, kind=kd("XQ2")).ap()
    XQ3 = dt("XQ3", [8192, TOK], BF16, kind=kd("XQ3")).ap()
    HG3 = dt("HG3", [TOK, 8192], BF16, kind=kd("HG3")).ap()
    dbg_out = {}
    for name, shape in cfg.get("dbg", {}).items():
        dbg_out[name] = dt("dbg_" + name, shape, F32, kind="ExternalOutput").ap()

    from contextlib import ExitStack
    es = ExitStack()
    kb._es = es
    sbuf = lambda n, s, d: es.enter_context(nc.sbuf_tensor(n, s, d))
    psum = lambda n, s, d: es.enter_context(nc.psum_tensor(n, s, d))
    kb.init_sems(es)

    c_ident = sbuf("c_ident", [128, 128], F32)
    c_identb = sbuf("c_identb", [128, 128], BF16)
    c_triu = sbuf("c_triu", [128, 128], F32)
    c_trilm = sbuf("c_trilm", [128, 128], F32)
    c_ones1 = sbuf("c_ones1", [1, 128], F32)
    c_ones1b = sbuf("c_ones1b", [1, 128], BF16)
    c_onesdb = sbuf("c_onesdb", [128, 1], BF16)
    c_e127 = sbuf("c_e127", [128, 1], F32)
    lamE2 = sbuf("lamE2", [128, 16], F32)
    lamdb = [sbuf(f"lamdb{h}", [128, 128], BF16) for h in range(16)]
    c_onesd = sbuf("c_onesd", [128, 1], F32)
    c_bab = sbuf("c_bab", [1, 32], F32)
    lam_sb = sbuf("lam_sb", [128, NH], F32)
    onw_sb = sbuf("onw_sb", [128, 1], F32)

    BF = [sbuf(f"BF{i}", [128, 2048], F32) for i in range(8)]
    BH = [sbuf(f"BH{i}", [128, 2048], BF16) for i in range(16)]

    PSA = psum("PSA", [128, 1024], F32)
    PSB = psum("PSB", [128, 1024], F32)
    PSC = psum("PSC", [128, 1024], F32)
    PSD = psum("PSD", [128, 1024], F32)
    psP = [PSA[:, 0:512], PSA[:, 512:1024], PSB[:, 0:512], PSB[:, 512:1024]]
    psZ = PSC[:, 0:512]
    psT = [PSC[:, 512:640], PSC[:, 640:768]]

    with es:
        kb.dma(c_ident[:], ident)
        kb.dma(c_identb[:], identb)
        kb.dma(c_triu[:], triu)
        kb.dma(c_trilm[:], tril_m)
        kb.dma(c_ones1[:], onescol)
        kb.dma(c_onesd[:], onesd)
        kb.dma(c_e127[:], e127)
        kb.copy("dve", c_ones1b[:], c_ones1[:])
        kb.copy("dve", c_onesdb[:], c_onesd[:])
        kb.dma(c_bab[:], bab)
        kb.dma(onw_sb[:], onw)
        t16a = sbuf("t16a", [128, NH], F32)
        t16b = sbuf("t16b", [128, NH], F32)
        kb.dma(lam_sb[:], lamT)
        softplus_dev(kb, lam_sb[:], lam_sb[:], t16a[:], t16b[:])
        kb.ts("dve", lam_sb[:], lam_sb[:], 0.25, ALU.add)
        kb.ts("dve", lamE2[:], lam_sb[:], 1e-8, ALU.add)
        for h in range(16):
            kb.ts("dve", lamdb[h][:], c_ident[:], lam_sb[:, h:h + 1], ALU.mult)

        if "p0" in cfg["phases"]:
            sx = [BF[0], BF[1]]
            xnt = [BF[2], BF[3]]
            stats = sbuf("p0_stats", [128, 24], F32)
            mv = sbuf("p0_mv", [128, 2], F32)
            rstd = sbuf("p0_rstd", [128, 1], F32)
            xfmb = sbuf("p0_xfmb", [128, 128], BF16)
            pT = psT
            for tt in range(16):
                b = tt % 2
                kb.dma(sx[b][:], xin[tt * 128:(tt + 1) * 128, :], grp=f"dx{b}")
                for j in range(4):
                    kb.op("dve", lambda g, j=j, b=b: g.bn_stats(
                        out=stats[:, j * 6:(j + 1) * 6], in_=sx[b][:, j * 512:(j + 1) * 512]),
                        reads=[sx[b]], writes=[stats])
                kb.op("dve", lambda g: g.bn_aggr(out=mv[:], in_=stats[:]),
                      reads=[stats], writes=[mv])
                kb.act(rstd[:], mv[:, 1:2], AF.Ln, bias=1e-5)
                kb.act(rstd[:], rstd[:], AF.Exp, scale=-0.5)
                kb.ts("dve", xnt[b][:], sx[b][:], mv[:, 0:1], ALU.subtract,
                      rstd[:, 0:1], ALU.mult)
                kb.dma(XN_TM[tt * 128:(tt + 1) * 128, :], xnt[b][:], q="act", grp="sx")
                for ct in range(16):
                    pb = ct % 2
                    kb.trans(pT[pb], xnt[b][:, ct * 128:(ct + 1) * 128], c_ident[:])
                    kb.copy("act", xfmb[:], pT[pb])
                    kb.dma(XN_FM[ct * 128:(ct + 1) * 128, tt * 128:(tt + 1) * 128],
                           xfmb[:], q="act", grp="sfm")

        if "p1" in cfg["phases"]:
            xnfm_sv = [BH[1], BH[2], BH[3], BH[4]]
            class _XN:
                def __getitem__(self, sl):
                    raise RuntimeError("use xnfm_at")
            def xnfm_at(ct, lo, hi):
                return xnfm_sv[ct // 4][:, (ct % 4) * 512 + lo:(ct % 4) * 512 + hi]
            wfp = [sbuf(f"p1_wfp{i}", [128, 512], F32) for i in range(2)]
            wbf = [sbuf(f"p1_wbf{i}", [128, 512], BF16) for i in range(2)]
            qtm = [BF[0], BF[1], BF[2], BF[3]]
            sq2 = BF[4]
            n2 = sbuf("p1_n2", [128, NH], F32)
            scal = sbuf("p1_scal", [128, NH], F32)
            ktb = BH[0]
            gb = sbuf("p1_gb", [128, 32], F32)
            gsm = sbuf("p1_gsm", [128, NH], F32)
            t16c = sbuf("p1_t16c", [128, NH], F32)
            fmb = sbuf("p1_fmb", [128, 128], BF16)
            fmf = sbuf("p1_fmf", [128, 128], F32)

            def proj_assemble(Wap, sblk):
                t0 = sblk * 512
                for nt in range(4):
                    for ct in range(16):
                        b = ct % 2
                        kb.dma(wfp[b][:], Wap[ct * 128:(ct + 1) * 128,
                                              nt * 512:(nt + 1) * 512], grp=f"dw{b}")
                        kb.copy("dve", wbf[b][:], wfp[b][:])
                        for tb in range(4):
                            kb.mm(psP[tb], xnfm_at(ct, tb * 128, (tb + 1) * 128),
                                  wbf[b][:], start=(ct == 0), stop=(ct == 15))
                    for tb in range(4):
                        kb.act(qtm[tb][:, nt * 512:(nt + 1) * 512], psP[tb], AF.Silu)

            for sblk in range(4):
                t0 = sblk * 512
                for ct in range(16):
                    kb.dma(xnfm_at(ct, 0, 512),
                           XN_FM[ct * 128:(ct + 1) * 128, t0:t0 + 512], grp="dxf")
                proj_assemble(wq, sblk)
                for tb in range(4):
                    ch = sblk * 4 + tb
                    q3 = qtm[tb][:].rearrange("p (h d) -> p h d", d=HD)
                    kb.act(sq2[:], qtm[tb][:], AF.Square)
                    kb.red("dve", n2[:], sq2[:].rearrange("p (h d) -> p h d", d=HD),
                           ALU.add)
                    kb.ts("dve", scal[:], n2[:], 1e-24, ALU.max, 1e-10, ALU.mult)
                    kb.dma(THR[ch], scal[:], q="act", grp="sth")
                    kb.act(t16c[:], n2[:], AF.Ln)
                    kb.act(t16c[:], t16c[:], AF.Exp, scale=0.5)
                    kb.ts("dve", t16c[:], t16c[:], 1e-12, ALU.max)
                    kb.op("dve", lambda g: g.reciprocal(out=t16c[:], in_=t16c[:]),
                          reads=[t16c], writes=[t16c])
                    kb.dma(RQ[ch], t16c[:], q="act", grp="sth")
                    for h in range(NH):
                        pb = h % 2
                        kb.trans(psT[pb], qtm[tb][:, h * HD:(h + 1) * HD], c_ident[:])
                        kb.copy("dve", fmf[:], psT[pb])
                        kb.dma(QFM[ch, h], fmf[:], q="act", grp="sq")
                proj_assemble(wk, sblk)
                for tb in range(4):
                    ch = sblk * 4 + tb
                    kb.act(sq2[:], qtm[tb][:], AF.Square)
                    kb.red("dve", n2[:], sq2[:].rearrange("p (h d) -> p h d", d=HD),
                           ALU.add)
                    kb.act(scal[:], n2[:], AF.Ln)
                    kb.act(scal[:], scal[:], AF.Exp, scale=0.5)
                    kb.ts("dve", scal[:], scal[:], 1e-12, ALU.max)
                    kb.op("dve", lambda g: g.reciprocal(out=scal[:], in_=scal[:]),
                          reads=[scal], writes=[scal])
                    for h in range(NH):
                        kb.ts("dve", sq2[:, h * HD:(h + 1) * HD],
                              qtm[tb][:, h * HD:(h + 1) * HD], scal[:, h:h + 1],
                              ALU.mult)
                    kb.copy("act", ktb[:], sq2[:])
                    kb.dma(KTM[ch], ktb[:], q="act", grp="sk")
                    for h in range(NH):
                        pb = h % 2
                        kb.trans(psT[pb], sq2[:, h * HD:(h + 1) * HD], c_ident[:])
                        kb.copy("dve", fmb[:], psT[pb])
                        kb.dma(KFM[ch, h], fmb[:], q="act", grp="sk")
                proj_assemble(wv, sblk)
                for tb in range(4):
                    ch = sblk * 4 + tb
                    kb.copy("dve", ktb[:], qtm[tb][:])
                    kb.dma(VTM[ch], ktb[:], q="act", grp="sv")
                    for h in range(NH):
                        pb = h % 2
                        kb.trans(psT[pb], qtm[tb][:, h * HD:(h + 1) * HD],
                                 c_ident[:])
                        kb.copy("dve", fmb[:], psT[pb])
                        kb.dma(VFM[ch, h], fmb[:], q="act", grp="sv")
                for tb in range(4):
                    ch = sblk * 4 + tb
                    for ct in range(16):
                        b = ct % 2
                        kb.dma(wfp[b][:, :32], wab[ct * 128:(ct + 1) * 128, :],
                               grp=f"dw{b}")
                        kb.copy("dve", wbf[b][:, :32], wfp[b][:, :32])
                        kb.mm(psZ[:, 0:32], xnfm_at(ct, tb * 128, (tb + 1) * 128),
                              wbf[b][:, :32], start=(ct == 0), stop=False)
                    kb.copy("dve", wbf[0][:1, :32], c_bab[:])
                    kb.mm(psZ[:, 0:32], c_ones1b[:], wbf[0][:1, :32],
                          start=False, stop=True)
                    kb.copy("dve", gb[:], psZ[:, 0:32])
                    kb.act(t16c[:], gb[:, 16:32], AF.Sigmoid)
                    kb.dma(BETA[ch], t16c[:], q="act", grp="sg")
                    log_sigmoid_dev(kb, gsm[:], gb[:, :16], t16c[:], n2[:])
                    kb.mm(psZ[:, 64:80], c_triu[:], gsm[:])
                    kb.copy("dve", t16c[:], psZ[:, 64:80])
                    kb.dma(GTM[ch], t16c[:], q="act", grp="sg")

        T = dict(sbuf=sbuf, psum=psum, BF=BF, BH=BH, KTM=KTM, VTM=VTM, KFM=KFM, VFM=VFM,
                 QFM=QFM, GTM=GTM, BETA=BETA, THR=THR, RQ=RQ, CIN=CIN,
                 COUT=COUT, ONORM=ONORM, ATTNO=ATTNO, XN_TM=XN_TM, xin=xin,
                 wo=wo, c_ident=c_ident, c_trilm=c_trilm, c_ones1=c_ones1,
                 c_ones1b=c_ones1b, c_onesd=c_onesd, c_onesdb=c_onesdb,
                 c_e127=c_e127, c_identb=c_identb, lam_sb=lam_sb, lamdb=lamdb, lamE2=lamE2,
                 onw_sb=onw_sb, PSA=PSA, PSB=PSB, PSC=PSC, PSD=PSD,
                 cmask=cmask, cmaskcol=cmaskcol, out=out,
                 w1T=w1T, w2T=w2T, w3T=w3T, g1r=g1r, g2r=g2r, g3r=g3r,
                 XQ1=XQ1, XQ2=XQ2, XQ3=XQ3, HG3=HG3)
        cfg["dumps"] = dbg_out
        if ("p2a" in cfg["phases"] or "p2c" in cfg["phases"]
                or "p2b" in cfg["phases"]):
            emit_p2(kb, cfg, T)
        if "p2d" in cfg["phases"]:
            emit_p2d(kb, cfg, T)
        if {"p3pre", "p3mid", "p3b"} & cfg["phases"]:
            emit_p3(kb, cfg, T)

        for e in ("sp",):
            for gname, cnt in list(kb.cnt.items()):
                if gname in ENGS or cnt == 0:
                    continue
                kb.eng("sp").wait_ge(kb.sem[gname], cnt)

    return nc, kb


def make_core_inputs(inp, core, cfg):
    b, qt = core // 4, core % 4
    TOK = 2048
    x = np.ascontiguousarray(inp["L0_attn_x"][b, qt * TOK:(qt + 1) * TOK, :], )
    ii = np.arange(128)
    m = {
        "xin": x.astype(np.float32),
        "wq": inp["L0_attn_Wq"], "wk": inp["L0_attn_Wk"], "wv": inp["L0_attn_Wv"],
        "wab": np.ascontiguousarray(np.concatenate(
            [inp["L0_attn_Wa"], inp["L0_attn_Wb"]], 1)),
        "bab": np.concatenate([inp["L0_attn_ba"], inp["L0_attn_bb"]])[None, :],
        "lamT": np.ascontiguousarray(inp["L0_attn_lambda_params"].reshape(16, 128).T),
        "onw": inp["L0_attn_o_norm_weight"][:, None],
        "wo": inp["L0_attn_Wo"],
        "w1T": np.ascontiguousarray(inp["L0_ffn_w1"].T),
        "w2T": np.ascontiguousarray(inp["L0_ffn_w2"].T),
        "w3T": np.ascontiguousarray(inp["L0_ffn_w3"].T),
        "g1r": inp["L0_ffn_g1"][None, :], "g2r": inp["L0_ffn_g2"][None, :],
        "g3r": inp["L0_ffn_g3"][None, :],
        "ident": np.eye(128, dtype=np.float32),
        "identb": np.eye(128).astype(BFnp),
        "triu": (ii[:, None] <= ii[None, :]).astype(np.float32),
        "tril_m": (ii[:, None] <= ii[None, :]).astype(np.float32),
        "onescol": np.ones((1, 128), np.float32),
        "onesd": np.ones((128, 1), np.float32),
        "cmask": np.zeros((8, 8), np.float32),
        "cmaskcol": np.zeros((8, 1), np.float32),
        "e127": np.eye(128, dtype=np.float32)[:, 127:128],
        "onescolb": np.ones((1, 128)).astype(BFnp),
        "onesdb": np.ones((128, 1)).astype(BFnp),
    }
    b_, qt_ = core // 4, core % 4
    preds = [r for r in range(b_ * 4, b_ * 4 + qt_)]
    cmc = np.zeros((8, 1), np.float32)
    cm2 = np.zeros((8, 8), np.float32)
    for r in preds:
        cmc[r, 0] = 1.0
        for rp in range(r + 1, b_ * 4 + qt_):
            cm2[rp, r] = 1.0
    m["cmask"] = cm2
    m["cmaskcol"] = cmc
    return {k: np.ascontiguousarray(v.astype(np.float32) if v.dtype == np.float64
                                    else v) for k, v in m.items()}


def emit_p2(kb, cfg, T):
    nc = kb.nc
    NH, HD, CC = 16, 128, 128
    NCH = cfg.get("n_chunks", 16)
    sbuf, psum = T["sbuf"], T["psum"]
    (KTM, VTM, KFM, VFM, QFM, GTM, BETA, THR, RQ, CIN, COUT, ONORM) = (
        T["KTM"], T["VTM"], T["KFM"], T["VFM"], T["QFM"], T["GTM"], T["BETA"],
        T["THR"], T["RQ"], T["CIN"], T["COUT"], T["ONORM"])
    c_ident, c_trilm, c_ones1, c_ones1b, c_onesd, c_onesdb, c_e127 = (
        T["c_ident"], T["c_trilm"], T["c_ones1"], T["c_ones1b"], T["c_onesd"],
        T["c_onesdb"], T["c_e127"])
    c_identb = T["c_identb"]
    lam_sb, lamE2, lamdb = T["lam_sb"], T["lamE2"], T["lamdb"]
    onw_sb = T["onw_sb"]
    PSA, PSB, PSC, PSD = T["PSA"], T["PSB"], T["PSC"], T["PSD"]
    S_u = (PSA, PSC)
    Q_u = (PSB, PSD)
    BF, BH = T["BF"], T["BH"]

    A0st = [sbuf(f"A0st{h}", [128, 128], F32) for h in range(NH)]
    HKst = [sbuf(f"HKst{h}", [128, 128], F32) for h in range(NH)]
    ktm_sb = BH[1]
    vtm_sb = BH[2]
    kfm_sb = BH[3]
    vfm_sb = BH[4]
    qfm_sb = BF[7]
    Gt = sbuf("Gt", [128, NH], F32)
    Bt = sbuf("Bt", [128, NH], F32)
    Tt = sbuf("Tt", [128, NH], F32)
    Rqt = sbuf("Rqt", [128, NH], F32)
    wcol = sbuf("wcol", [128, NH], F32)
    eGlb = sbuf("eGlb", [128, NH], F32)
    glrow = sbuf("glrow", [1, NH], F32)
    dgacc = sbuf("dgacc", [1, NH], F32)
    kwb = sbuf("kwb", [128, 128], BF16)
    vwb = sbuf("vwb", [128, 128], BF16)
    t16a = sbuf("p2_t16a", [128, NH], F32)
    t16b = sbuf("p2_t16b", [128, NH], F32)
    t16c = sbuf("p2_t16c", [128, NH], F32)
    t16d = sbuf("p2_t16d", [128, NH], F32)

    def glast_wcol(c):
        kb.mm(PSC[:1, 0:16], c_e127[:], Gt[:])
        kb.copy("dve", glrow[:], PSC[:1, 0:16])
        kb.mm(PSC[:, 16:32], c_ones1[:], glrow[:])
        kb.tt("dve", t16a[:], PSC[:, 16:32], Gt[:], ALU.subtract)
        kb.ts("dve", t16a[:], t16a[:], 0.0, ALU.min)
        kb.act(t16b[:], t16a[:], AF.Exp)
        kb.tt("dve", wcol[:], t16b[:], Bt[:], ALU.mult)
        kb.act(eGlb[:], PSC[:, 16:32], AF.Exp)

    def state_update(c):
        for h in range(NH):
            u, hh = h // 8, h % 8
            kb.ts("dve", kwb[:], ktm_sb[:, h * HD:(h + 1) * HD],
                  wcol[:, h:h + 1], ALU.mult)
            kb.ts("dve", vwb[:], vtm_sb[:, h * HD:(h + 1) * HD],
                  wcol[:, h:h + 1], ALU.mult)
            kb.mm(S_u[u][:, hh * 128:(hh + 1) * 128], kwb[:],
                  ktm_sb[:, h * HD:(h + 1) * HD])
            kb.mm(Q_u[u][:, hh * 128:(hh + 1) * 128], vwb[:],
                  ktm_sb[:, h * HD:(h + 1) * HD])
            kb.stt(A0st[h][:], A0st[h][:], eGlb[:, h:h + 1],
                   S_u[u][:, hh * 128:(hh + 1) * 128], ALU.mult, ALU.add)
            kb.stt(HKst[h][:], HKst[h][:], eGlb[:, h:h + 1],
                   Q_u[u][:, hh * 128:(hh + 1) * 128], ALU.mult, ALU.add)

    if "p2a" in cfg["phases"]:
        for h in range(NH):
            kb.memset("dve", A0st[h][:], 0.0)
            kb.memset("dve", HKst[h][:], 0.0)
        kb.memset("dve", dgacc[:], 0.0)
        for c in range(NCH):
            kb.dma(ktm_sb[:], KTM[c], grp="p2k")
            kb.dma(vtm_sb[:], VTM[c], grp="p2v")
            kb.dma(Gt[:], GTM[c], grp="p2g")
            kb.dma(Bt[:], BETA[c], grp="p2g")
            glast_wcol(c)
            kb.tt("dve", dgacc[:], dgacc[:], glrow[:], ALU.add)
            state_update(c)
        for h in range(NH):
            kb.dma(CIN[:, h * 128:(h + 1) * 128], A0st[h][:], grp="cinw")
            kb.dma(CIN[:, 2048 + h * 128:2048 + (h + 1) * 128], HKst[h][:],
                   grp="cinw")
        kb.memset("dve", t16d[:], 0.0)
        kb.copy("dve", t16d[0:1, :], dgacc[:])
        kb.dma(CIN[:, 4096:4112], t16d[:], grp="cinw")
        kb.op("pool", lambda g: g.collective_compute(
            "AllGather", ALU.bypass, replica_groups=[list(range(8))],
            ins=[CIN], outs=[COUT]), reads=[CIN], writes=[COUT])

    if "p2b" in cfg["phases"]:
        cmsk = sbuf("cmsk", [8, 8], F32)
        cmcol = sbuf("cmcol", [8, 1], F32)
        dgall = sbuf("dgall", [8, NH], F32)
        wt8 = sbuf("wt8", [8, NH], F32)
        wflat = sbuf("wflat", [1, 128], F32)
        wBs = sbuf("wBs", [128, 128], F32)
        gsbA = BF[0]
        gsbB = BF[1]
        kb.dma(cmsk[:], T["cmask"])
        kb.dma(cmcol[:], T["cmaskcol"])
        for r in range(8):
            kb.dma(dgall[r:r + 1, :], COUT[r * 128:r * 128 + 1, 4096:4112],
                   grp="p2b")
        kb.mm(PSC[:8, 0:16], cmsk[:], dgall[:])
        kb.act(wt8[:], PSC[:8, 0:16], AF.Exp)
        kb.ts("dve", wt8[:], wt8[:], cmcol[:, 0:1], ALU.mult)
        kb.dma(wflat[:], wt8[:], grp="p2b")
        kb.mm(PSC[:, 32:160], c_ones1[:], wflat[:])
        kb.copy("dve", wBs[:], PSC[:, 32:160])
        for h in range(NH):
            kb.memset("dve", A0st[h][:], 0.0)
            kb.memset("dve", HKst[h][:], 0.0)
        for r in range(8):
            kb.dma(gsbA[:], COUT[r * 128:(r + 1) * 128, 0:2048], grp="p2ga")
            kb.dma(gsbB[:], COUT[r * 128:(r + 1) * 128, 2048:4096], grp="p2gb")
            for h in range(NH):
                kb.stt(A0st[h][:], gsbA[:, h * 128:(h + 1) * 128],
                       wBs[:, r * 16 + h:r * 16 + h + 1], A0st[h][:],
                       ALU.mult, ALU.add)
                kb.stt(HKst[h][:], gsbB[:, h * 128:(h + 1) * 128],
                       wBs[:, r * 16 + h:r * 16 + h + 1], HKst[h][:],
                       ALU.mult, ALU.add)

    if "p2c" in cfg["phases"]:
        Xcg = BF[0]
        Rcg = BF[1]
        Pcg = BF[2]
        Pb = BH[5]
        Psc = BH[6]
        swb = BH[7]
        pqb = BH[8]
        rrb = BH[9]
        wm = BH[10]
        k2tm = BH[11]
        A0p = BH[12]
        hkvb = BH[13]
        expGb = BF[3]
        rqb = BF[4]
        egrqb = BF[5]
        ab_sb = BH[14]
        dn_sb = BF[6]
        onorm_sb = BH[15]
        gtr2 = sbuf("gtr2", [128, 32], F32)
        tr32 = sbuf("tr32", [32, 128], F32)
        rows32 = sbuf("rows32", [1, 4096], F32)
        T["rows32"] = rows32
        dgtile = sbuf("dgtile", [128, 128], F32)
        bmtile = sbuf("bmtile", [128, 128], F32)
        wexp = sbuf("wexp", [128, 128], F32)
        datile = sbuf("datile", [128, 128], F32)
        dsc = sbuf("dsc", [128, NH], F32)
        dnsc = sbuf("dnsc", [128, NH], F32)
        pqsc = sbuf("pqsc", [128, NH], F32)
        donec = sbuf("donec", [128, NH], F32)
        ndone = sbuf("ndone", [128, NH], F32)
        alsc = sbuf("alsc", [128, NH], F32)
        besc = sbuf("besc", [128, NH], F32)
        s16a = sbuf("s16a", [128, NH], F32)
        s16b = sbuf("s16b", [128, NH], F32)
        abT = sbuf("abT", [16, 128], BF16)
        abrow2 = sbuf("abrow2", [1, 2048], BF16)

        def U(t, u):
            return (t[:, u * 1024:(u + 1) * 1024], u)

        def Uh(t, u, hh, w=128):
            return (t[:, u * 1024 + hh * w:u * 1024 + (hh + 1) * w], u)

        def Us(t, u):
            return (t[:, u * 8:(u + 1) * 8], u)

        def Ush(t, u, hh):
            return (t[:, u * 8 + hh:u * 8 + hh + 1], u)

        def mv_unit(u, Pb_src, Psc_src):
            for hh in range(8):
                h = u * 8 + hh
                kb.mm((S_u[u][:, hh * 128:(hh + 1) * 128], u),
                      Uh(kfm_sb, u, hh), Uh(Pb_src, u, hh))
            kb.tt("dve", U(swb, u), U(wm, u), (S_u[u][:], u), ALU.mult)
            for hh in range(8):
                h = u * 8 + hh
                kb.mm((Q_u[u][:, hh * 128:(hh + 1) * 128], u),
                      Uh(ktm_sb, u, hh), Uh(swb, u, hh),
                      start=True, stop=False)
                kb.mm((Q_u[u][:, hh * 128:(hh + 1) * 128], u),
                      Uh(A0p, u, hh), Uh(Psc_src, u, hh),
                      start=False, stop=False)
                kb.mm((Q_u[u][:, hh * 128:(hh + 1) * 128], u),
                      lamdb[h][:], Uh(Pb_src, u, hh),
                      start=False, stop=True)

        def colreduce_u(u, src, lo):
            for hh in range(8):
                kb.mm((S_u[u][:, lo + hh:lo + hh + 1], u),
                      Uh(src, u, hh), c_onesdb[:])


        NIT = cfg.get("n_iter", 30)
        dumps = cfg.get("dumps", {})
        def dump(name, ap):
            if name in dumps:
                kb.dma(dumps[name], ap, q="pool", grp="dmp")
        if "p2b" not in cfg["phases"]:
            for h in range(NH):
                kb.memset("dve", A0st[h][:], 0.0)
                kb.memset("dve", HKst[h][:], 0.0)
        for c in range(NCH):
            kb.dma(ktm_sb[:], KTM[c], grp="p2k")
            kb.dma(vtm_sb[:], VTM[c], grp="p2v")
            kfm_ap = KFM[c].rearrange("h d t -> d h t")
            vfm_ap = VFM[c].rearrange("h d t -> d h t")
            qfm_ap = QFM[c].rearrange("h d t -> d h t")
            kb.dma(kfm_sb[:], kfm_ap, grp="p2k")
            kb.dma(vfm_sb[:], vfm_ap, grp="p2v")
            kb.dma(qfm_sb[:], qfm_ap, grp="p2q")
            kb.dma(Gt[:], GTM[c], grp="p2g")
            kb.dma(Bt[:], BETA[c], grp="p2g")
            kb.dma(Tt[:], THR[c], grp="p2g")
            kb.dma(Rqt[:], RQ[c], grp="p2g")

            kb.copy("dve", gtr2[:, 0:16], Gt[:])
            kb.copy("dve", gtr2[:, 16:32], Rqt[:])
            kb.trans(PSC[:32, 128:256], gtr2[:], c_ident[:])
            kb.copy("dve", tr32[:], PSC[:32, 128:256])
            kb.dma(rows32[:], tr32[:], grp="fl0")
            for h in range(NH):
                u, hh = h // 8, h % 8
                kb.mm((S_u[u][:, 384:512], u), c_ones1[:],
                      rows32[:1, h * 128:(h + 1) * 128])
                kb.ts("dve", dgtile[:], (S_u[u][:, 384:512], u), Gt[:, h:h + 1],
                      ALU.subtract, 0.0, ALU.min)
                kb.act(wexp[:], dgtile[:], AF.Exp)
                kb.ts("dve", bmtile[:], c_trilm[:], Bt[:, h:h + 1], ALU.mult)
                kb.tt("dve", Uh(wm, u, hh), wexp[:], bmtile[:], ALU.mult)
            for u in range(2):
                for qq in range(2):
                    kb.mm((S_u[u][:, qq * 512:(qq + 1) * 512], u), c_ones1[:],
                          rows32[:1, 2048 + u * 1024 + qq * 512:
                                 2048 + u * 1024 + (qq + 1) * 512])
                kb.copy("dve", U(rqb, u), (S_u[u][:], u))
            kb.act(rows32[:1, 0:2048], rows32[:1, 0:2048], AF.Exp)
            for u in range(2):
                for qq in range(2):
                    kb.mm((S_u[u][:, qq * 512:(qq + 1) * 512], u), c_ones1[:],
                          rows32[:1, u * 1024 + qq * 512:
                                 u * 1024 + (qq + 1) * 512])
                kb.copy("dve", U(expGb, u), (S_u[u][:], u))
                kb.tt("pool", U(egrqb, u), U(expGb, u), U(rqb, u), ALU.mult)
            kb.act(k2tm[:], ktm_sb[:], AF.Square)
            for h in range(NH):
                u, hh = h // 8, h % 8
                kb.copy("act", Uh(A0p, u, hh), A0st[h][:])
                kb.copy("act", Uh(hkvb, u, hh), HKst[h][:])
            for h in range(NH):
                u, hh = h // 8, h % 8
                kb.mm((Q_u[u][:, hh * 128:(hh + 1) * 128], u),
                      Uh(k2tm, u, hh), Uh(wm, u, hh), start=True, stop=False)
                kb.tt("dve", datile[:], A0st[h][:], c_ident[:], ALU.mult)
                kb.mm((S_u[u][:1, 0:128], u), c_onesd[:], datile[:])
                kb.copy("dve", (rows32[:1, 2048 + h * 128:2048 + (h + 1) * 128],
                                u), (S_u[u][:1, 0:128], u))
                kb.mm((Q_u[u][:, hh * 128:(hh + 1) * 128], u),
                      (rows32[:1, 2048 + h * 128:2048 + (h + 1) * 128], u),
                      (rows32[:1, h * 128:(h + 1) * 128], "ro"),
                      start=False, stop=True)
            for h in range(NH):
                u, hh = h // 8, h % 8
                kb.act(Uh(dn_sb, u, hh), (Q_u[u][:, hh * 128:(hh + 1) * 128], u),
                       AF.Ln, bias=lamE2[:, h:h + 1])
            for u in range(2):
                kb.act(U(dn_sb, u), U(dn_sb, u), AF.Exp, scale=-1.0)
                kb.tt("dve", U(Xcg, u), U(qfm_sb, u), U(dn_sb, u), ALU.mult)
                kb.copy("act", U(Pb, u), U(Xcg, u))
                kb.tt("pool", U(Psc, u), U(Xcg, u), U(expGb, u), ALU.mult)
            for u in range(2):
                mv_unit(u, Pb, Psc)
            for u in range(2):
                kb.tt("dve", U(Rcg, u), U(qfm_sb, u), (Q_u[u][:], u),
                      ALU.subtract)
                kb.copy("dve", U(Pcg, u), U(Rcg, u))
                kb.act(U(rrb, u), U(Rcg, u), AF.Square)
                colreduce_u(u, rrb, 0)
                kb.copy("dve", Us(dsc, u), (S_u[u][:, 0:8], u))
                kb.memset("dve", Us(donec, u), 0.0)
            if c == 0:
                dump("X0", Xcg[:])

            def route(src16, evac=True):
                kb.trans((S_u[0][0:16, 128:256], 0), src16, c_ident[:])
                kb.copy("dve", abT[:], (S_u[0][0:16, 128:256], 0))
                kb.dma(abrow2[:], abT[:], grp="fl0")
                for u in range(2):
                    kb.mm((S_u[u][:, 0:512], u), c_ones1b[:],
                          abrow2[:1, u * 1024:u * 1024 + 512])
                    kb.mm((S_u[u][:, 512:1024], u), c_ones1b[:],
                          abrow2[:1, u * 1024 + 512:(u + 1) * 1024])
                    if evac:
                        kb.copy("act", U(ab_sb, u), (S_u[u][:], u))

            def seg1(u):
                kb.copy("act", U(Pb, u), U(Pcg, u))
                kb.tt("pool", U(Psc, u), U(Pcg, u), U(expGb, u), ALU.mult)
                mv_unit(u, Pb, Psc)
                kb.tt("dve", U(pqb, u), U(Pcg, u), (Q_u[u][:], u), ALU.mult)
                colreduce_u(u, pqb, 0)
                kb.copy("dve", Us(pqsc, u), (S_u[u][:, 0:8], u))

            def smalla():
                kb.tt("dve", s16a[:], dsc[:], Tt[:], ALU.is_lt)
                kb.tt("dve", donec[:], donec[:], s16a[:], ALU.max)
                kb.act(s16a[:], pqsc[:], AF.Abs)
                kb.tt("dve", s16a[:], s16a[:], Tt[:], ALU.is_lt)
                kb.tt("dve", donec[:], donec[:], s16a[:], ALU.max)
                kb.ts("dve", ndone[:], donec[:], -1.0, ALU.mult, 1.0, ALU.add)
                kb.tt("dve", s16a[:], pqsc[:], ndone[:], ALU.mult)
                kb.tt("dve", s16a[:], s16a[:], donec[:], ALU.add)
                kb.op("dve", lambda g: g.reciprocal(out=s16b[:], in_=s16a[:]),
                      reads=[s16a], writes=[s16b])
                kb.tt("dve", alsc[:], dsc[:], s16b[:], ALU.mult)
                kb.tt("dve", alsc[:], alsc[:], ndone[:], ALU.mult)

            def upd(u):
                kb.tt("pool", U(pqb, u), U(ab_sb, u), U(Pcg, u), ALU.mult)
                kb.tt("pool", U(Xcg, u), U(Xcg, u), U(pqb, u), ALU.add)
                kb.tt("dve", U(rrb, u), U(ab_sb, u), (Q_u[u][:], u), ALU.mult)
                kb.tt("dve", U(Rcg, u), U(Rcg, u), U(rrb, u), ALU.subtract)
                kb.act(U(rrb, u), U(Rcg, u), AF.Square)
                colreduce_u(u, rrb, 8)
                kb.copy("dve", Us(dnsc, u), (S_u[u][:, 8:16], u))

            def smallb():
                kb.tt("dve", s16a[:], dnsc[:], dsc[:], ALU.subtract)
                kb.tt("dve", s16a[:], s16a[:], ndone[:], ALU.mult)
                kb.tt("dve", dnsc[:], dsc[:], s16a[:], ALU.add)
                kb.tt("dve", s16a[:], dsc[:], ndone[:], ALU.mult)
                kb.tt("dve", s16a[:], s16a[:], donec[:], ALU.add)
                kb.op("dve", lambda g: g.reciprocal(out=s16b[:], in_=s16a[:]),
                      reads=[s16a], writes=[s16b])
                kb.tt("dve", besc[:], dnsc[:], s16b[:], ALU.mult)
                kb.tt("dve", besc[:], besc[:], ndone[:], ALU.mult)
                kb.copy("dve", dsc[:], dnsc[:])

            def pupd(u):
                kb.tt("dve", U(pqb, u), U(Pcg, u), (S_u[u][:], u), ALU.mult)
                kb.tt("pool", U(Pcg, u), U(Rcg, u), U(pqb, u), ALU.add)

            for it in range(NIT):
                seg1(0)
                seg1(1)
                smalla()
                route(alsc[:])
                upd(0)
                upd(1)
                smallb()
                route(besc[:], evac=False)
                pupd(0)
                pupd(1)

            for u in range(2):
                kb.tt("dve", U(pqb, u), U(Xcg, u), U(rqb, u), ALU.mult)
                kb.tt("pool", U(rrb, u), U(Xcg, u), U(egrqb, u), ALU.mult)
                for hh in range(8):
                    kb.mm((S_u[u][:, hh * 128:(hh + 1) * 128], u),
                          Uh(vfm_sb, u, hh), Uh(pqb, u, hh))
                kb.tt("dve", U(swb, u), U(wm, u), (S_u[u][:], u), ALU.mult)
                for hh in range(8):
                    h = u * 8 + hh
                    kb.mm((Q_u[u][:, hh * 128:(hh + 1) * 128], u),
                          Uh(ktm_sb, u, hh), Uh(swb, u, hh),
                          start=True, stop=False)
                    kb.mm((Q_u[u][:, hh * 128:(hh + 1) * 128], u),
                          Uh(hkvb, u, hh), Uh(rrb, u, hh),
                          start=False, stop=True)
                kb.act(U(pqb, u), (Q_u[u][:], u), AF.Square)
                colreduce_u(u, pqb, 0)
                kb.ts("dve", Us(s16a, u), (S_u[u][:, 0:8], u), 1.0 / 128,
                      ALU.mult, 1e-6, ALU.add)
                kb.act(Us(s16a, u), Us(s16a, u), AF.Ln)
                kb.act(Us(s16a, u), Us(s16a, u), AF.Exp, scale=-0.5)
                kb.trans((S_u[u][0:8, 128:256], u), Us(s16a, u), c_ident[:])
                kb.copy("dve", abT[0:8, :], (S_u[u][0:8, 128:256], u))
                kb.dma(abrow2[:1, 0:1024], abT[0:8, :], grp="fl0")
                kb.mm((S_u[u][:, 0:512], u), c_ones1b[:], abrow2[:1, 0:512])
                kb.mm((S_u[u][:, 512:1024], u), c_ones1b[:],
                      abrow2[:1, 512:1024])
                kb.copy("act", U(ab_sb, u), (S_u[u][:], u))
                kb.stt(U(onorm_sb, u), (Q_u[u][:], u), onw_sb[:, 0:1],
                       U(ab_sb, u), ALU.mult, ALU.mult)
                kb.dma(ONORM[c][:, u * 1024:(u + 1) * 1024],
                       U(onorm_sb, u), grp=f"son{u}")
            glast_wcol(c)
            state_update(c)


def emit_p2d(kb, cfg, T):
    nc = kb.nc
    BF, BH = T["BF"], T["BH"]
    PSA, PSB = T["PSA"], T["PSB"]
    pd4 = [PSA[:, 0:512], PSA[:, 512:1024], PSB[:, 0:512], PSB[:, 512:1024]]
    ONORM, ATTNO, xin, wo = T["ONORM"], T["ATTNO"], T["xin"], T["wo"]
    sbuf = T["sbuf"]
    wod = [sbuf(f"wod{i}", [128, 512], F32) for i in range(2)]
    wob = [sbuf(f"wob{i}", [128, 512], BF16) for i in range(2)]
    NCH = cfg.get("n_chunks", 16)
    for tg in range(4):
        chs = [tg * 4 + i for i in range(4) if tg * 4 + i < NCH]
        if not chs:
            continue
        for i, ch in enumerate(chs):
            kb.dma(BH[1 + i][:], ONORM[ch], grp="p2d_on")
        for cq in range(4):
            for ct in range(16):
                b = ct % 2
                kb.dma(wod[b][:], wo[ct * 128:(ct + 1) * 128,
                                     cq * 512:(cq + 1) * 512], grp=f"p2dw{b}")
                kb.copy("dve", wob[b][:], wod[b][:])
                for i, ch in enumerate(chs):
                    kb.mm(pd4[i],
                          BH[1 + i][:, ct * 128:(ct + 1) * 128], wob[b][:],
                          start=(ct == 0), stop=(ct == 15))
            for i, ch in enumerate(chs):
                kb.dma(BF[4 + i][:, 0:512],
                       xin[ch * 128:(ch + 1) * 128, cq * 512:(cq + 1) * 512],
                       grp="p2dx")
                kb.tt("dve", BF[4 + i][:, 512:1024], pd4[i],
                      BF[4 + i][:, 0:512], ALU.add)
                kb.dma(ATTNO[ch * 128:(ch + 1) * 128, cq * 512:(cq + 1) * 512],
                       BF[4 + i][:, 512:1024], q="act", grp="p2do")


def emit_p3(kb, cfg, T):
    nc = kb.nc
    BF, BH = T["BF"], T["BH"]
    PSA, PSB, PSC, PSD = T["PSA"], T["PSB"], T["PSC"], T["PSD"]
    sbuf = T["sbuf"]
    XN_TM, ATTNO, out = T["XN_TM"], T["ATTNO"], T["out"]
    w1T, w2T, w3T = T["w1T"], T["w2T"], T["w3T"]
    g1r, g2r, g3r = T["g1r"], T["g2r"], T["g3r"]
    XQ1, XQ2, XQ3, HG3 = T["XQ1"], T["XQ2"], T["XQ3"], T["HG3"]
    c_ones1, c_ones1b, c_onesdb, c_ident, c_identb = (
        T["c_ones1"], T["c_ones1b"], T["c_onesdb"], T["c_ident"], T["c_identb"])
    c_onesd = T["c_onesd"]
    zi32 = sbuf("zi32", [128, 2048], I32)
    col1 = sbuf("p3_col1", [128, 1], F32)
    col2 = sbuf("p3_col2", [128, 1], F32)
    rstd = sbuf("p3_rstd", [128, 1], F32)
    inv1 = sbuf("p3_inv1", [128, 16], F32)
    inv2 = sbuf("p3_inv2", [128, 16], F32)
    h2st = sbuf("p3_h2st", [128, 16], F32)
    hmx = sbuf("p3_hmx", [128, 16], F32)
    sc3 = sbuf("p3_sc3", [128, 16], F32)
    iv3 = sbuf("p3_iv3", [128, 16], F32)
    srow = sbuf("p3_srow", [1, 8], F32)
    acol = sbuf("p3_acol", [128, 4], F32)
    thrc = sbuf("p3_thrc", [128, 4], F32)
    NCH = 16

    for wi, (wT, nblk) in enumerate(((w1T, 64), (w2T, 64), (w3T, 64))):
        wvT = wT.rearrange("a b -> (a b)").rearrange("(n p f) -> n p f",
                                                     p=128, f=2048)
        s16 = sbuf("p3_s16", [1, 16], F32) if wi == 0 else s16
        for blk in range(nblk):
            b = blk % 2
            kb.dma(BF[b][:], wvT[blk], grp=f"p3s{b}")
            kb.act(BF[2 + b][:], BF[b][:], AF.Abs)
            tgts = (PSA[:1, 0:512], PSA[:1, 512:1024],
                    PSB[:1, 0:512], PSB[:1, 512:1024])
            for q4 in range(4):
                kb.mm(tgts[q4], c_onesd[:], BF[2 + b][:, q4 * 512:(q4 + 1) * 512],
                      start=(blk == 0), stop=(blk == nblk - 1))
        for q4, tgt in enumerate((PSA[:1, 0:512], PSA[:1, 512:1024],
                                  PSB[:1, 0:512], PSB[:1, 512:1024])):
            kb.red("dve", s16[:1, q4 * 4:(q4 + 1) * 4],
                   tgt.rearrange("a (g x) -> a g x", x=128), ALU.add)
        kb.red("dve", srow[:1, wi:wi + 1], s16[:1, :], ALU.add)
    kb.ts("dve", srow[:1, 0:3], srow[:1, 0:3], 1.0 / 16777216.0, ALU.mult)
    kb.ts("dve", srow[:1, 3:6], srow[:1, 0:3], -1.0, ALU.mult, 1.0, ALU.add)
    kb.op("dve", lambda g: g.reciprocal(out=srow[:1, 3:6], in_=srow[:1, 3:6]),
          reads=[srow], writes=[srow])
    kb.ts("dve", srow[:1, 6:8], srow[:1, 0:2], 1.0, ALU.add)
    kb.tt("dve", srow[:1, 6:8], srow[:1, 6:8], srow[:1, 3:5], ALU.mult)
    kb.act(srow[:1, 6:8], srow[:1, 6:8], AF.Ln)
    kb.ts("dve", srow[:1, 6:8], srow[:1, 6:8], 0.5, ALU.mult)
    kb.ts("dve", srow[:1, 5:6], srow[:1, 2:3], 1.0, ALU.add)
    kb.tt("dve", srow[:1, 5:6], srow[:1, 5:6], srow[:1, 3:4].rearrange("a b -> a b"),
          ALU.mult) if False else None
    kb.ts("dve", srow[:1, 3:4], srow[:1, 2:3], -1.0, ALU.mult, 1.0, ALU.add)
    kb.op("dve", lambda g: g.reciprocal(out=srow[:1, 3:4], in_=srow[:1, 3:4]),
          reads=[srow], writes=[srow])
    kb.tt("dve", srow[:1, 5:6], srow[:1, 5:6], srow[:1, 3:4], ALU.mult)
    kb.act(srow[:1, 5:6], srow[:1, 5:6], AF.Ln)
    kb.ts("dve", srow[:1, 5:6], srow[:1, 5:6], 0.5, ALU.mult)
    kb.ts("dve", srow[:1, 0:3], srow[:1, 0:3], 1e-8, ALU.add,
          0.5493061443340549, ALU.mult)
    kb.mm(PSC[:, 0:8], c_ones1[:], srow[:1, 0:8])
    kb.copy("dve", thrc[:, 0:3], PSC[:, 0:3])
    kb.copy("dve", acol[:, 0:3], PSC[:, 5:8])

    def round_chain(y_f32, out_f32):
        kb.copy("dve", zi32[:], y_f32)
        kb.stt(BF[5][:], zi32[:], -1.0, y_f32, ALU.mult, ALU.add)
        kb.ts("dve", BH[14][:], BF[5][:], 0.5, ALU.is_gt)
        kb.ts("dve", BH[15][:], BF[5][:], -0.5, ALU.is_lt)
        kb.tt("pool", BF[5][:], BH[14][:], zi32[:], ALU.add)
        kb.tt("dve", out_f32, BF[5][:], BH[15][:], ALU.subtract)

    if "p3pre" in cfg["phases"]:
        for gi, grow in enumerate((g1r, g2r)):
            kb.dma(srow[:1, 0:1], grow[:1, 0:1], grp="p3g") if False else None
        g1t, g2t = BF[2], BF[3]
        grows = T["rows32"]
        for gi, (grow, gt) in enumerate(((g1r, g1t), (g2r, g2t))):
            kb.dma(grows[:1, 0:2048], grow, grp="p3g")
            for qq in range(2):
                kb.mm(PSA[:, qq * 512:(qq + 1) * 512], c_ones1[:],
                      grows[:1, qq * 512:(qq + 1) * 512])
                kb.mm(PSB[:, qq * 512:(qq + 1) * 512], c_ones1[:],
                      grows[:1, 1024 + qq * 512:1024 + (qq + 1) * 512])
            kb.copy("dve", gt[:, 0:1024], PSA[:])
            kb.copy("dve", gt[:, 1024:2048], PSB[:])
        for tb in range(NCH):
            kb.dma(BF[0][:], XN_TM[tb * 128:(tb + 1) * 128, :], grp="p3x")
            kb.act(BF[1][:], BF[0][:], AF.Square)
            kb.red("dve", col1[:], BF[1][:], ALU.add)
            kb.ts("dve", col1[:], col1[:], 1.0 / 2048, ALU.mult, 1e-8, ALU.add)
            kb.act(rstd[:], col1[:], AF.Ln)
            kb.act(rstd[:], rstd[:], AF.Exp, scale=-0.5)
            for gi, (gt, XQ, invv) in enumerate(((g1t, XQ1, inv1),
                                                 (g2t, XQ2, inv2))):
                kb.stt(BF[1][:], BF[0][:], rstd[:, 0:1], gt[:], ALU.mult,
                       ALU.mult)
                kb.red("dve", col1[:], BF[1][:], ALU.max, absval=True)
                kb.ts("dve", col1[:], col1[:], 1e-4, ALU.max)
                kb.op("dve", lambda g: g.reciprocal(out=col2[:], in_=col1[:]),
                      reads=[col1], writes=[col2])
                kb.ts("dve", invv[:, tb:tb + 1], col1[:], 1.0 / 127, ALU.mult)
                kb.ts("dve", col2[:], col2[:], 127.0, ALU.mult)
                kb.act(BF[4][:], BF[1][:], AF.Copy, scale=col2[:, 0:1])
                round_chain(BF[4][:], BF[1][:])
                for ct in range(16):
                    pb = ct % 2
                    kb.trans(PSC[:, 512 + pb * 128:512 + (pb + 1) * 128],
                             BF[1][:, ct * 128:(ct + 1) * 128], c_ident[:])
                    kb.copy("act", BH[12][:, ct * 128:(ct + 1) * 128],
                            PSC[:, 512 + pb * 128:512 + (pb + 1) * 128])
                kb.dma(XQ[:, tb * 128:(tb + 1) * 128].rearrange(
                           "(c p) t -> p c t", p=128),
                       BH[12][:], q="act", grp="p3q")

    if "p3mid" in cfg["phases"]:
        g3t = BF[6]
        grow3 = T["rows32"]
        kb.memset("dve", h2st[:], 0.0)
        kb.memset("dve", hmx[:], 0.0)
        for hb in range(16):
            kb.dma(grow3[:1, 0:512], g3r[0:1, hb * 512:(hb + 1) * 512], grp="p3g")
            kb.mm(PSC[:, 0:512], c_ones1[:], grow3[:1, 0:512])
            kb.copy("dve", g3t[:, 0:512], PSC[:, 0:512])
            for wi, (wT, wst) in enumerate(((w1T, 0), (w2T, 1))):
                for ct in range(16):
                    kb.dma(BF[0][:, 0:512], wT[ct * 128:(ct + 1) * 128,
                                               hb * 512:(hb + 1) * 512],
                           grp=f"p3w{wi}")
                    kb.ts("dve", BF[1][:, 0:512], BF[0][:, 0:512],
                          thrc[:, wi:wi + 1], ALU.is_gt)
                    kb.ts("dve", BF[1][:, 512:1024], BF[0][:, 0:512],
                          thrc[:, wi:wi + 1], ALU.is_lt) if False else None
                    kb.ts("pool", BF[4][:, 0:512], BF[0][:, 0:512],
                          0.0, ALU.subtract) if False else None
                    kb.ts("dve", BF[4][:, 0:512], BF[0][:, 0:512], -1.0, ALU.mult)
                    kb.ts("dve", BF[4][:, 512:1024], BF[4][:, 0:512],
                          thrc[:, wi:wi + 1], ALU.is_gt)
                    kb.tt("dve", BH[wst * 4 + ct // 4][:,
                          (ct % 4) * 512:(ct % 4 + 1) * 512],
                          BF[1][:, 0:512], BF[4][:, 512:1024], ALU.subtract)
            def ld_stripe(tb, sl):
                kb.dma(BH[8 + sl][:], XQ1[:, tb * 128:(tb + 1) * 128].rearrange(
                    "(c p) t -> p c t", p=128), grp=f"p3xs{sl}")
                kb.dma(BH[10 + sl][:], XQ2[:, tb * 128:(tb + 1) * 128].rearrange(
                    "(c p) t -> p c t", p=128), grp=f"p3xs{sl}")
            ld_stripe(0, 0)
            for tb in range(NCH):
                sl = tb % 2
                if tb + 1 < NCH:
                    ld_stripe(tb + 1, (tb + 1) % 2)
                for ct in range(16):
                    kb.mm(PSA[:, 0:512], BH[8 + sl][:, ct * 128:(ct + 1) * 128],
                          BH[0 + ct // 4][:, (ct % 4) * 512:(ct % 4 + 1) * 512],
                          start=(ct == 0), stop=(ct == 15))
                    kb.mm(PSB[:, 0:512], BH[10 + sl][:, ct * 128:(ct + 1) * 128],
                          BH[4 + ct // 4][:, (ct % 4) * 512:(ct % 4 + 1) * 512],
                          start=(ct == 0), stop=(ct == 15))
                kb.ts("dve", col1[:], inv1[:, tb:tb + 1], 1.0, ALU.mult)
                kb.tt("dve", col1[:], col1[:], acol[:, 1:2], ALU.mult)
                kb.ts("dve", col2[:], inv2[:, tb:tb + 1], 1.0, ALU.mult)
                kb.tt("dve", col2[:], col2[:], acol[:, 2:3], ALU.mult)
                kb.act(BF[0][:, 0:512], PSA[:, 0:512], AF.Silu,
                       scale=col1[:, 0:1])
                kb.act(BF[0][:, 512:1024], PSB[:, 0:512], AF.Copy,
                       scale=col2[:, 0:1])
                kb.tt("dve", BF[1][:, 0:512], BF[0][:, 0:512],
                      BF[0][:, 512:1024], ALU.mult)
                kb.tt("dve", BH[13][:, 0:512], BF[1][:, 0:512], g3t[:, 0:512],
                      ALU.mult)
                kb.dma(HG3[tb * 128:(tb + 1) * 128, hb * 512:(hb + 1) * 512],
                       BH[13][:, 0:512], q="act", grp="p3h")
                kb.act(BF[4][:, 0:512], BF[1][:, 0:512], AF.Square,
                       accum=col1[:])
                kb.tt("dve", h2st[:, tb:tb + 1], h2st[:, tb:tb + 1], col1[:],
                      ALU.add)
                kb.red("dve", col2[:], BH[13][:, 0:512], ALU.max, absval=True)
                kb.tt("dve", hmx[:, tb:tb + 1], hmx[:, tb:tb + 1], col2[:],
                      ALU.max)

    if "p3b" in cfg["phases"]:
        kb.ts("dve", h2st[:], h2st[:], 1.0 / 8192, ALU.mult, 1e-8, ALU.add)
        kb.act(h2st[:], h2st[:], AF.Ln)
        kb.act(h2st[:], h2st[:], AF.Exp, scale=-0.5)
        kb.tt("dve", hmx[:], hmx[:], h2st[:], ALU.mult)
        kb.ts("dve", hmx[:], hmx[:], 1e-4, ALU.max)
        kb.tt("dve", iv3[:], hmx[:], acol[:, 0:1].rearrange("a b -> a b")
              if False else hmx[:], ALU.bypass) if False else None
        kb.op("dve", lambda g: g.reciprocal(out=sc3[:], in_=hmx[:]),
              reads=[hmx], writes=[sc3])
        kb.ts("dve", sc3[:], sc3[:], 127.0, ALU.mult)
        kb.tt("dve", sc3[:], sc3[:], h2st[:], ALU.mult)
        kb.ts("dve", iv3[:], hmx[:], 1.0 / 127, ALU.mult)
        for i in range(16):
            kb.tt("dve", iv3[:, i:i + 1], iv3[:, i:i + 1], acol[:, 0:1], ALU.mult)
        hidx = [(tb, hq) for tb in range(NCH) for hq in range(4)]
        def ldh(i):
            tb_, hq_ = hidx[i]
            kb.dma(BH[8 + i % 2][:], HG3[tb_ * 128:(tb_ + 1) * 128,
                                         hq_ * 2048:(hq_ + 1) * 2048],
                   grp="p3h2")
        ldh(0)
        for i, (tb, hq) in enumerate(hidx):
            if True:
                if i + 1 < len(hidx):
                    ldh(i + 1)
                kb.act(BF[1][:], BH[8 + i % 2][:], AF.Copy,
                       scale=sc3[:, tb:tb + 1])
                round_chain(BF[1][:], BF[4][:])
                for ct in range(16):
                    pb = ct % 2
                    kb.trans(PSC[:, 512 + pb * 128:512 + (pb + 1) * 128],
                             BF[4][:, ct * 128:(ct + 1) * 128], c_ident[:])
                    kb.copy("act", BH[12][:, ct * 128:(ct + 1) * 128],
                            PSC[:, 512 + pb * 128:512 + (pb + 1) * 128])
                kb.dma(XQ3[hq * 2048:(hq + 1) * 2048,
                           tb * 128:(tb + 1) * 128].rearrange(
                           "(c p) t -> p c t", p=128),
                       BH[12][:], q="act", grp="p3q3")
        for c8 in range(8):
            for hh in range(16):
                for sub in range(4):
                    r0 = hh * 512 + sub * 128
                    kb.dma(BF[0][:, 0:256], w3T[r0:r0 + 128,
                                                c8 * 256:(c8 + 1) * 256],
                           grp="p3w3")
                    kb.ts("dve", BF[1][:, 0:256], BF[0][:, 0:256],
                          thrc[:, 2:3], ALU.is_gt)
                    kb.ts("dve", BF[4][:, 0:256], BF[0][:, 0:256], -1.0, ALU.mult)
                    kb.ts("dve", BF[4][:, 256:512], BF[4][:, 0:256],
                          thrc[:, 2:3], ALU.is_gt)
                    kb.tt("dve", BH[(hh % 4) * 2 + sub // 2][:,
                          (sub % 2) * 256 + ((hh // 4) % 2) * 512:
                          (sub % 2) * 256 + ((hh // 4) % 2) * 512 + 256],
                          BF[1][:, 0:256], BF[4][:, 256:512], ALU.subtract) \
                        if False else None
                    kb.tt("dve", BH[hh // 2][:, (hh % 2) * 1024 + sub * 256:
                                              (hh % 2) * 1024 + (sub + 1) * 256],
                          BF[1][:, 0:256], BF[4][:, 256:512], ALU.subtract)
            def ld3(tb, base):
                for hq in range(4):
                    kb.dma(BH[base + hq][:, 0:2048], XQ3[
                        hq * 2048:(hq + 1) * 2048,
                        tb * 128:(tb + 1) * 128].rearrange(
                        "(c p) t -> p c t", p=128), grp=f"p3xs{base}")
            if c8 == 0:
                ld3(0, 8)
            for tb in range(NCH):
                base = 8 + 4 * (tb % 2)
                nxt = tb + 1 if tb + 1 < NCH else (0 if c8 + 1 < 8 else None)
                if nxt is not None:
                    ld3(nxt, 8 + 4 * ((tb + 1) % 2))
                for hh in range(16):
                    for sub in range(4):
                        hidx = hh * 4 + sub
                        kb.mm(PSD[:, 0:256],
                              BH[base + hidx // 16][:, (hidx % 16) * 128:
                                                    (hidx % 16 + 1) * 128],
                              BH[hh // 2][:, (hh % 2) * 1024 + sub * 256:
                                           (hh % 2) * 1024 + (sub + 1) * 256],
                              start=(hidx == 0), stop=(hidx == 63))
                fi = c8 * NCH + tb
                if fi == 0:
                    kb.dma(BF[6][:, 0:256], ATTNO[0:128, 0:256], grp="p3at")
                if fi + 1 < 8 * NCH:
                    c8n, tbn = (fi + 1) // NCH, (fi + 1) % NCH
                    kb.dma(BF[6][:, 512 + ((fi + 1) % 2) * 256:
                                  768 + ((fi + 1) % 2) * 256],
                           ATTNO[tbn * 128:(tbn + 1) * 128,
                                 c8n * 256:(c8n + 1) * 256], grp="p3at")
                abuf = BF[6][:, 0:256] if fi == 0 else \
                    BF[6][:, 512 + (fi % 2) * 256:768 + (fi % 2) * 256]
                kb.stt(BF[6][:, 256:512], PSD[:, 0:256], iv3[:, tb:tb + 1],
                       abuf, ALU.mult, ALU.add)
                kb.dma(out[tb * 128:(tb + 1) * 128, c8 * 256:(c8 + 1) * 256],
                       BF[6][:, 256:512], q="act", grp="p3out")





def kernel(**inputs):
    cfg = {"phases": {"p0", "p1", "p2a", "p2b", "p2c", "p2d", "p3pre",
                      "p3mid", "p3b"}, "n_chunks": 16, "expose": (), "dbg": {}}
    nc, kb = build_program(cfg)
    maps = [make_core_inputs(inputs, c, cfg) for c in range(8)]
    res = run_bass_kernel_spmd(nc, maps, core_ids=list(range(8)))
    out = np.zeros((2, 8192, 2048), np.float32)
    for c in range(8):
        b, qt = c // 4, c % 4
        out[b, qt * 2048:(qt + 1) * 2048] = np.asarray(res.results[c]["outp"])
    return out
